# version 08
# baseline (speedup 1.0000x reference)
import numpy as np
import ml_dtypes
import concourse.bass as bass
import concourse.mybir as mybir
from concourse.bass_utils import run_bass_kernel_spmd
from contextlib import ExitStack

F32 = mybir.dt.float32
BF16 = mybir.dt.bfloat16
I8 = mybir.dt.int8
I32 = mybir.dt.int32
AF = mybir.ActivationFunctionType
ALU = mybir.AluOpType
AX = mybir.AxisListType

D = 2048
L = 2048
NH = 32
DH = 64
CH = 64
QT = 512
MARG = 64
QW = QT + MARG
LNEPS = 1e-5
RMSEPS = 1.1920929e-07
RD = 1.0 / D

ENGINES = ("sync", "tensor", "vector", "scalar", "gpsimd")
OWN = {"tensor": "pe", "vector": "dve", "scalar": "act", "gpsimd": "pool"}


class Gen:

    def __init__(self):
        self.ops = {e: [] for e in ENGINES}
        import collections
        self.cnt = collections.defaultdict(int)
        for c in ("pe", "dve", "act", "pool"):
            self.cnt[c] = 0
        self.pend = {e: {} for e in ENGINES}
        self.ptags = []

    def now(self, ch):
        return self.cnt[ch]

    def snap(self):
        return dict(self.cnt)

    def barrier(self, snap=None):
        s = snap or self.snap()
        for e in ENGINES:
            for ch, v in s.items():
                if OWN.get(e) == ch:
                    continue
                if v > self.pend[e].get(ch, 0):
                    self.pend[e][ch] = v

    def em(self, eng, fn, wait=(), inc=()):
        if eng == "tensor":
            self.ptags.append(PHASE[0])
        wd = dict(self.pend[eng])
        self.pend[eng] = {}
        for ch, v in wait:
            if v is None:
                continue
            if OWN.get(eng) == ch:
                continue
            if v > wd.get(ch, 0):
                wd[ch] = v
        post = {}
        for ch, amt in inc:
            self.cnt[ch] += amt
            post[ch] = self.cnt[ch]
        if fn is None and inc:
            raise RuntimeError("inc requires a real instruction")
        self.ops[eng].append((fn, tuple(wd.items()), tuple(inc)))
        return post


_LAST_GEN = None


def _op(method, *args, **kwargs):
    return lambda eng: getattr(eng, method)(*args, **kwargs)


SUBSTOP = [3]
SUBQ2 = [3]
SUBQ5 = [5]
SUBG = [2]
SUBNP = [4]
SUBB4 = [9]
EARLYTAP = [1]
CONVENG = ["gpsimd"]
TREVAC = ["mixed"]
DRAINFIX = [0]
DBGK = [0]
DBGWHAT = ["u"]
SUBQ4 = [4]
ZTMP = [0]
WAITPAD = [0]
PHASE = ["?"]
TAPQ = [0]
WOUTDST = [0]


def build(dump_x1=False, stop_after="full", taps=False):
    nc = bass.Bass("TRN2", target_bir_lowering=False)

    u = nc.dram_tensor("u", [L, D], F32, kind="ExternalInput").ap()
    w_inT = nc.dram_tensor("w_inT", [D, 8224], F32, kind="ExternalInput").ap()
    w_outT = nc.dram_tensor("w_outT", [D, D], F32, kind="ExternalInput").ap()
    wg8 = nc.dram_tensor("wg8", [D, 8192], I8, kind="ExternalInput").ap()
    wu8 = nc.dram_tensor("wu8", [D, 8192], I8, kind="ExternalInput").ap()
    wd8 = nc.dram_tensor("wd8", [8192, D], I8, kind="ExternalInput").ap()
    in_ident = nc.dram_tensor("in_ident", [128, 128], BF16, kind="ExternalInput").ap()
    in_gmask = nc.dram_tensor("in_gmask", [128, 128], BF16, kind="ExternalInput").ap()
    in_convw = nc.dram_tensor("in_convw", [128, 192], F32, kind="ExternalInput").ap()
    in_sgate = nc.dram_tensor("in_sgate", [128, 64], F32, kind="ExternalInput").ap()
    in_sup = nc.dram_tensor("in_sup", [128, 64], F32, kind="ExternalInput").ap()
    in_sdbc = nc.dram_tensor("in_sdbc", [128, D], F32, kind="ExternalInput").ap()
    in_rmsbc = nc.dram_tensor("in_rmsbc", [128, D], F32, kind="ExternalInput").ap()
    in_arep = nc.dram_tensor("in_arep", [128, 16], F32, kind="ExternalInput").ap()

    out = nc.dram_tensor("out", [L, D], F32, kind="ExternalOutput").ap()
    tap_specs = [("d_ln1T", 9216, BF16), ("d_act", 18432, BF16), ("d_z", 8192, BF16),
                 ("d_xs", 8192, BF16), ("d_y", 4096, BF16), ("d_cs", 128, F32),
                 ("d_prev", 144, F32), ("d_attn", 4096, F32), ("d_u", 4096, F32),
                 ("d_yT", 4096, BF16), ("d_ln1T2", 9216, BF16), ("d_dbgu", 2048, F32)]
    tapd = {}
    if taps:
        for nm, w, dtp in tap_specs:
            tapd[nm] = nc.dram_tensor(nm, [128, w], dtp, kind="ExternalOutput").ap()
        tapd["d_lnt"] = nc.dram_tensor("d_lnt", [512, D], BF16, kind="ExternalOutput").ap()
        tapd["d_ut"] = nc.dram_tensor("d_ut", [512, D], F32, kind="ExternalOutput").ap()
    x1 = nc.dram_tensor("x1", [L, D], F32,
                        kind="ExternalOutput" if dump_x1 else "Internal").ap()
    cwin = nc.dram_tensor("cwin", [128, 16 * 16 * 512], BF16, kind="Internal").ap()
    cwout = nc.dram_tensor("cwout", [128, 4 * 16 * 512], BF16, kind="Internal").ap()
    cwg = nc.dram_tensor("cwg", [128, 16 * 16 * 512], BF16, kind="Internal").ap()
    cwu = nc.dram_tensor("cwu", [128, 16 * 16 * 512], BF16, kind="Internal").ap()
    cwd = nc.dram_tensor("cwd", [128, 8 * 64 * 256], BF16, kind="Internal").ap()

    cwin_v = cwin.rearrange("p (s dc j) -> p s dc j", s=16, dc=16)
    cwout_v = cwout.rearrange("p (q dc j) -> p q dc j", q=4, dc=16)
    cwg_v = cwg.rearrange("p (s hc j) -> p s hc j", s=16, hc=16)
    cwu_v = cwu.rearrange("p (s hc j) -> p s hc j", s=16, hc=16)
    cwd_v = cwd.rearrange("p (s oc j) -> p s oc j", s=8, oc=64)

    g = Gen()
    stk_root = ExitStack()

    def sb(stack, name, shape, dtype):
        return stack.enter_context(nc.sbuf_tensor(name, shape, dtype))

    c_ident = sb(stk_root, "c_ident", [128, 128], BF16)
    c_gmask = sb(stk_root, "c_gmask", [128, 128], BF16)
    c_convw = sb(stk_root, "c_convw", [128, 192], F32)
    c_sgate = sb(stk_root, "c_sgate", [128, 64], F32)
    c_sup = sb(stk_root, "c_sup", [128, 64], F32)
    c_sdbc = sb(stk_root, "c_sdbc", [128, D], F32)
    c_rmsbc = sb(stk_root, "c_rmsbc", [128, D], F32)
    c_arep = sb(stk_root, "c_arep", [128, 16], F32)
    c_decay = sb(stk_root, "c_decay", [128, 16], F32)
    scr = sb(stk_root, "scr", [128, 2080], F32)
    st = sb(stk_root, "st", [128, 16], F32)

    SQ = scr[:, 0:2048]
    RAW = [scr[:, 0:516], scr[:, 516:1032]]
    ACC = [scr[:, 1032:1544], scr[:, 1544:2056]]

    PB = stk_root.enter_context(nc.psum_tensor("PB", [128, 2048], F32))
    PX = stk_root.enter_context(nc.psum_tensor("PX", [128, 1024], BF16))
    PG = stk_root.enter_context(nc.psum_tensor("PG", [128, 512], F32))
    PTR = stk_root.enter_context(nc.psum_tensor("PTR", [128, 1024], BF16))
    PV = stk_root.enter_context(nc.psum_tensor("PV", [128, 512], F32))
    PBq = [PB[:, 512 * i:512 * (i + 1)] for i in range(4)]

    guard = {}

    def gw(*names):
        out = []
        for n in names:
            if n in guard:
                v = guard[n]
                out.extend(v if isinstance(v, list) else [v])
        return out

    def gset(name, ch):
        guard[name] = (ch, g.now(ch))

    PHASE[0] = "prologue"
    g.em("scalar", _op("copy", out=st[:, 14:15], in_=st[:, 14:15]))
    for dst, src in ((c_ident, in_ident), (c_gmask, in_gmask), (c_convw, in_convw),
                     (c_sgate, in_sgate), (c_sup, in_sup), (c_sdbc, in_sdbc),
                     (c_rmsbc, in_rmsbc), (c_arep, in_arep)):
        g.em("sync", _op("dma_start", out=dst[:], in_=src), inc=(("ldc", 16),))
    consts_ld = g.now("ldc")
    g.em("scalar", _op("activation", out=c_decay[:], in_=c_arep[:], func=AF.Exp,
                       bias=0.0, scale=63.0),
         wait=[("ldc", consts_ld)])
    g.em("scalar", _op("copy", out=st[:, 14:15], in_=c_decay[:, 0:1]),
         inc=(("act", 1),))

    stk_pro = ExitStack()
    pw_f32 = [sb(stk_pro, f"pw_f32_{i}", [128, 8192], F32) for i in range(2)]
    pw_bf = [sb(stk_pro, f"pw_bf_{i}", [128, 8192], BF16) for i in range(2)]
    pw_i8 = [sb(stk_pro, f"pw_i8_{i}", [128, 8192], I8) for i in range(2)]

    cast_rr = ["vector", "scalar"]
    cast_ch = {"vector": "dve", "scalar": "act", "gpsimd": "pool"}
    cast_i = [0]

    def cast_op(dst_ap, src_ap, wait):
        e = cast_rr[cast_i[0] % len(cast_rr)]
        cast_i[0] += 1
        ch = cast_ch[e]
        if e == "scalar":
            fn = _op("copy", out=dst_ap, in_=src_ap)
        else:
            fn = _op("tensor_copy", out=dst_ap, in_=src_ap)
        g.em(e, fn, wait=wait, inc=((ch, 1),))
        return (ch, g.now(ch))

    def prologue_stream(n_blocks, load_src_fn, in_bufs, width, store_dst_fn, nsplit):
        for b in range(n_blocks):
            sl = b % 2
            ib = in_bufs[sl][:, 0:width]
            ob = pw_bf[sl][:, 0:width]
            half = width // 2
            src_ap = load_src_fn(b)
            g.em("sync", _op("dma_start", out=ib[:, 0:half], in_=src_ap[:, 0:half]),
                 wait=gw(f"pin{sl}"), inc=((f"ldp{sl}", 16),))
            g.em("sync", _op("dma_start", out=ib[:, half:width],
                             in_=src_ap[:, half:width]),
                 inc=((f"ldu{sl}", 16),))
            ld_w = [(f"ldp{sl}", g.now(f"ldp{sl}")), (f"ldu{sl}", g.now(f"ldu{sl}"))]
            cw = cast_op(ob, ib, ld_w + gw(f"pbf{sl}"))
            guard[f"pin{sl}"] = cw
            srcs, dsts = store_dst_fn(b, ob)
            if len(srcs) == 1:
                sap, dap = srcs[0], dsts[0]
                hs = sap.shape[1] // 2
                srcs = [sap[:, 0:hs, :], sap[:, hs:, :]]
                dsts = [dap[:, 0:hs, :], dap[:, hs:, :]]
            st_lanes = [f"stp{sl}", f"stu{sl}"]
            for i, (sap, dap) in enumerate(zip(srcs, dsts)):
                g.em("gpsimd", _op("dma_start", out=dap, in_=sap),
                     wait=[cw], inc=((st_lanes[i % 2], 16),))
            guard[f"pbf{sl}"] = [(f"stp{sl}", g.now(f"stp{sl}")),
                                 (f"stu{sl}", g.now(f"stu{sl}"))]

    def win_store(b, ob):
        srcs, dsts = [], []
        obv = ob.rearrange("p (s j) -> p s j", s=16)
        for a in range(4):
            srcs.append(obv[:, 4 * a:4 * a + 4, :])
            dsts.append(cwin_v[:, 4 * a:4 * a + 4, b, :])
        return srcs, dsts

    prologue_stream(16, lambda b: w_inT[128 * b:128 * (b + 1), 0:8192],
                    pw_f32, 8192, win_store, 4)

    def wout_store(b, ob):
        obv = ob.rearrange("p (q j) -> p q j", q=4)
        return [obv], [cwout_v[:, :, b, :]]

    prologue_stream(16, lambda b: w_outT[128 * b:128 * (b + 1), :],
                    pw_f32, 2048, wout_store, 1)

    def mk_wguw_store(cv):
        def f(b, ob):
            srcs, dsts = [], []
            obv = ob.rearrange("p (s j) -> p s j", s=16)
            for a in range(4):
                srcs.append(obv[:, 4 * a:4 * a + 4, :])
                dsts.append(cv[:, 4 * a:4 * a + 4, b, :])
            return srcs, dsts
        return f

    stk_pro.close()
    g.barrier()

    bg_i8 = [sb(stk_root, f"bg_i8_{i}", [128, 1024], I8) for i in range(2)]
    bg_bf = [sb(stk_root, f"bg_bf_{i}", [128, 1024], BF16) for i in range(2)]
    bg_units = []
    bg_ready = {}

    def mk_bg_unit(src_ap, dst_fn, tag):
        def emit(ui):
            sl = ui % 2
            g.em("gpsimd", _op("dma_start", out=bg_i8[sl][:], in_=src_ap),
                 wait=gw(f"bgb{sl}"), inc=((f"ldb{sl}", 16),))
            ld = g.now(f"ldb{sl}")
            g.em("gpsimd", _op("tensor_copy", out=bg_bf[sl][:], in_=bg_i8[sl][:]),
                 wait=[(f"ldb{sl}", ld)], inc=(("pox", 1),))
            src4, dst4 = dst_fn(bg_bf[sl])
            g.em("gpsimd", _op("dma_start", out=dst4, in_=src4),
                 wait=[("pox", g.now("pox"))], inc=((f"stb{sl}", 16),))
            gset(f"bgb{sl}", f"stb{sl}")
            if tag:
                bg_ready[tag] = [("stb0", g.now("stb0")), ("stb1", g.now("stb1"))]
        return emit

    for b in range(16):
        for a in range(4):
            for h in range(2):
                for w8, cv in ((wg8, cwg_v), (wu8, cwu_v)):
                    def mk_dst(cv=cv, b=b, a=a, h=h):
                        def f(ob):
                            return (ob.rearrange("p (s j) -> p s j", s=2),
                                    cv[:, 4 * a + 2 * h:4 * a + 2 * h + 2, b, :])
                        return f
                    last = (b == 15 and a == 3 and h == 1 and cv is cwu_v)
                    bg_units.append(mk_bg_unit(
                        w8[128 * b:128 * (b + 1),
                           2048 * a + 1024 * h:2048 * a + 1024 * (h + 1)],
                        mk_dst(), "wgwu" if last else None))
    for b in range(64):
        for h in range(2):
            def mk_dstd(b=b, h=h):
                def f(ob):
                    return (ob.rearrange("p (s j) -> p s j", s=4),
                            cwd_v[:, 4 * h:4 * h + 4, b, :])
                return f
            bg_units.append(mk_bg_unit(
                wd8[128 * b:128 * (b + 1), 1024 * h:1024 * (h + 1)], mk_dstd(),
                "wd" if (b == 63 and h == 1) else None))
    bg_i = [0]

    def bg_emit(k):
        while k > 0 and bg_i[0] < len(bg_units):
            bg_units[bg_i[0]](bg_i[0])
            bg_i[0] += 1
            k -= 1

    stk_mix = ExitStack()
    m_ln1T = sb(stk_mix, "m_ln1T", [128, 16 * QW], BF16)
    m_act = sb(stk_mix, "m_act", [128, 32 * QW], BF16)
    m_xs = sb(stk_mix, "m_xs", [128, 4 * 2048], BF16)
    m_z = sb(stk_mix, "m_z", [128, 4 * 2048], BF16)
    m_y = sb(stk_mix, "m_y", [128, 2 * 2048], BF16)
    m_xstemp = sb(stk_mix, "m_xstemp", [128, QW], BF16)
    m_xscarry = sb(stk_mix, "m_xscarry", [128, 16 * 64], BF16)
    m_rawcarry = sb(stk_mix, "m_rawcarry", [128, 96], F32)
    m_cs = sb(stk_mix, "m_cs", [128, 128], F32)
    m_prev = sb(stk_mix, "m_prev", [128, 144], F32)
    m_prevbf = sb(stk_mix, "m_prevbf", [128, 144], BF16)
    m_gt = sb(stk_mix, "m_gt", [128, 1024], BF16)
    m_vcol = sb(stk_mix, "m_vcol", [128, 32], F32)
    m_wbuf = sb(stk_mix, "m_wbuf", [128, 16384], BF16)
    m_u = sb(stk_mix, "m_u", [128, 2 * 2048], F32)
    m_ln1 = sb(stk_mix, "m_ln1", [128, 2048], BF16)
    m_yT = sb(stk_mix, "m_yT", [128, 2 * 2048], BF16)
    m_tmpA = sb(stk_mix, "m_tmpA", [128, 2048], BF16)
    m_attn = sb(stk_mix, "m_attn", [128, 2 * 2048], BF16)
    m_dbgu = m_attn.bitcast(F32)[:, 0:2048]

    sbuf_left = nc.sbuf_bytes_remaining
    assert sbuf_left >= 0, f"SBUF over budget: {sbuf_left}"

    def ln1T_dc(dc):
        return m_ln1T[:, QW * dc: QW * (dc + 1)]

    def act_blk(b):
        return m_act[:, QW * b: QW * (b + 1)]

    TRB = [PTR[:], PX[:]]
    trb_state = {"bank": 0, "batch": 0}

    def transpose_batch(jobs, wait):
        last = None
        for j0 in range(0, len(jobs), 8):
            sub = jobs[j0:j0 + 8]
            b = trb_state["bank"]
            trb_state["bank"] ^= 1
            bank = TRB[b]
            eng = "vector" if (trb_state["batch"] % 2 or TREVAC[0] == "vector") else "scalar"
            trb_state["batch"] += 1
            ch = "dve" if eng == "vector" else "act"
            for i, (dst, src) in enumerate(sub):
                g.em("tensor", _op("transpose", bank[:, 128 * i:128 * (i + 1)],
                                   src, c_ident[:]),
                     wait=(list(wait) + gw(f"trb{b}")) if i == 0 else (),
                     inc=(("pe", 1),) if i == len(sub) - 1 else ())
            pe_v = g.now("pe")
            for i, (dst, src) in enumerate(sub):
                fn = (_op("tensor_copy", out=dst, in_=bank[:, 128 * i:128 * (i + 1)])
                      if eng == "vector" else
                      _op("copy", out=dst, in_=bank[:, 128 * i:128 * (i + 1)]))
                g.em(eng, fn, wait=[("pe", pe_v)] if i == 0 else (),
                     inc=((ch, 1),) if i == len(sub) - 1 else ())
            guard[f"trb{b}"] = (ch, g.now(ch))
            last = (ch, g.now(ch))
        return last

    def rsqrt_col(dst, src):
        c13i = st[:, 13:14].bitcast(I32)
        c13f = st[:, 13:14]
        c12f = st[:, 12:13]
        g.em("vector", _op("tensor_scalar", out=c13i, in0=src.bitcast(I32),
                           scalar1=1, scalar2=None, op0=ALU.arith_shift_right))
        g.em("vector", _op("tensor_scalar", out=c13i, in0=c13i,
                           scalar1=-1, scalar2=None, op0=ALU.bitwise_xor))
        g.em("vector", _op("tensor_scalar", out=c13i, in0=c13i,
                           scalar1=0x5f3759df + 1, scalar2=None, op0=ALU.add))
        for it in range(3):
            g.em("vector", _op("tensor_mul", c12f, c13f, c13f))
            g.em("vector", _op("tensor_mul", c12f, c12f, src))
            g.em("vector", _op("tensor_scalar", out=c12f, in0=c12f,
                               scalar1=-0.5, scalar2=1.5, op0=ALU.mult, op1=ALU.add))
            g.em("vector", _op("tensor_mul", dst if it == 2 else c13f, c13f, c12f))

    def ln_stats(tile_ap, eps):
        g.em("vector", _op("tensor_mul", SQ, tile_ap, tile_ap), inc=(("dve", 1),))
        g.em("vector", _op("tensor_reduce", out=st[:, 1:2], in_=SQ, axis=AX.X,
                           op=ALU.add), inc=(("dvx", 1),))
        vA = g.now("dvx")
        g.em("vector", _op("tensor_reduce", out=st[:, 0:1], in_=tile_ap, axis=AX.X,
                           op=ALU.add), inc=(("dvx", 1),))
        g.em("vector", _op("tensor_scalar", out=st[:, 3:4], in0=st[:, 1:2],
                           scalar1=RD, scalar2=eps, op0=ALU.mult, op1=ALU.add),
             wait=[("dvx", vA)], inc=(("dvx", 1),))
        g.em("vector", _op("tensor_scalar", out=st[:, 2:3], in0=st[:, 0:1],
                           scalar1=-RD, scalar2=None, op0=ALU.mult),
             wait=[("dvx", vA + 1)], inc=(("dvx", 1),))
        g.em("vector", _op("tensor_mul", st[:, 4:5], st[:, 2:3], st[:, 2:3]),
             wait=[("dvx", g.now("dvx"))], inc=(("dvx", 1),))
        g.em("vector", _op("tensor_sub", st[:, 5:6], st[:, 3:4], st[:, 4:5]),
             wait=[("dvx", g.now("dvx"))], inc=(("dve", 1),))
        g.em("scalar", _op("activation", out=st[:, 6:7], in_=st[:, 5:6], func=AF.Sqrt,
                           bias=0.0, scale=1.0),
             wait=[("dve", g.now("dve"))])
        g.em("scalar", _op("copy", out=st[:, 14:15], in_=st[:, 6:7]),
             inc=(("act", 1),))
        g.em("vector", _op("reciprocal", out=st[:, 7:8], in_=st[:, 6:7]),
             wait=[("act", g.now("act"))], inc=(("dvx", 1),))
        g.em("vector", _op("tensor_mul", st[:, 8:9], st[:, 2:3], st[:, 7:8]),
             wait=[("dvx", g.now("dvx"))], inc=(("dve", 1),))

    SLAB_ORDER = [0, 1, 2, 3, 8, 9, 10, 11, 12, 13, 14, 15, 4, 5, 6, 7]
    pbx_slots = [PBq[0], PBq[1], PBq[2], PBq[3]]
    pbx_i = [0]

    def next_group_slot():
        i = pbx_i[0] % 4
        pbx_i[0] += 1
        return pbx_slots[i], f"pbx{i}"

    def gen_quarter(q):
        gen_quarter_inner(q)
        g.barrier()
        if taps and q == TAPQ[0]:
            for nm, buf in (("d_ln1T", m_ln1T), ("d_act", m_act), ("d_z", m_z),
                            ("d_xs", m_xs), ("d_y", m_y), ("d_cs", m_cs),
                            ("d_prev", m_prev), ("d_attn", m_attn), ("d_u", m_u),
                            ("d_yT", m_yT)):
                g.em("gpsimd", _op("dma_start", out=tapd[nm], in_=buf[:]),
                     inc=(("stt", 16),))
            g.barrier()
        gset("mix_pe_prev", "pe")

    def gen_quarter_inner(q):
        t0 = QT * q
        PHASE[0] = f"q{q}.rolls"
        first = (q == 0)

        if first:
            g.em("vector", _op("memset", m_rawcarry[:], 0.0), inc=(("dve", 1),))
            g.em("vector", _op("memset", m_xscarry[:], 0.0))
            g.em("vector", _op("memset", m_prev[:], 0.0))
            g.em("vector", _op("memset", m_xs[0:64, 0:2048], 0.0))
            for dc in range(16):
                g.em("vector", _op("memset", ln1T_dc(dc)[:, 0:MARG], 0.0))
            for b in range(32):
                g.em("vector", _op("memset", act_blk(b)[:, 0:MARG], 0.0))
            g.em("vector", _op("memset", m_xstemp[:, 0:MARG], 0.0), inc=(("dve", 1),))
        else:
            w = gw("mix_pe_prev")
            for dc in range(16):
                g.em("vector", _op("tensor_copy", out=ln1T_dc(dc)[:, 0:MARG],
                                   in_=ln1T_dc(dc)[:, QT:QT + MARG]), wait=w)
            for b in range(32):
                g.em("vector", _op("tensor_copy", out=act_blk(b)[:, 0:63],
                                   in_=act_blk(b)[:, QT:QT + 63]), wait=w)
            g.em("vector", _op("memset", st[:, 15:16], 0.0), inc=(("dve", 1),))
        rolls_done = ("dve", g.now("dve"))

        if SUBSTOP[0] < 1:
            return
        for k in range(4):
            PHASE[0] = f"q{q}.Q2k{k}"
            sl = k % 2
            ut = m_u[:, 2048 * sl:2048 * (sl + 1)]
            g.em("sync", _op("dma_start", out=ut,
                             in_=u[t0 + 128 * k: t0 + 128 * (k + 1), :]),
                 wait=gw(f"mu{sl}"), inc=((f"ldu{sl}", 16),))
            ldv = g.now(f"ldu{sl}")
            g.em("vector", None, wait=[(f"ldu{sl}", ldv)])
            if taps and q == 0 and k == DBGK[0] and DBGWHAT[0] == "u":
                g.em("vector", _op("tensor_copy", out=m_dbgu, in_=ut),
                     inc=(("dve", 1),))
                g.em("gpsimd", _op("dma_start", out=tapd["d_dbgu"], in_=m_dbgu),
                     wait=[("dve", g.now("dve"))], inc=(("stt", 16),))
                gset("attn0", "stt")
            if taps and q == 0 and k == DBGK[0] and DBGWHAT[0] == "ln1":
                pass
            if SUBQ2[0] >= 1:
                ln_stats(ut, LNEPS)
            if SUBQ2[0] >= 2:
                g.em("scalar", _op("activation", out=m_ln1[:], in_=ut, func=AF.Identity,
                                   scale=st[:, 7:8], bias=st[:, 8:9]),
                     wait=[("dve", g.now("dve"))] + gw("m_ln1"))
                g.em("scalar", _op("copy", out=st[:, 14:15], in_=st[:, 7:8]),
                     inc=(("act", 1),))
                gset(f"mu{sl}", "act")
                if taps and q == 0:
                    g.em("vector", _op("tensor_copy",
                                       out=m_cs[:, 16 * k:16 * (k + 1)],
                                       in_=st[:]),
                         wait=[("act", g.now("act"))])
            else:
                g.em("vector", _op("tensor_copy", out=m_ln1[:], in_=ut),
                     wait=gw("m_ln1"), inc=(("dve", 1),))
                gset(f"mu{sl}", "dve")
            if DRAINFIX[0]:
                g.em("vector", _op("memset", st[:, 13:14], 0.0), inc=(("dve", 1),))
            if taps and q == 0 and k == DBGK[0] and DBGWHAT[0] == "ln1":
                g.em("vector", _op("tensor_copy", out=m_dbgu[:, 0:1024],
                                   in_=m_ln1[:, 0:1024]), inc=(("dve", 1),))
                g.em("vector", _op("tensor_copy", out=m_dbgu[:, 1024:2048],
                                   in_=m_ln1[:, 1024:2048]), inc=(("dve", 1),))
                g.em("gpsimd", _op("dma_start", out=tapd["d_dbgu"], in_=m_dbgu),
                     wait=[("dve", g.now("dve"))], inc=(("stt", 16),))
                gset("attn0", "stt")
            apply_done = (("act", g.now("act")) if SUBQ2[0] >= 2
                          else ("dve", g.now("dve")))
            if taps and (SUBSTOP[0] == 1 or EARLYTAP[0] in (2, 3)) and q == 0:
                g.em("gpsimd", _op("dma_start",
                                   out=tapd["d_lnt"][128 * k:128 * (k + 1), :],
                                   in_=m_ln1[:]),
                     wait=[apply_done], inc=(("stt", 16),))
                g.em("gpsimd", _op("dma_start",
                                   out=tapd["d_ut"][128 * k:128 * (k + 1), :],
                                   in_=ut),
                     wait=[apply_done], inc=(("stt", 16),))
                gset(f"mu{sl}", "stt")
            if SUBQ2[0] >= 3:
                for j0 in (0, 8):
                    transpose_batch(
                        [(ln1T_dc(j)[:, MARG + 128 * k: MARG + 128 * (k + 1)],
                          m_ln1[:, 128 * j:128 * (j + 1)]) for j in range(j0, j0 + 8)],
                        [apply_done, rolls_done])
                gset("m_ln1", "pe")
        q2_done = {"pe": g.now("pe"), "dve": g.now("dve"), "act": g.now("act")}
        if taps and q == 0 and EARLYTAP[0]:
            g.em("gpsimd", _op("dma_start", out=tapd["d_ln1T2"], in_=m_ln1T[:]),
                 wait=[("pe", q2_done["pe"]), ("dve", q2_done["dve"]),
                       ("act", q2_done["act"])], inc=(("stt", 16),))

        if SUBSTOP[0] < 2:
            return
        ztemp = ([m_ln1[:, 0:512], m_ln1[:, 512:1024]] if ZTMP[0] == 0 else
                 [m_tmpA[:, 0:512], m_tmpA[:, 512:1024]])
        for si, s in enumerate(SLAB_ORDER):
            PHASE[0] = f"q{q}.Q4s{si}"
            wsl = si % 2
            wv = m_wbuf[:, 8192 * wsl:8192 * (wsl + 1)].rearrange(
                "p (dc j) -> p dc j", dc=16)
            for a in range(4):
                lane = f"ldw{wsl}" if a % 2 == 0 else f"ldp{wsl}"
                g.em("sync", _op("dma_start", out=wv[:, 4 * a:4 * a + 4, :],
                                 in_=cwin[:, 8192 * s + 2048 * a: 8192 * s + 2048 * (a + 1)]
                                 .rearrange("p (dc j) -> p dc j", dc=4)),
                     wait=gw(f"mwbuf{wsl}") if a == 0 else (),
                     inc=((lane, 16),))
            slab_ld = [(f"ldw{wsl}", g.now(f"ldw{wsl}")),
                       (f"ldp{wsl}", g.now(f"ldp{wsl}"))]
            for c4 in range(4):
                if SUBQ4[0] < 2:
                    continue
                oc = 4 * s + c4
                if SUBQ4[0] < 4 and oc >= 16:
                    continue
                slot, sname = next_group_slot()
                rhs_c0 = 0 if oc < 16 else MARG
                waits = slab_ld + [
                         ("dve", q2_done["dve"]), ("act", q2_done["act"])] + gw(sname)
                for dc in range(16):
                    g.em("tensor", _op("matmul", slot[:],
                                       wv[:, dc, 128 * c4:128 * (c4 + 1)],
                                       ln1T_dc(dc)[:, rhs_c0:rhs_c0 + 512],
                                       start=(dc == 0), stop=(dc == 15)),
                         wait=waits if dc == 0 else (),
                         inc=(("pe", 1),) if dc == 15 else ())
                gset(f"mwbuf{wsl}", "pe")
                mm_done = ("pe", g.now("pe"))
                if oc < 16:
                    zt = ztemp[oc % 2]
                    if SUBQ4[0] == 20:
                        continue
                    g.em("vector", _op("tensor_copy", out=zt, in_=slot[:]),
                         wait=[mm_done] + gw(f"ztemp{oc % 2}"), inc=(("dve", 1),))
                    gset(sname, "dve")
                    ev = ("dve", g.now("dve"))
                    if SUBQ4[0] >= 3:
                        transpose_batch(
                            [(m_z[:, 2048 * j + 128 * oc: 2048 * j + 128 * (oc + 1)],
                              zt[:, 128 * j:128 * (j + 1)]) for j in range(4)], [ev])
                        gset(f"ztemp{oc % 2}", "pe")
                else:
                    r = oc - 16
                    par = r % 2
                    raw = RAW[par]
                    xs_chunk = 16 <= oc < 32
                    c = oc - 16 if xs_chunk else None
                    dst = (m_xstemp[:, 63:575] if xs_chunk
                           else act_blk((oc - 32) if oc < 48 else (16 + oc - 48))[:, 63:575])
                    g.em("vector", _op("tensor_copy", out=raw[:, 2:514], in_=slot[:]),
                         wait=[mm_done] + gw(f"raw{par}"), inc=(("dvx", 1),))
                    ev_dvx = g.now("dvx")
                    gset(sname, "dvx")
                    g.em("vector", _op("tensor_copy", out=raw[:, 0:2],
                                       in_=m_rawcarry[:, 2 * r:2 * r + 2]),
                         wait=gw(f"rawc{r}"), inc=(("dvx", 1),))
                    rc_dvx = g.now("dvx")
                    g.em("vector", _op("tensor_copy", out=m_rawcarry[:, 2 * r:2 * r + 2],
                                       in_=raw[:, 512:514]),
                         wait=[("dvx", ev_dvx)], inc=(("dve", 1),))
                    gset(f"rawc{r}", "dve")
                    g.em("vector", _op("tensor_mul", ACC[1], raw[:, 0:512],
                                       c_convw[:, r:r + 1].broadcast_to([128, 512])),
                         wait=[("dvx", rc_dvx)] + gw("acc"))
                    g.em("vector", _op("scalar_tensor_tensor", out=ACC[0],
                                       in0=raw[:, 1:513],
                                       scalar=c_convw[:, 48 + r:49 + r],
                                       in1=ACC[1], op0=ALU.mult, op1=ALU.add))
                    g.em("vector", _op("scalar_tensor_tensor", out=ACC[0],
                                       in0=raw[:, 2:514],
                                       scalar=c_convw[:, 96 + r:97 + r],
                                       in1=ACC[0], op0=ALU.mult, op1=ALU.add),
                         inc=(("dve", 1),))
                    gset(f"raw{par}", "dve")
                    conv_done = ("dve", g.now("dve"))
                    if xs_chunk:
                        if first:
                            g.em("vector", _op("memset", m_xstemp[:, 0:63], 0.0),
                                 wait=gw("xstemp_r"), inc=(("dve", 1),))
                        else:
                            g.em("vector", _op("tensor_copy", out=m_xstemp[:, 0:63],
                                               in_=m_xscarry[:, 64 * c:64 * c + 63]),
                                 wait=gw("xstemp_r"), inc=(("dve", 1),))
                    carry_in = ("dve", g.now("dve"))
                    sw = [conv_done] + ([carry_in] if xs_chunk else []) + \
                        (gw("xstemp_r") if xs_chunk else [])
                    g.em("scalar", _op("activation", out=dst, in_=ACC[0], func=AF.Silu,
                                       bias=c_convw[:, 144 + r:145 + r], scale=1.0),
                         wait=sw)
                    g.em("scalar", _op("copy", out=st[:, 14:15], in_=dst[:, 0:1]),
                         inc=(("act", 1),))
                    gset("acc", "act")
                    silu_done = ("act", g.now("act"))
                    if xs_chunk:
                        if first:
                            g.em("vector", _op("memset", m_xstemp[:, 63:64], 0.0),
                                 wait=[silu_done], inc=(("dve", 1),))
                        g.em("vector", _op("tensor_copy",
                                           out=m_xscarry[:, 64 * c:64 * c + 63],
                                           in_=m_xstemp[:, 512:575]),
                             wait=[silu_done], inc=(("dve", 1),))
                        g.em("vector", _op("tensor_mul", m_tmpA[:, 0:512],
                                           m_xstemp[:, 0:512], act_blk(c)[:, 0:512]),
                             wait=[silu_done] + gw("m_tmpA"), inc=(("dve", 1),))
                        g.em("vector", _op("tensor_reduce", out=m_cs[:, 8 * c:8 * (c + 1)],
                                           in_=m_tmpA[:, 0:512].rearrange(
                                               "p (cc t) -> p cc t", cc=8),
                                           axis=AX.X, op=ALU.add),
                             inc=(("dvx", 1),))
                        if first:
                            g.em("vector", _op("memset", m_cs[:, 8 * c:8 * c + 1], 0.0))
                        transpose_batch(
                            [(m_xs[:, 2048 * j + 128 * c: 2048 * j + 128 * (c + 1)],
                              m_xstemp[:, 128 * j:128 * (j + 1)]) for j in range(4)],
                            [silu_done, carry_in])
                        gset("xstemp_r", "pe")
        q4_pe = g.now("pe")
        cs_dvx = g.now("dvx")

        for ci in range(1, 8):
            for b in range(16):
                w = ()
                if b == 0:
                    w = ([("dvx", cs_dvx)] + gw("prev_r") if ci == 1
                         else [("dvx", g.now("dvx"))])
                g.em("vector", _op("scalar_tensor_tensor",
                                   out=m_prev[:, 9 * b + ci:9 * b + ci + 1],
                                   in0=m_prev[:, 9 * b + ci - 1:9 * b + ci],
                                   scalar=c_decay[:, b:b + 1],
                                   in1=m_cs[:, 8 * b + ci - 1:8 * b + ci],
                                   op0=ALU.mult, op1=ALU.add),
                     wait=w, inc=(("dvx", 1),) if b == 15 else ())
        g.em("vector", _op("tensor_copy", out=m_prevbf[:], in_=m_prev[:]),
             wait=[("dvx", g.now("dvx"))], inc=(("dve", 1),))
        prevbf_done = ("dve", g.now("dve"))
        for b in range(16):
            g.em("vector", _op("scalar_tensor_tensor",
                               out=m_prev[:, 9 * b:9 * b + 1],
                               in0=m_prev[:, 9 * b + 7:9 * b + 8],
                               scalar=c_decay[:, b:b + 1],
                               in1=m_cs[:, 8 * b + 7:8 * b + 8],
                               op0=ALU.mult, op1=ALU.add),
                 inc=(("dve", 1),) if b == 15 else ())

        if SUBSTOP[0] >= 3:
            gen_ssd_and_q6(q, t0, prevbf_done, q4_pe)
        bg_emit(96)

    def gen_ssd_pair(q, p, prevbf_done, n_vchunks=2):
        if SUBG[0] == 0:
            return
        PHASE[0] = f"q{q}.ssd{p}"
        horder = list(range(0, 32, 2)) + list(range(1, 32, 2))
        mask_end = {}
        ydiag_end = {}
        for b4 in range(min(9, SUBB4[0])):
            if b4 < 8:
                for i in range(4):
                    h = horder[4 * b4 + i]
                    hb = 64 * (h % 2)
                    blk = h // 2
                    lhsT = m_act[hb:hb + 64, QW * blk + 128 * p: QW * blk + 128 * (p + 1)]
                    rhs = m_act[hb:hb + 64,
                                QW * (16 + blk) + 128 * p: QW * (16 + blk) + 128 * (p + 1)]
                    w = []
                    if i == 0:
                        w = gw("pg_bank")
                        if b4 == 0:
                            w = w + [prevbf_done, ("act", g.now("act")),
                                     ("pool", g.now("pool"))]
                    g.em("tensor", _op("matmul", PG[:, 128 * i:128 * (i + 1)],
                                       lhsT, rhs, start=True, stop=True,
                                       tile_position=(hb, 0)),
                         wait=w, inc=(("pe", 1),) if i == 3 else ())
                g_end = g.now("pe")
                if SUBG[0] >= 2:
                    for i in range(4):
                        w = [("pe", g_end)] if i == 0 else []
                        if b4 >= 2 and i == 0 and (b4 - 2) in ydiag_end:
                            w = w + [("pe", ydiag_end[b4 - 2])]
                        g.em("vector", _op("tensor_mul",
                                           m_gt[:, 512 * (b4 % 2) + 128 * i:
                                                512 * (b4 % 2) + 128 * (i + 1)],
                                           PG[:, 128 * i:128 * (i + 1)], c_gmask[:]),
                             wait=w, inc=(("dve", 1),) if i == 3 else ())
                    mask_end[b4] = g.now("dve")
                    guard["pg_bank"] = ("dve", mask_end[b4])
                else:
                    g.em("vector", _op("memset", st[:, 14:15], 0.0),
                         wait=[("pe", g_end)], inc=(("dve", 1),))
                    mask_end[b4] = g.now("dve")
                    guard["pg_bank"] = ("dve", mask_end[b4])
            if b4 >= 1 and SUBQ5[0] >= 2:
                bb = b4 - 1
                for i in range(4):
                    hh = horder[4 * bb + i]
                    hb = 64 * (hh % 2)
                    blk = hh // 2
                    w = []
                    if i == 0:
                        w = [("dve", mask_end[bb])]
                        if bb == 0:
                            w = w + gw("pb_yacc", "pbx0", "pbx1", "pbx2", "pbx3")
                    g.em("tensor", _op("matmul", PB[:, 64 * hh:64 * (hh + 1)],
                                       m_gt[:, 512 * (bb % 2) + 128 * i:
                                            512 * (bb % 2) + 128 * (i + 1)],
                                       m_xs[:, 2048 * p + 64 * hh: 2048 * p + 64 * (hh + 1)],
                                       start=True, stop=True),
                         wait=w, inc=())
                    for cl in range(n_vchunks):
                        ci = 2 * p + cl
                        g.em("tensor", _op("matmul",
                                           PV[64 * (ci % 2):64 * (ci % 2) + 64, hh:hh + 1],
                                           m_act[hb:hb + 64,
                                                 QW * (16 + blk) + 64 * ci: QW * (16 + blk) + 64 * (ci + 1)],
                                           m_prevbf[hb:hb + 64, 9 * blk + ci:9 * blk + ci + 1],
                                           start=True, stop=True,
                                           tile_position=(hb, 64 * (ci % 2))),
                             wait=gw("pv_r") if (hh == 0 and cl == 0) else (),
                             inc=(("pe", 1),) if i == 3 and cl == n_vchunks - 1 else ())
                ydiag_end[bb] = g.now("pe")
        pe_all = g.now("pe")
        if SUBQ5[0] < 2:
            guard[f"yready{p}"] = ("pe", pe_all)
            return
        g.em("vector", _op("tensor_copy", out=m_vcol[:], in_=PV[:, 0:32]),
             wait=[("pe", pe_all)], inc=(("dvx", 1),))
        gset("pv_r", "dvx")
        vcol_dvx = g.now("dvx")
        ysl = p % 2
        for b in range(4):
            g.em("vector", _op("tensor_add",
                               m_y[:, 2048 * ysl + 512 * b: 2048 * ysl + 512 * (b + 1)]
                               .rearrange("p (hh t) -> p hh t", hh=8),
                               PB[:, 512 * b:512 * (b + 1)]
                               .rearrange("p (hh t) -> p hh t", hh=8),
                               m_vcol[:, 8 * b:8 * (b + 1)].unsqueeze(2)
                               .broadcast_to([128, 8, 64])),
                 wait=(gw(f"my{ysl}") + [("dvx", vcol_dvx)]) if b == 0 else (),
                 inc=(("dve", 1),) if b == 3 else ())
        gset("pb_yacc", "dve")
        guard[f"yready{p}"] = ("dve", g.now("dve"))

    def gen_q6_tilepair(q, t0, tiles, store_rows=None):
        if SUBQ5[0] < 3:
            return
        tmpB = m_ln1
        for ti, t in enumerate(tiles):
            PHASE[0] = f"q{q}.q6a_t{t}"
            ysl = t % 2
            ytile = m_y[:, 2048 * ysl:2048 * (ysl + 1)]
            ztile = m_z[:, 2048 * t:2048 * (t + 1)]
            g.em("scalar", _op("activation", out=m_tmpA[:], in_=ztile, func=AF.Silu,
                               bias=0.0, scale=1.0),
                 wait=[guard[f"yready{t}"]] + gw("m_tmpA"))
            g.em("scalar", _op("copy", out=st[:, 14:15], in_=m_tmpA[:, 0:1]),
                 inc=(("act", 1),))
            sz = ("act", g.now("act"))
            g.em("vector", _op("tensor_mul", tmpB[:], ytile, m_tmpA[:]),
                 wait=[sz] + gw("m_ln1"), inc=(("dve", 1),))
            gset(f"my{ysl}", "dve")
            g.em("vector", _op("tensor_mul", SQ, tmpB[:], tmpB[:]))
            g.em("vector", _op("tensor_reduce", out=st[:, 8:9], in_=SQ, axis=AX.X,
                               op=ALU.add), inc=(("dvx", 1),))
            g.em("vector", _op("tensor_scalar", out=st[:, 9:10], in0=st[:, 8:9],
                               scalar1=RD, scalar2=RMSEPS, op0=ALU.mult, op1=ALU.add),
                 wait=[("dvx", g.now("dvx"))], inc=(("dve", 1),))
            g.em("scalar", _op("activation", out=st[:, 10:11], in_=st[:, 9:10],
                               func=AF.Sqrt, bias=0.0, scale=1.0),
                 wait=[("dve", g.now("dve"))])
            g.em("scalar", _op("copy", out=st[:, 14:15], in_=st[:, 10:11]),
                 inc=(("act", 1),))
            g.em("vector", _op("reciprocal", out=st[:, 11:12], in_=st[:, 10:11]),
                 wait=[("act", g.now("act"))], inc=(("dve", 1),))
            g.em("scalar", _op("activation", out=m_tmpA[:], in_=tmpB[:], func=AF.Identity,
                               scale=st[:, 11:12], bias=0.0),
                 wait=[("dve", g.now("dve"))])
            g.em("scalar", _op("copy", out=st[:, 14:15], in_=st[:, 11:12]),
                 inc=(("act", 1),))
            g.em("vector", _op("tensor_mul", tmpB[:], m_tmpA[:], c_rmsbc[:]),
                 wait=[("act", g.now("act"))], inc=(("dve", 1),))
            gset("m_tmpA", "dve")
            yh = ("dve", g.now("dve"))
            for j0 in (0, 8):
                transpose_batch(
                    [(m_yT[:, 2048 * ti + 128 * j: 2048 * ti + 128 * (j + 1)],
                      tmpB[:, 128 * j:128 * (j + 1)]) for j in range(j0, j0 + 8)],
                    [yh])
            gset("m_ln1", "pe")
        a_ev = [("dve", g.now("dve")), ("act", g.now("act"))]
        if SUBQ5[0] < 4 and SUBQ5[0] not in (40, 41):
            return

        for j in range(4):
            PHASE[0] = f"q{q}.q6b_j{j}"
            wsl = j % 2
            for aa in range(4):
                dst_ap = (m_wbuf[:, 8192 * wsl + 2048 * aa:
                                 8192 * wsl + 2048 * (aa + 1)]
                          if WOUTDST[0] == 0 else
                          x1[0:128, 0:2048].bitcast(BF16)[:, 0:2048])
                g.em("sync", _op("dma_start",
                                 out=dst_ap,
                                 in_=cwout[:, 8192 * j + 2048 * aa:
                                            8192 * j + 2048 * (aa + 1)]),
                     wait=gw(f"mwbuf{wsl}") if aa == 0 else (),
                     inc=((f"ldw{wsl}", 16),))
            wld = g.now(f"ldw{wsl}")
            for ti, t in enumerate(tiles):
                if SUBQ5[0] == 40:
                    continue
                slot = PBq[(2 * j + ti) % 4]
                for dc in range(16):
                    g.em("tensor", _op("matmul", slot,
                                       m_yT[:, 2048 * ti + 128 * dc: 2048 * ti + 128 * (dc + 1)],
                                       m_wbuf[:, 8192 * wsl + 512 * dc: 8192 * wsl + 512 * (dc + 1)],
                                       start=(dc == 0), stop=(dc == 15)),
                         wait=([(f"ldw{wsl}", wld)] + a_ev
                               + gw(f"pbx{(2 * j + ti) % 4}", "pb_yacc")
                               if dc == 0 else ()),
                         inc=(("pe", 1),) if dc == 15 else ())
                gset(f"mwbuf{wsl}", "pe")
                mmv = ("pe", g.now("pe"))
                if SUBQ5[0] == 41:
                    g.em("vector", _op("tensor_copy", out=st[:, 12:13],
                                       in_=slot[:, 0:1]),
                         wait=[mmv], inc=(("dve", 1),))
                    guard[f"pbx{(2 * j + ti) % 4}"] = ("dve", g.now("dve"))
                    continue
                g.em("scalar", _op("copy",
                                   out=m_attn[:, 2048 * ti + 512 * j: 2048 * ti + 512 * (j + 1)],
                                   in_=slot),
                     wait=[mmv] + (gw(f"attn{ti}") if j == 0 else []), inc=(("act", 1),))
                guard[f"pbx{(2 * j + ti) % 4}"] = ("act", g.now("act"))
        attn_act = g.now("act")
        if SUBQ5[0] < 5 or SUBQ5[0] in (40, 41):
            return

        for ti, t in enumerate(tiles):
            rows = (t0 - MARG + 128 * t, t0 - MARG + 128 * (t + 1))
            sr = store_rows[ti] if store_rows else (max(rows[0], 0), min(rows[1], L))
            atile = m_attn[:, 2048 * ti:2048 * (ti + 1)]
            usl = t % 2
            ut = m_u[:, 2048 * usl:2048 * (usl + 1)]
            lo_off = sr[0] - rows[0]
            hi_off = sr[1] - rows[0]
            g.em("sync", _op("dma_start", out=ut[lo_off:hi_off, :],
                             in_=u[sr[0]:sr[1], :]),
                 wait=gw(f"mu{usl}"), inc=((f"ldu{usl}", 16),))
            uld = g.now(f"ldu{usl}")
            g.em("vector", None, wait=[("act", attn_act)])
            ln_stats(atile, LNEPS)
            g.em("scalar", _op("activation", out=SQ, in_=atile, func=AF.Identity,
                               scale=st[:, 7:8], bias=st[:, 8:9]),
                 wait=[("dve", g.now("dve"))])
            g.em("scalar", _op("copy", out=st[:, 14:15], in_=st[:, 7:8]),
                 inc=(("act", 1),))
            gset(f"attn{ti}", "act")
            g.em("vector", _op("tensor_add", ut, SQ, ut),
                 wait=[(f"ldu{usl}", uld), ("act", g.now("act"))], inc=(("dve", 1),))
            addv = ("dve", g.now("dve"))
            g.em("gpsimd", _op("dma_start", out=x1[sr[0]:sr[1], :],
                               in_=ut[lo_off:hi_off, :]),
                 wait=[addv], inc=((f"stu{usl}", 16),))
            gset(f"mu{usl}", f"stu{usl}")

    def gen_ssd_and_q6(q, t0, prevbf_done, q4_pe):
        for tp in (0, 2):
            for p in (tp, tp + 1):
                if p < SUBNP[0]:
                    gen_ssd_pair(q, p, prevbf_done)
            gen_q6_tilepair(q, t0, [tp, tp + 1])
        gset("prev_r", "pe")

    def gen_epilogue():
        t0 = L
        PHASE[0] = "epi"
        w = gw("mix_pe_prev")
        for dc in range(16):
            g.em("vector", _op("tensor_copy", out=ln1T_dc(dc)[:, 0:MARG],
                               in_=ln1T_dc(dc)[:, QT:QT + MARG]), wait=w)
        for b in range(32):
            g.em("vector", _op("tensor_copy", out=act_blk(b)[:, 0:63],
                               in_=act_blk(b)[:, QT:QT + 63]), wait=w)
        g.em("vector", _op("tensor_copy", out=m_prevbf[:], in_=m_prev[:]),
             inc=(("dve", 1),))
        prevbf_done = ("dve", g.now("dve"))
        rolls = ("dve", g.now("dve"))

        raw3 = ACC[1][:, 0:3]
        prev_pool = None
        for r in range(48):
            g.em("vector", _op("tensor_copy", out=raw3[:, 0:2],
                               in_=m_rawcarry[:, 2 * r:2 * r + 2]),
                 wait=(gw("acc") if r == 0 else []) +
                      ([prev_pool] if prev_pool else []))
            g.em("vector", _op("memset", raw3[:, 2:3], 0.0), inc=(("dve", 1),))
            rdv = ("dve", g.now("dve"))
            g.em("gpsimd", _op("tensor_mul", ACC[0][:, 0:1], raw3[:, 1:2],
                               c_convw[:, 48 + r:49 + r]),
                 wait=[rdv, ("act", g.now("act"))])
            g.em("gpsimd", _op("tensor_mul", ACC[0][:, 2:3], raw3[:, 0:1],
                               c_convw[:, r:r + 1]), inc=(("pox", 1),))
            g.em("gpsimd", _op("tensor_add", ACC[0][:, 1:2], ACC[0][:, 0:1],
                               ACC[0][:, 2:3]),
                 wait=[("pox", g.now("pox"))], inc=(("pool", 1),))
            cdv = ("pool", g.now("pool"))
            prev_pool = cdv
            xs_chunk = r < 16
            if xs_chunk:
                c = r
                g.em("vector", _op("tensor_copy", out=m_xstemp[:, 0:63],
                                   in_=m_xscarry[:, 64 * c:64 * c + 63]),
                     wait=([rolls] if c == 0 else []) + gw("xstemp_r"),
                     inc=(("dve", 1),))
                cin = ("dve", g.now("dve"))
                g.em("scalar", _op("activation", out=m_xstemp[:, 63:64],
                                   in_=ACC[0][:, 1:2], func=AF.Silu,
                                   bias=c_convw[:, 144 + r:145 + r], scale=1.0),
                     wait=[cdv, cin])
                g.em("scalar", _op("copy", out=st[:, 14:15], in_=m_xstemp[:, 63:64]),
                     inc=(("act", 1),))
                sd = ("act", g.now("act"))
                transpose_batch([(m_xs[:, 128 * c:128 * (c + 1)],
                                  m_xstemp[:, 0:128])], [sd])
                gset("xstemp_r", "pe")
            else:
                b = (r - 16) if r < 32 else (16 + r - 32)
                g.em("scalar", _op("activation", out=act_blk(b)[:, 63:64],
                                   in_=ACC[0][:, 1:2], func=AF.Silu,
                                   bias=c_convw[:, 144 + r:145 + r], scale=1.0),
                     wait=[cdv, rolls])
                g.em("scalar", _op("copy", out=st[:, 14:15], in_=act_blk(b)[:, 63:64]),
                     inc=(("act", 1),))
        conv_done = {"act": g.now("act"), "pe": g.now("pe"), "dve": g.now("dve")}

        for si, s in enumerate(range(4)):
            wsl = si % 2
            wv = m_wbuf[:, 8192 * wsl:8192 * (wsl + 1)].rearrange(
                "p (dc j) -> p dc j", dc=16)
            for a in range(4):
                g.em("sync", _op("dma_start", out=wv[:, 4 * a:4 * a + 4, :],
                                 in_=cwin[:, 8192 * s + 2048 * a: 8192 * s + 2048 * (a + 1)]
                                 .rearrange("p (dc j) -> p dc j", dc=4)),
                     wait=gw(f"mwbuf{wsl}") if a == 0 else (), inc=((f"ldw{wsl}", 16),))
            slab_ld = g.now(f"ldw{wsl}")
            for c4 in range(4):
                oc = 4 * s + c4
                slot, sname = next_group_slot()
                for dc in range(16):
                    g.em("tensor", _op("matmul", slot[:, 0:128],
                                       wv[:, dc, 128 * c4:128 * (c4 + 1)],
                                       ln1T_dc(dc)[:, 0:128],
                                       start=(dc == 0), stop=(dc == 15)),
                         wait=([(f"ldw{wsl}", slab_ld), ("dve", conv_done["dve"]),
                                ("act", conv_done["act"])] + gw(sname))
                         if dc == 0 else (),
                         inc=(("pe", 1),) if dc == 15 else ())
                gset(f"mwbuf{wsl}", "pe")
                mm_done = ("pe", g.now("pe"))
                zt = ztemp_ep[oc % 2]
                g.em("vector", _op("tensor_copy", out=zt[:, 0:128], in_=slot[:, 0:128]),
                     wait=[mm_done] + gw(f"ztemp{oc % 2}"), inc=(("dve", 1),))
                gset(sname, "dve")
                ev = ("dve", g.now("dve"))
                transpose_batch([(m_z[:, 128 * oc:128 * (oc + 1)], zt[:, 0:128])],
                                [ev])
                gset(f"ztemp{oc % 2}", "pe")

        gen_ssd_pair(4, 0, prevbf_done, n_vchunks=1)
        gen_q6_tilepair(4, t0, [0])
        g.barrier()
        if taps and TAPQ[0] == 4:
            for nm, buf in (("d_y", m_y), ("d_z", m_z), ("d_xs", m_xs),
                            ("d_act", m_act), ("d_prev", m_prev),
                            ("d_ln1T", m_ln1T)):
                g.em("gpsimd", _op("dma_start", out=tapd[nm], in_=buf[:]),
                     inc=(("stt", 16),))
            g.barrier()

    ztemp_ep = None

    nq = {"prologue": 0, "q0": 1, "q1": 2, "q2": 3, "q3": 4}.get(stop_after, 4)
    for q in range(nq):
        gen_quarter(q)
    ztemp_ep = [m_ln1[:, 0:512], m_ln1[:, 512:1024]]
    if stop_after in ("full", "mixer", "ffn1"):
        gen_epilogue()

    stk_mix.close()
    g.barrier()

    stk_ffn = ExitStack()
    f_big1 = sb(stk_ffn, "f_big1", [128, 32768], BF16)
    f_big2 = sb(stk_ffn, "f_big2", [128, 32768], BF16)
    f_mid = sb(stk_ffn, "f_mid", [128, 16384], BF16)
    f_gtmp = sb(stk_ffn, "f_gtmp", [128, 1024], BF16)
    f_utmp = sb(stk_ffn, "f_utmp", [128, 1024], BF16)
    assert nc.sbuf_bytes_remaining >= 0

    ln2T = f_mid[:, 0:8192]
    x1f1 = [f_mid[:, 8192 + 4096 * i:8192 + 4096 * (i + 1)].bitcast(F32)
            for i in range(2)]
    down_stage = f_mid[:, 0:16384].bitcast(F32)
    x1f4 = [f_big1[:, 4096 * i:4096 * (i + 1)].bitcast(F32) for i in range(2)]
    ln2stage = f_big1[:, 0:2048]

    def gen_ffn_pass(pi):
        p0 = QT * pi
        for k in range(4):
            PHASE[0] = f"f{pi}.F1k{k}"
            sl = k % 2
            xt = x1f1[sl]
            g.em("sync", _op("dma_start", out=xt,
                             in_=x1[p0 + 128 * k:p0 + 128 * (k + 1), :]),
                 wait=gw(f"fx1{sl}"), inc=((f"ldu{sl}", 16),))
            ldv = g.now(f"ldu{sl}")
            g.em("vector", None, wait=[(f"ldu{sl}", ldv)])
            ln_stats(xt, LNEPS)
            g.em("scalar", _op("activation", out=ln2stage, in_=xt, func=AF.Identity,
                               scale=st[:, 7:8], bias=st[:, 8:9]),
                 wait=[("dve", g.now("dve"))] + gw("ln2stage"))
            g.em("scalar", _op("copy", out=st[:, 14:15], in_=st[:, 7:8]),
                 inc=(("act", 1),))
            gset(f"fx1{sl}", "act")
            ap = ("act", g.now("act"))
            for j0 in (0, 8):
                transpose_batch(
                    [(ln2T[:, 512 * j + 128 * k: 512 * j + 128 * (k + 1)],
                      ln2stage[:, 128 * j:128 * (j + 1)]) for j in range(j0, j0 + 8)],
                    [ap])
            gset("ln2stage", "pe")
        f1_ev = [("dve", g.now("dve")), ("act", g.now("act"))]

        upslots = [PV[:], PG[:]]
        for s in range(16):
            PHASE[0] = f"f{pi}.F2s{s}"
            wsl = s % 2
            wgb = f_big2[:, 8192 * wsl:8192 * (wsl + 1)]
            wub = f_big2[:, 16384 + 8192 * wsl:16384 + 8192 * (wsl + 1)]
            bgw = bg_ready.get("wgwu", []) if (pi == 0 and s < 2) else []
            g.em("sync", _op("dma_start", out=wgb,
                             in_=cwg[:, 8192 * s:8192 * (s + 1)]),
                 wait=gw(f"fwg{wsl}") + bgw, inc=((f"ldw{wsl}", 16),))
            g.em("sync", _op("dma_start", out=wub,
                             in_=cwu[:, 8192 * s:8192 * (s + 1)]),
                 wait=gw(f"fwu{wsl}") + bgw, inc=((f"ldv{wsl}", 16),))
            slab_ld = [(f"ldw{wsl}", g.now(f"ldw{wsl}")), (f"ldv{wsl}", g.now(f"ldv{wsl}"))]
            for c4 in range(4):
                oc = 4 * s + c4
                gslot = PBq[oc % 4]
                uslot = upslots[oc % 2]
                for hc in range(16):
                    g.em("tensor", _op("matmul", gslot,
                                       wgb.rearrange("p (hc j) -> p hc j", hc=16)[:, hc, 128 * c4:128 * (c4 + 1)],
                                       ln2T[:, 512 * hc:512 * (hc + 1)],
                                       start=(hc == 0), stop=(hc == 15)),
                         wait=(slab_ld + f1_ev + gw(f"fpb{oc % 4}"))
                         if hc == 0 else (),
                         inc=(("pe", 1),) if hc == 15 else ())
                gmm = ("pe", g.now("pe"))
                for hc in range(16):
                    g.em("tensor", _op("matmul", uslot,
                                       wub.rearrange("p (hc j) -> p hc j", hc=16)[:, hc, 128 * c4:128 * (c4 + 1)],
                                       ln2T[:, 512 * hc:512 * (hc + 1)],
                                       start=(hc == 0), stop=(hc == 15)),
                         wait=gw(f"fup{oc % 2}") if hc == 0 else (),
                         inc=(("pe", 1),) if hc == 15 else ())
                gset(f"fwg{wsl}", "pe")
                gset(f"fwu{wsl}", "pe")
                umm = ("pe", g.now("pe"))
                gt = f_gtmp[:, 512 * (oc % 2):512 * (oc % 2 + 1)]
                ut = f_utmp[:, 512 * (oc % 2):512 * (oc % 2 + 1)]
                g.em("scalar", _op("activation", out=gt, in_=gslot, func=AF.Silu,
                                   bias=0.0, scale=c_sgate[:, oc:oc + 1]),
                     wait=[gmm] + gw(f"fgt{oc % 2}"))
                g.em("scalar", _op("copy", out=st[:, 14:15], in_=gt[:, 0:1]),
                     inc=(("act", 1),))
                guard[f"fpb{oc % 4}"] = ("act", g.now("act"))
                sg = ("act", g.now("act"))
                g.em("vector", _op("tensor_scalar", out=ut, in0=uslot,
                                   scalar1=c_sup[:, oc:oc + 1], scalar2=None,
                                   op0=ALU.mult),
                     wait=[umm] + gw(f"fut{oc % 2}"), inc=(("dve", 1),))
                guard[f"fup{oc % 2}"] = ("dve", g.now("dve"))
                g.em("vector", _op("tensor_mul",
                                   f_big1[:, 512 * oc:512 * (oc + 1)], gt, ut),
                     wait=[sg], inc=(("dve", 1),))
                guard[f"fgt{oc % 2}"] = ("dve", g.now("dve"))
                guard[f"fut{oc % 2}"] = ("dve", g.now("dve"))
        f2_dve = g.now("dve")

        for s in range(8):
            PHASE[0] = f"f{pi}.F3s{s}"
            wsl = s % 2
            wdb = f_big2[:, 16384 * wsl:16384 * (wsl + 1)]
            g.em("sync", _op("dma_start", out=wdb,
                             in_=cwd[:, 16384 * s:16384 * (s + 1)]),
                 wait=gw(f"fwd{wsl}")
                 + (gw("fwg0", "fwg1") if (s % 2 == 0) else gw("fwu0", "fwu1"))
                 + ([("dve", f2_dve)] if s < 2 else [])
                 + (bg_ready.get("wd", []) if (pi == 0 and s < 2) else []),
                 inc=((f"ldw{wsl}", 16),))
            slab_ld = g.now(f"ldw{wsl}")
            wdv = wdb.rearrange("p (oc j) -> p oc j", oc=64)
            for k in range(4):
                slot = PBq[(4 * s + k) % 4]
                for oc in range(64):
                    g.em("tensor", _op("matmul", slot[:, 0:256],
                                       f_big1[:, 512 * oc + 128 * k: 512 * oc + 128 * (k + 1)],
                                       wdv[:, oc, :],
                                       start=(oc == 0), stop=(oc == 63)),
                         wait=([(f"ldw{wsl}", slab_ld), ("dve", f2_dve)]
                               + gw(f"fpb{(4 * s + k) % 4}", "dstage")) if oc == 0 else (),
                         inc=(("pe", 1),) if oc == 63 else ())
                gset(f"fwd{wsl}", "pe")
                mmv = ("pe", g.now("pe"))
                g.em("vector", _op("tensor_mul",
                                   down_stage[:, 2048 * k + 256 * s: 2048 * k + 256 * (s + 1)],
                                   slot[:, 0:256], c_sdbc[:, 256 * s:256 * (s + 1)]),
                     wait=[mmv], inc=(("dve", 1),))
                guard[f"fpb{(4 * s + k) % 4}"] = ("dve", g.now("dve"))
        f3_dve = g.now("dve")

        for k in range(4):
            sl = k % 2
            xt = x1f4[sl]
            g.em("sync", _op("dma_start", out=xt,
                             in_=x1[p0 + 128 * k:p0 + 128 * (k + 1), :]),
                 wait=gw(f"fx4{sl}") + ([("dve", f3_dve)] if k < 2 else []),
                 inc=((f"ldu{sl}", 16),))
            ldv = g.now(f"ldu{sl}")
            dtile = down_stage[:, 2048 * k:2048 * (k + 1)]
            g.em("vector", None, wait=[("dve", f3_dve)])
            ln_stats(dtile, LNEPS)
            g.em("scalar", _op("activation", out=SQ, in_=dtile, func=AF.Identity,
                               scale=st[:, 7:8], bias=st[:, 8:9]),
                 wait=[("dve", g.now("dve"))])
            g.em("scalar", _op("copy", out=st[:, 14:15], in_=st[:, 7:8]),
                 inc=(("act", 1),))
            gset("dstage", "act")
            g.em("vector", _op("tensor_add", xt, SQ, xt),
                 wait=[(f"ldu{sl}", ldv), ("act", g.now("act"))], inc=(("dve", 1),))
            addv = ("dve", g.now("dve"))
            g.em("gpsimd", _op("dma_start", out=out[p0 + 128 * k:p0 + 128 * (k + 1), :],
                               in_=xt),
                 wait=[addv], inc=((f"stu{sl}", 16),))
            gset(f"fx4{sl}", f"stu{sl}")
        g.barrier()

    bg_emit(len(bg_units))
    nf = {"full": 4, "ffn1": 1}.get(stop_after, 0)
    for pi in range(nf):
        gen_ffn_pass(pi)

    stk_ffn.close()

    final_waits = [(c, g.cnt[c]) for c in list(g.cnt) if c.startswith(("st", "ld"))]
    g.em("gpsimd", None, wait=final_waits)

    global _LAST_GEN
    _LAST_GEN = g

    with stk_root:
        sems = {}
        sem_stack = ExitStack()
        with sem_stack:
            for chn in list(g.cnt.keys()):
                sems[chn] = sem_stack.enter_context(nc.semaphore(f"s_{chn}"))
            with nc.Block() as block:
                def mk(name):
                    def body(eng):
                        for fn, waits, incs in g.ops[name]:
                            for chn, v in waits:
                                eng.wait_ge(sems[chn], v)
                            ins = fn(eng) if fn is not None else None
                            for chn, amt in incs:
                                ins.then_inc(sems[chn], amt)
                    return body
                block.sync(mk("sync"))
                block.tensor(mk("tensor"))
                block.vector(mk("vector"))
                block.scalar(mk("scalar"))
                block.gpsimd(mk("gpsimd"))
    return nc


def _host_inputs(inputs):
    bf = ml_dtypes.bfloat16
    W_in = np.asarray(inputs["L0_attn_W_in"], dtype=np.float32)
    W_out = np.asarray(inputs["L0_attn_W_out"], dtype=np.float32)
    conv_w = np.asarray(inputs["L0_attn_conv_w"], dtype=np.float32)
    conv_b = np.asarray(inputs["L0_attn_conv_b"], dtype=np.float32)
    A_log = np.asarray(inputs["L0_attn_A_log"], dtype=np.float32)
    rms_w = np.asarray(inputs["L0_attn_rms_w"], dtype=np.float32)
    s_gate = np.asarray(inputs["L0_ffn_s_gate"], dtype=np.float32)
    s_up = np.asarray(inputs["L0_ffn_s_up"], dtype=np.float32)
    s_down = np.asarray(inputs["L0_ffn_s_down"], dtype=np.float32)

    shared = {
        "w_inT": np.ascontiguousarray(W_in.T),
        "w_outT": np.ascontiguousarray(W_out.T),
        "wg8": np.ascontiguousarray(np.asarray(inputs["L0_ffn_w_gate"], dtype=np.int8)),
        "wu8": np.ascontiguousarray(np.asarray(inputs["L0_ffn_w_up"], dtype=np.int8)),
        "wd8": np.ascontiguousarray(np.asarray(inputs["L0_ffn_w_down"], dtype=np.int8)),
        "in_ident": np.eye(128, dtype=bf),
        "in_gmask": np.asarray(
            (np.arange(128)[:, None] // 64 == np.arange(128)[None, :] // 64)
            & (np.arange(128)[None, :] >= np.arange(128)[:, None]), dtype=bf),
        "in_convw": np.ascontiguousarray(np.concatenate(
            [conv_w[:, 0, k].reshape(48, 128).T for k in range(3)]
            + [conv_b.reshape(48, 128).T], axis=1)),
        "in_sgate": np.ascontiguousarray(s_gate.reshape(64, 128).T),
        "in_sup": np.ascontiguousarray(s_up.reshape(64, 128).T),
        "in_sdbc": np.ascontiguousarray(np.tile(s_down[None, :], (128, 1))),
        "in_rmsbc": np.ascontiguousarray(np.tile(rms_w[None, :], (128, 1))),
        "in_arep": np.ascontiguousarray(np.repeat(A_log, 64).reshape(16, 128).T),
    }
    u_all = np.asarray(inputs["L0_attn_u"], dtype=np.float32)
    in_maps = []
    for b in range(8):
        m = {"u": np.ascontiguousarray(u_all[b])}
        m.update(shared)
        in_maps.append(m)
    return in_maps


def kernel(**inputs):
    in_maps = _host_inputs(inputs)
    nc = build()
    res = run_bass_kernel_spmd(nc, in_maps, core_ids=list(range(8)))
    return np.stack([np.asarray(res.results[b]["out"]) for b in range(8)], axis=0)


# version 10
# speedup vs baseline: 1.3062x; 1.3062x over previous
import numpy as np
import ml_dtypes
import concourse.bass as bass
import concourse.mybir as mybir
from concourse.bass_utils import run_bass_kernel_spmd
from contextlib import ExitStack

F32 = mybir.dt.float32
BF16 = mybir.dt.bfloat16
I8 = mybir.dt.int8
I32 = mybir.dt.int32
AF = mybir.ActivationFunctionType
ALU = mybir.AluOpType
AX = mybir.AxisListType

D = 2048
L = 2048
NH = 32
DH = 64
CH = 64
QT = 512
MARG = 64
QW = QT + MARG
LNEPS = 1e-5
RMSEPS = 1.1920929e-07
RD = 1.0 / D

ENGINES = ("sync", "tensor", "vector", "scalar", "gpsimd")
OWN = {"tensor": "pe", "vector": "dve", "scalar": "act", "gpsimd": "pool"}


class Gen:

    def __init__(self):
        self.ops = {e: [] for e in ENGINES}
        import collections
        self.cnt = collections.defaultdict(int)
        for c in ("pe", "dve", "act", "pool"):
            self.cnt[c] = 0
        self.pend = {e: {} for e in ENGINES}
        self.ptags = []

    def now(self, ch):
        return self.cnt[ch]

    def snap(self):
        return dict(self.cnt)

    def barrier(self, snap=None, exclude=()):
        s = snap or self.snap()
        for e in ENGINES:
            for ch, v in s.items():
                if OWN.get(e) == ch or ch in exclude:
                    continue
                if v > self.pend[e].get(ch, 0):
                    self.pend[e][ch] = v

    def em(self, eng, fn, wait=(), inc=()):
        if eng == "tensor":
            self.ptags.append(PHASE[0])
        wd = dict(self.pend[eng])
        self.pend[eng] = {}
        for ch, v in wait:
            if v is None:
                continue
            if OWN.get(eng) == ch:
                continue
            if v > wd.get(ch, 0):
                wd[ch] = v
        post = {}
        for ch, amt in inc:
            self.cnt[ch] += amt
            post[ch] = self.cnt[ch]
        if fn is None and inc:
            raise RuntimeError("inc requires a real instruction")
        self.ops[eng].append((fn, tuple(wd.items()), tuple(inc)))
        return post


_LAST_GEN = None


def _op(method, *args, **kwargs):
    return lambda eng: getattr(eng, method)(*args, **kwargs)


SUBSTOP = [3]
SUBQ2 = [3]
SUBQ5 = [5]
SUBG = [2]
SUBNP = [4]
SUBB4 = [9]
EARLYTAP = [1]
CONVENG = ["gpsimd"]
TREVAC = ["mixed"]
DRAINFIX = [0]
DBGK = [0]
DBGWHAT = ["u"]
SUBQ4 = [4]
ZTMP = [0]
WAITPAD = [0]
PHASE = ["?"]
TAPQ = [0]
WOUTDST = [0]


def build(dump_x1=False, stop_after="full", taps=False):
    nc = bass.Bass("TRN2", target_bir_lowering=False)

    u = nc.dram_tensor("u", [L, D], F32, kind="ExternalInput").ap()
    w_inT = nc.dram_tensor("w_inT", [D, 8224], F32, kind="ExternalInput").ap()
    w_outT = nc.dram_tensor("w_outT", [D, D], F32, kind="ExternalInput").ap()
    wg8 = nc.dram_tensor("wg8", [D, 8192], I8, kind="ExternalInput").ap()
    wu8 = nc.dram_tensor("wu8", [D, 8192], I8, kind="ExternalInput").ap()
    wd8 = nc.dram_tensor("wd8", [8192, D], I8, kind="ExternalInput").ap()
    in_ident = nc.dram_tensor("in_ident", [128, 128], BF16, kind="ExternalInput").ap()
    in_gmask = nc.dram_tensor("in_gmask", [128, 128], BF16, kind="ExternalInput").ap()
    in_convw = nc.dram_tensor("in_convw", [128, 192], F32, kind="ExternalInput").ap()
    in_sgate = nc.dram_tensor("in_sgate", [128, 64], F32, kind="ExternalInput").ap()
    in_sup = nc.dram_tensor("in_sup", [128, 64], F32, kind="ExternalInput").ap()
    in_sdbc = nc.dram_tensor("in_sdbc", [128, D], F32, kind="ExternalInput").ap()
    in_rmsbc = nc.dram_tensor("in_rmsbc", [128, D], F32, kind="ExternalInput").ap()
    in_arep = nc.dram_tensor("in_arep", [128, 16], F32, kind="ExternalInput").ap()

    out = nc.dram_tensor("out", [L, D], F32, kind="ExternalOutput").ap()
    tap_specs = [("d_ln1T", 9216, BF16), ("d_act", 18432, BF16), ("d_z", 8192, BF16),
                 ("d_xs", 8192, BF16), ("d_y", 4096, BF16), ("d_cs", 128, F32),
                 ("d_prev", 144, F32), ("d_attn", 4096, F32), ("d_u", 4096, F32),
                 ("d_yT", 4096, BF16), ("d_ln1T2", 9216, BF16), ("d_dbgu", 2048, F32)]
    tapd = {}
    if taps:
        for nm, w, dtp in tap_specs:
            tapd[nm] = nc.dram_tensor(nm, [128, w], dtp, kind="ExternalOutput").ap()
        tapd["d_lnt"] = nc.dram_tensor("d_lnt", [512, D], BF16, kind="ExternalOutput").ap()
        tapd["d_ut"] = nc.dram_tensor("d_ut", [512, D], F32, kind="ExternalOutput").ap()
    x1 = nc.dram_tensor("x1", [L, D], F32,
                        kind="ExternalOutput" if dump_x1 else "Internal").ap()
    cwin = nc.dram_tensor("cwin", [128, 16 * 16 * 512], BF16, kind="Internal").ap()
    cwout = nc.dram_tensor("cwout", [128, 4 * 16 * 512], BF16, kind="Internal").ap()
    cwg = nc.dram_tensor("cwg", [128, 16 * 16 * 512], BF16, kind="Internal").ap()
    cwu = nc.dram_tensor("cwu", [128, 16 * 16 * 512], BF16, kind="Internal").ap()
    cwd = nc.dram_tensor("cwd", [128, 8 * 64 * 256], BF16, kind="Internal").ap()

    cwin_v = cwin.rearrange("p (s dc j) -> p s dc j", s=16, dc=16)
    cwout_v = cwout.rearrange("p (q dc j) -> p q dc j", q=4, dc=16)
    cwg_v = cwg.rearrange("p (s hc j) -> p s hc j", s=16, hc=16)
    cwu_v = cwu.rearrange("p (s hc j) -> p s hc j", s=16, hc=16)
    cwd_v = cwd.rearrange("p (s oc j) -> p s oc j", s=8, oc=64)

    g = Gen()
    stk_root = ExitStack()

    def sb(stack, name, shape, dtype):
        return stack.enter_context(nc.sbuf_tensor(name, shape, dtype))

    c_ident = sb(stk_root, "c_ident", [128, 128], BF16)
    c_gmask = sb(stk_root, "c_gmask", [128, 128], BF16)
    c_convw = sb(stk_root, "c_convw", [128, 192], F32)
    c_sgate = sb(stk_root, "c_sgate", [128, 64], F32)
    c_sup = sb(stk_root, "c_sup", [128, 64], F32)
    c_sdbc = sb(stk_root, "c_sdbc", [128, D], F32)
    c_rmsbc = sb(stk_root, "c_rmsbc", [128, D], F32)
    c_arep = sb(stk_root, "c_arep", [128, 16], F32)
    c_decay = sb(stk_root, "c_decay", [128, 16], F32)
    scr = sb(stk_root, "scr", [128, 2080], F32)
    st = sb(stk_root, "st", [128, 16], F32)

    SQ = scr[:, 0:2048]
    RAW = [scr[:, 0:516], scr[:, 516:1032]]
    ACC = [scr[:, 1032:1544], scr[:, 1544:2056]]

    PB = stk_root.enter_context(nc.psum_tensor("PB", [128, 2048], F32))
    PX = stk_root.enter_context(nc.psum_tensor("PX", [128, 1024], BF16))
    PG = stk_root.enter_context(nc.psum_tensor("PG", [128, 512], F32))
    PTR = stk_root.enter_context(nc.psum_tensor("PTR", [128, 1024], BF16))
    PV = stk_root.enter_context(nc.psum_tensor("PV", [128, 512], F32))
    PBq = [PB[:, 512 * i:512 * (i + 1)] for i in range(4)]

    guard = {}

    def gw(*names):
        out = []
        for n in names:
            if n in guard:
                v = guard[n]
                out.extend(v if isinstance(v, list) else [v])
        return out

    def gset(name, ch):
        guard[name] = (ch, g.now(ch))

    PHASE[0] = "prologue"
    g.em("scalar", _op("copy", out=st[:, 14:15], in_=st[:, 14:15]))
    for dst, src in ((c_ident, in_ident), (c_gmask, in_gmask), (c_convw, in_convw),
                     (c_sgate, in_sgate), (c_sup, in_sup), (c_sdbc, in_sdbc),
                     (c_rmsbc, in_rmsbc), (c_arep, in_arep)):
        g.em("sync", _op("dma_start", out=dst[:], in_=src), inc=(("ldc", 16),))
    consts_ld = g.now("ldc")
    g.em("scalar", _op("activation", out=c_decay[:], in_=c_arep[:], func=AF.Exp,
                       bias=0.0, scale=63.0),
         wait=[("ldc", consts_ld)])
    g.em("scalar", _op("copy", out=st[:, 14:15], in_=c_decay[:, 0:1]),
         inc=(("act", 1),))

    stk_pro = ExitStack()
    pw_f32 = [sb(stk_pro, f"pw_f32_{i}", [128, 8192], F32) for i in range(2)]
    pw_bf = [sb(stk_pro, f"pw_bf_{i}", [128, 8192], BF16) for i in range(2)]
    pw_i8 = [sb(stk_pro, f"pw_i8_{i}", [128, 8192], I8) for i in range(2)]

    cast_rr = ["vector", "scalar"]
    cast_ch = {"vector": "dve", "scalar": "act", "gpsimd": "pool"}
    cast_i = [0]

    def cast_op(dst_ap, src_ap, wait):
        e = cast_rr[cast_i[0] % len(cast_rr)]
        cast_i[0] += 1
        ch = cast_ch[e]
        if e == "scalar":
            fn = _op("copy", out=dst_ap, in_=src_ap)
        else:
            fn = _op("tensor_copy", out=dst_ap, in_=src_ap)
        g.em(e, fn, wait=wait, inc=((ch, 1),))
        return (ch, g.now(ch))

    def prologue_stream(n_blocks, load_src_fn, in_bufs, width, store_dst_fn, nsplit):
        for b in range(n_blocks):
            sl = b % 2
            ib = in_bufs[sl][:, 0:width]
            ob = pw_bf[sl][:, 0:width]
            half = width // 2
            src_ap = load_src_fn(b)
            g.em("sync", _op("dma_start", out=ib[:, 0:half], in_=src_ap[:, 0:half]),
                 wait=gw(f"pin{sl}"), inc=((f"ldp{sl}", 16),))
            g.em("sync", _op("dma_start", out=ib[:, half:width],
                             in_=src_ap[:, half:width]),
                 inc=((f"ldu{sl}", 16),))
            ld_w = [(f"ldp{sl}", g.now(f"ldp{sl}")), (f"ldu{sl}", g.now(f"ldu{sl}"))]
            cw = cast_op(ob, ib, ld_w + gw(f"pbf{sl}"))
            guard[f"pin{sl}"] = cw
            srcs, dsts = store_dst_fn(b, ob)
            if len(srcs) == 1:
                sap, dap = srcs[0], dsts[0]
                hs = sap.shape[1] // 2
                srcs = [sap[:, 0:hs, :], sap[:, hs:, :]]
                dsts = [dap[:, 0:hs, :], dap[:, hs:, :]]
            st_lanes = [f"stp{sl}", f"stu{sl}"]
            for i, (sap, dap) in enumerate(zip(srcs, dsts)):
                g.em("gpsimd", _op("dma_start", out=dap, in_=sap),
                     wait=[cw], inc=((st_lanes[i % 2], 16),))
            guard[f"pbf{sl}"] = [(f"stp{sl}", g.now(f"stp{sl}")),
                                 (f"stu{sl}", g.now(f"stu{sl}"))]

    def win_store(b, ob):
        srcs, dsts = [], []
        obv = ob.rearrange("p (s j) -> p s j", s=16)
        for a in range(4):
            srcs.append(obv[:, 4 * a:4 * a + 4, :])
            dsts.append(cwin_v[:, 4 * a:4 * a + 4, b, :])
        return srcs, dsts

    prologue_stream(16, lambda b: w_inT[128 * b:128 * (b + 1), 0:8192],
                    pw_f32, 8192, win_store, 4)

    def wout_store(b, ob):
        obv = ob.rearrange("p (q j) -> p q j", q=4)
        return [obv], [cwout_v[:, :, b, :]]

    prologue_stream(16, lambda b: w_outT[128 * b:128 * (b + 1), :],
                    pw_f32, 2048, wout_store, 1)

    def mk_wguw_store(cv):
        def f(b, ob):
            srcs, dsts = [], []
            obv = ob.rearrange("p (s j) -> p s j", s=16)
            for a in range(4):
                srcs.append(obv[:, 4 * a:4 * a + 4, :])
                dsts.append(cv[:, 4 * a:4 * a + 4, b, :])
            return srcs, dsts
        return f

    stk_pro.close()
    g.barrier()

    bg_i8 = [sb(stk_root, f"bg_i8_{i}", [128, 1024], I8) for i in range(2)]
    bg_bf = [sb(stk_root, f"bg_bf_{i}", [128, 1024], BF16) for i in range(2)]
    bg_units = []
    bg_ready = {}

    def mk_bg_unit(src_ap, dst_fn, tag):
        def emit(ui):
            sl = ui % 2
            g.em("gpsimd", _op("dma_start", out=bg_i8[sl][:], in_=src_ap),
                 wait=gw(f"bgb{sl}"), inc=((f"ldb{sl}", 16),))
            ld = g.now(f"ldb{sl}")
            g.em("gpsimd", _op("tensor_copy", out=bg_bf[sl][:], in_=bg_i8[sl][:]),
                 wait=[(f"ldb{sl}", ld)], inc=(("pox", 1),))
            src4, dst4 = dst_fn(bg_bf[sl])
            g.em("gpsimd", _op("dma_start", out=dst4, in_=src4),
                 wait=[("pox", g.now("pox"))], inc=((f"stb{sl}", 16),))
            gset(f"bgb{sl}", f"stb{sl}")
            if tag:
                bg_ready[tag] = [("stb0", g.now("stb0")), ("stb1", g.now("stb1"))]
        return emit

    for b in range(16):
        for a in range(4):
            for h in range(2):
                for w8, cv in ((wg8, cwg_v), (wu8, cwu_v)):
                    def mk_dst(cv=cv, b=b, a=a, h=h):
                        def f(ob):
                            return (ob.rearrange("p (s j) -> p s j", s=2),
                                    cv[:, 4 * a + 2 * h:4 * a + 2 * h + 2, b, :])
                        return f
                    last = (b == 15 and a == 3 and h == 1 and cv is cwu_v)
                    bg_units.append(mk_bg_unit(
                        w8[128 * b:128 * (b + 1),
                           2048 * a + 1024 * h:2048 * a + 1024 * (h + 1)],
                        mk_dst(), "wgwu" if last else None))
    for b in range(64):
        for h in range(2):
            def mk_dstd(b=b, h=h):
                def f(ob):
                    return (ob.rearrange("p (s j) -> p s j", s=4),
                            cwd_v[:, 4 * h:4 * h + 4, b, :])
                return f
            bg_units.append(mk_bg_unit(
                wd8[128 * b:128 * (b + 1), 1024 * h:1024 * (h + 1)], mk_dstd(),
                "wd" if (b == 63 and h == 1) else None))
    bg_i = [0]

    def bg_emit(k):
        while k > 0 and bg_i[0] < len(bg_units):
            bg_units[bg_i[0]](bg_i[0])
            bg_i[0] += 1
            k -= 1

    stk_mix = ExitStack()
    m_ln1T = sb(stk_mix, "m_ln1T", [128, 16 * QW], BF16)
    m_act = sb(stk_mix, "m_act", [128, 32 * QW], BF16)
    m_xs = sb(stk_mix, "m_xs", [128, 4 * 2048], BF16)
    m_z = sb(stk_mix, "m_z", [128, 4 * 2048], BF16)
    m_y = sb(stk_mix, "m_y", [128, 2 * 2048], BF16)
    m_xstemp = sb(stk_mix, "m_xstemp", [128, QW], BF16)
    m_xscarry = sb(stk_mix, "m_xscarry", [128, 16 * 64], BF16)
    m_rawcarry = sb(stk_mix, "m_rawcarry", [128, 96], F32)
    m_cs = sb(stk_mix, "m_cs", [128, 128], F32)
    m_prev = sb(stk_mix, "m_prev", [128, 144], F32)
    m_prevbf = sb(stk_mix, "m_prevbf", [128, 144], BF16)
    m_gt = sb(stk_mix, "m_gt", [128, 1024], BF16)
    m_vcol = sb(stk_mix, "m_vcol", [128, 32], F32)
    m_wbuf = sb(stk_mix, "m_wbuf", [128, 16384], BF16)
    m_u = sb(stk_mix, "m_u", [128, 2 * 2048], F32)
    m_ln1 = sb(stk_mix, "m_ln1", [128, 2048], BF16)
    m_yT = sb(stk_mix, "m_yT", [128, 2 * 2048], BF16)
    m_tmpA = sb(stk_mix, "m_tmpA", [128, 2048], BF16)
    m_attn = sb(stk_mix, "m_attn", [128, 2 * 2048], BF16)
    m_dbgu = m_attn.bitcast(F32)[:, 0:2048]

    sbuf_left = nc.sbuf_bytes_remaining
    assert sbuf_left >= 0, f"SBUF over budget: {sbuf_left}"

    def ln1T_dc(dc):
        return m_ln1T[:, QW * dc: QW * (dc + 1)]

    def act_blk(b):
        return m_act[:, QW * b: QW * (b + 1)]

    TRB = [PTR[:], PX[:]]
    trb_state = {"bank": 0, "batch": 0}

    def transpose_batch(jobs, wait):
        last = None
        for j0 in range(0, len(jobs), 8):
            sub = jobs[j0:j0 + 8]
            b = trb_state["bank"]
            trb_state["bank"] ^= 1
            bank = TRB[b]
            eng = "vector" if (trb_state["batch"] % 2 or TREVAC[0] == "vector") else "scalar"
            trb_state["batch"] += 1
            ch = "dve" if eng == "vector" else "act"
            for i, (dst, src) in enumerate(sub):
                g.em("tensor", _op("transpose", bank[:, 128 * i:128 * (i + 1)],
                                   src, c_ident[:]),
                     wait=(list(wait) + gw(f"trb{b}")) if i == 0 else (),
                     inc=(("pe", 1),) if i == len(sub) - 1 else ())
            pe_v = g.now("pe")
            for i, (dst, src) in enumerate(sub):
                fn = (_op("tensor_copy", out=dst, in_=bank[:, 128 * i:128 * (i + 1)])
                      if eng == "vector" else
                      _op("copy", out=dst, in_=bank[:, 128 * i:128 * (i + 1)]))
                g.em(eng, fn, wait=[("pe", pe_v)] if i == 0 else (),
                     inc=((ch, 1),) if i == len(sub) - 1 else ())
            guard[f"trb{b}"] = (ch, g.now(ch))
            last = (ch, g.now(ch))
        return last

    def rsqrt_col(dst, src):
        c13i = st[:, 13:14].bitcast(I32)
        c13f = st[:, 13:14]
        c12f = st[:, 12:13]
        g.em("vector", _op("tensor_scalar", out=c13i, in0=src.bitcast(I32),
                           scalar1=1, scalar2=None, op0=ALU.arith_shift_right))
        g.em("vector", _op("tensor_scalar", out=c13i, in0=c13i,
                           scalar1=-1, scalar2=None, op0=ALU.bitwise_xor))
        g.em("vector", _op("tensor_scalar", out=c13i, in0=c13i,
                           scalar1=0x5f3759df + 1, scalar2=None, op0=ALU.add))
        for it in range(3):
            g.em("vector", _op("tensor_mul", c12f, c13f, c13f))
            g.em("vector", _op("tensor_mul", c12f, c12f, src))
            g.em("vector", _op("tensor_scalar", out=c12f, in0=c12f,
                               scalar1=-0.5, scalar2=1.5, op0=ALU.mult, op1=ALU.add))
            g.em("vector", _op("tensor_mul", dst if it == 2 else c13f, c13f, c12f))

    def ln_stats(tile_ap, eps):
        g.em("vector", _op("tensor_mul", SQ, tile_ap, tile_ap), inc=(("dve", 1),))
        g.em("vector", _op("tensor_reduce", out=st[:, 1:2], in_=SQ, axis=AX.X,
                           op=ALU.add), inc=(("dvx", 1),))
        vA = g.now("dvx")
        g.em("vector", _op("tensor_reduce", out=st[:, 0:1], in_=tile_ap, axis=AX.X,
                           op=ALU.add), inc=(("dvx", 1),))
        g.em("vector", _op("tensor_scalar", out=st[:, 3:4], in0=st[:, 1:2],
                           scalar1=RD, scalar2=eps, op0=ALU.mult, op1=ALU.add),
             wait=[("dvx", vA)], inc=(("dvx", 1),))
        g.em("vector", _op("tensor_scalar", out=st[:, 2:3], in0=st[:, 0:1],
                           scalar1=-RD, scalar2=None, op0=ALU.mult),
             wait=[("dvx", vA + 1)], inc=(("dvx", 1),))
        g.em("vector", _op("tensor_mul", st[:, 4:5], st[:, 2:3], st[:, 2:3]),
             wait=[("dvx", g.now("dvx"))], inc=(("dvx", 1),))
        g.em("vector", _op("tensor_sub", st[:, 5:6], st[:, 3:4], st[:, 4:5]),
             wait=[("dvx", g.now("dvx"))], inc=(("dve", 1),))
        g.em("scalar", _op("activation", out=st[:, 6:7], in_=st[:, 5:6], func=AF.Sqrt,
                           bias=0.0, scale=1.0),
             wait=[("dve", g.now("dve"))])
        g.em("scalar", _op("copy", out=st[:, 14:15], in_=st[:, 6:7]),
             inc=(("act", 1),))
        g.em("vector", _op("reciprocal", out=st[:, 7:8], in_=st[:, 6:7]),
             wait=[("act", g.now("act"))], inc=(("dvx", 1),))
        g.em("vector", _op("tensor_mul", st[:, 8:9], st[:, 2:3], st[:, 7:8]),
             wait=[("dvx", g.now("dvx"))], inc=(("dve", 1),))

    SLAB_ORDER = [0, 1, 2, 3, 8, 9, 10, 11, 12, 13, 14, 15, 4, 5, 6, 7]
    pbx_slots = [PBq[0], PBq[1], PBq[2], PBq[3]]
    pbx_i = [0]

    def next_group_slot():
        i = pbx_i[0] % 4
        pbx_i[0] += 1
        return pbx_slots[i], f"pbx{i}"

    BG_CH = ("ldb0", "ldb1", "stb0", "stb1", "pox")

    def gen_quarter(q):
        gen_quarter_inner(q)
        g.barrier(exclude=BG_CH)
        if taps and q == TAPQ[0]:
            for nm, buf in (("d_ln1T", m_ln1T), ("d_act", m_act), ("d_z", m_z),
                            ("d_xs", m_xs), ("d_y", m_y), ("d_cs", m_cs),
                            ("d_prev", m_prev), ("d_attn", m_attn), ("d_u", m_u),
                            ("d_yT", m_yT)):
                g.em("gpsimd", _op("dma_start", out=tapd[nm], in_=buf[:]),
                     inc=(("stt", 16),))
            g.barrier()
        gset("mix_pe_prev", "pe")

    def gen_quarter_inner(q):
        t0 = QT * q
        PHASE[0] = f"q{q}.rolls"
        first = (q == 0)

        if first:
            g.em("vector", _op("memset", m_rawcarry[:], 0.0), inc=(("dve", 1),))
            g.em("vector", _op("memset", m_xscarry[:], 0.0))
            g.em("vector", _op("memset", m_prev[:], 0.0))
            g.em("vector", _op("memset", m_xs[0:64, 0:2048], 0.0))
            for dc in range(16):
                g.em("vector", _op("memset", ln1T_dc(dc)[:, 0:MARG], 0.0))
            for b in range(32):
                g.em("vector", _op("memset", act_blk(b)[:, 0:MARG], 0.0))
            g.em("vector", _op("memset", m_xstemp[:, 0:MARG], 0.0), inc=(("dve", 1),))
        else:
            w = gw("mix_pe_prev")
            for dc in range(16):
                g.em("vector", _op("tensor_copy", out=ln1T_dc(dc)[:, 0:MARG],
                                   in_=ln1T_dc(dc)[:, QT:QT + MARG]), wait=w)
            for b in range(32):
                g.em("vector", _op("tensor_copy", out=act_blk(b)[:, 0:63],
                                   in_=act_blk(b)[:, QT:QT + 63]), wait=w)
            g.em("vector", _op("memset", st[:, 15:16], 0.0), inc=(("dve", 1),))
        rolls_done = ("dve", g.now("dve"))

        if SUBSTOP[0] < 1:
            return
        for k in range(4):
            PHASE[0] = f"q{q}.Q2k{k}"
            sl = k % 2
            ut = m_u[:, 2048 * sl:2048 * (sl + 1)]
            g.em("sync", _op("dma_start", out=ut,
                             in_=u[t0 + 128 * k: t0 + 128 * (k + 1), :]),
                 wait=gw(f"mu{sl}"), inc=((f"ldu{sl}", 16),))
            ldv = g.now(f"ldu{sl}")
            g.em("vector", None, wait=[(f"ldu{sl}", ldv)])
            if taps and q == 0 and k == DBGK[0] and DBGWHAT[0] == "u":
                g.em("vector", _op("tensor_copy", out=m_dbgu, in_=ut),
                     inc=(("dve", 1),))
                g.em("gpsimd", _op("dma_start", out=tapd["d_dbgu"], in_=m_dbgu),
                     wait=[("dve", g.now("dve"))], inc=(("stt", 16),))
                gset("attn0", "stt")
            if taps and q == 0 and k == DBGK[0] and DBGWHAT[0] == "ln1":
                pass
            if SUBQ2[0] >= 1:
                ln_stats(ut, LNEPS)
            if SUBQ2[0] >= 2:
                g.em("scalar", _op("activation", out=m_ln1[:], in_=ut, func=AF.Identity,
                                   scale=st[:, 7:8], bias=st[:, 8:9]),
                     wait=[("dve", g.now("dve"))] + gw("m_ln1"))
                g.em("scalar", _op("copy", out=st[:, 14:15], in_=st[:, 7:8]),
                     inc=(("act", 1),))
                gset(f"mu{sl}", "act")
                if taps and q == 0:
                    g.em("vector", _op("tensor_copy",
                                       out=m_cs[:, 16 * k:16 * (k + 1)],
                                       in_=st[:]),
                         wait=[("act", g.now("act"))])
            else:
                g.em("vector", _op("tensor_copy", out=m_ln1[:], in_=ut),
                     wait=gw("m_ln1"), inc=(("dve", 1),))
                gset(f"mu{sl}", "dve")
            if DRAINFIX[0]:
                g.em("vector", _op("memset", st[:, 13:14], 0.0), inc=(("dve", 1),))
            if taps and q == 0 and k == DBGK[0] and DBGWHAT[0] == "ln1":
                g.em("vector", _op("tensor_copy", out=m_dbgu[:, 0:1024],
                                   in_=m_ln1[:, 0:1024]), inc=(("dve", 1),))
                g.em("vector", _op("tensor_copy", out=m_dbgu[:, 1024:2048],
                                   in_=m_ln1[:, 1024:2048]), inc=(("dve", 1),))
                g.em("gpsimd", _op("dma_start", out=tapd["d_dbgu"], in_=m_dbgu),
                     wait=[("dve", g.now("dve"))], inc=(("stt", 16),))
                gset("attn0", "stt")
            apply_done = (("act", g.now("act")) if SUBQ2[0] >= 2
                          else ("dve", g.now("dve")))
            if taps and (SUBSTOP[0] == 1 or EARLYTAP[0] in (2, 3)) and q == 0:
                g.em("gpsimd", _op("dma_start",
                                   out=tapd["d_lnt"][128 * k:128 * (k + 1), :],
                                   in_=m_ln1[:]),
                     wait=[apply_done], inc=(("stt", 16),))
                g.em("gpsimd", _op("dma_start",
                                   out=tapd["d_ut"][128 * k:128 * (k + 1), :],
                                   in_=ut),
                     wait=[apply_done], inc=(("stt", 16),))
                gset(f"mu{sl}", "stt")
            if SUBQ2[0] >= 3:
                for j0 in (0, 8):
                    transpose_batch(
                        [(ln1T_dc(j)[:, MARG + 128 * k: MARG + 128 * (k + 1)],
                          m_ln1[:, 128 * j:128 * (j + 1)]) for j in range(j0, j0 + 8)],
                        [apply_done, rolls_done])
                gset("m_ln1", "pe")
        q2_done = {"pe": g.now("pe"), "dve": g.now("dve"), "act": g.now("act")}
        if taps and q == 0 and EARLYTAP[0]:
            g.em("gpsimd", _op("dma_start", out=tapd["d_ln1T2"], in_=m_ln1T[:]),
                 wait=[("pe", q2_done["pe"]), ("dve", q2_done["dve"]),
                       ("act", q2_done["act"])], inc=(("stt", 16),))

        if SUBSTOP[0] < 2:
            return
        ztemp = ([m_ln1[:, 0:512], m_ln1[:, 512:1024]] if ZTMP[0] == 0 else
                 [m_tmpA[:, 0:512], m_tmpA[:, 512:1024]])
        for si, s in enumerate(SLAB_ORDER):
            PHASE[0] = f"q{q}.Q4s{si}"
            wsl = si % 2
            wv = m_wbuf[:, 8192 * wsl:8192 * (wsl + 1)].rearrange(
                "p (dc j) -> p dc j", dc=16)
            for a in range(4):
                lane = f"ldw{wsl}" if a % 2 == 0 else f"ldp{wsl}"
                g.em("sync", _op("dma_start", out=wv[:, 4 * a:4 * a + 4, :],
                                 in_=cwin[:, 8192 * s + 2048 * a: 8192 * s + 2048 * (a + 1)]
                                 .rearrange("p (dc j) -> p dc j", dc=4)),
                     wait=gw(f"mwbuf{wsl}") if a == 0 else (),
                     inc=((lane, 16),))
            slab_ld = [(f"ldw{wsl}", g.now(f"ldw{wsl}")),
                       (f"ldp{wsl}", g.now(f"ldp{wsl}"))]
            for c4 in range(4):
                if SUBQ4[0] < 2:
                    continue
                oc = 4 * s + c4
                if SUBQ4[0] < 4 and oc >= 16:
                    continue
                slot, sname = next_group_slot()
                rhs_c0 = 0 if oc < 16 else MARG
                waits = slab_ld + [
                         ("dve", q2_done["dve"]), ("act", q2_done["act"])] + gw(sname)
                for dc in range(16):
                    g.em("tensor", _op("matmul", slot[:],
                                       wv[:, dc, 128 * c4:128 * (c4 + 1)],
                                       ln1T_dc(dc)[:, rhs_c0:rhs_c0 + 512],
                                       start=(dc == 0), stop=(dc == 15)),
                         wait=waits if dc == 0 else (),
                         inc=(("pe", 1),) if dc == 15 else ())
                gset(f"mwbuf{wsl}", "pe")
                mm_done = ("pe", g.now("pe"))
                if oc < 16:
                    zt = ztemp[oc % 2]
                    if SUBQ4[0] == 20:
                        continue
                    g.em("vector", _op("tensor_copy", out=zt, in_=slot[:]),
                         wait=[mm_done] + gw(f"ztemp{oc % 2}"), inc=(("dve", 1),))
                    gset(sname, "dve")
                    ev = ("dve", g.now("dve"))
                    if SUBQ4[0] >= 3:
                        transpose_batch(
                            [(m_z[:, 2048 * j + 128 * oc: 2048 * j + 128 * (oc + 1)],
                              zt[:, 128 * j:128 * (j + 1)]) for j in range(4)], [ev])
                        gset(f"ztemp{oc % 2}", "pe")
                else:
                    r = oc - 16
                    par = r % 2
                    raw = RAW[par]
                    xs_chunk = 16 <= oc < 32
                    c = oc - 16 if xs_chunk else None
                    dst = (m_xstemp[:, 63:575] if xs_chunk
                           else act_blk((oc - 32) if oc < 48 else (16 + oc - 48))[:, 63:575])
                    g.em("vector", _op("tensor_copy", out=raw[:, 2:514], in_=slot[:]),
                         wait=[mm_done] + gw(f"raw{par}"), inc=(("dvx", 1),))
                    ev_dvx = g.now("dvx")
                    gset(sname, "dvx")
                    g.em("vector", _op("tensor_copy", out=raw[:, 0:2],
                                       in_=m_rawcarry[:, 2 * r:2 * r + 2]),
                         wait=gw(f"rawc{r}"), inc=(("dvx", 1),))
                    rc_dvx = g.now("dvx")
                    g.em("vector", _op("tensor_copy", out=m_rawcarry[:, 2 * r:2 * r + 2],
                                       in_=raw[:, 512:514]),
                         wait=[("dvx", ev_dvx)], inc=(("dve", 1),))
                    gset(f"rawc{r}", "dve")
                    g.em("vector", _op("tensor_mul", ACC[1], raw[:, 0:512],
                                       c_convw[:, r:r + 1].broadcast_to([128, 512])),
                         wait=[("dvx", rc_dvx)] + gw("acc"))
                    g.em("vector", _op("scalar_tensor_tensor", out=ACC[0],
                                       in0=raw[:, 1:513],
                                       scalar=c_convw[:, 48 + r:49 + r],
                                       in1=ACC[1], op0=ALU.mult, op1=ALU.add))
                    g.em("vector", _op("scalar_tensor_tensor", out=ACC[0],
                                       in0=raw[:, 2:514],
                                       scalar=c_convw[:, 96 + r:97 + r],
                                       in1=ACC[0], op0=ALU.mult, op1=ALU.add),
                         inc=(("dve", 1),))
                    gset(f"raw{par}", "dve")
                    conv_done = ("dve", g.now("dve"))
                    if xs_chunk:
                        if first:
                            g.em("vector", _op("memset", m_xstemp[:, 0:63], 0.0),
                                 wait=gw("xstemp_r"), inc=(("dve", 1),))
                        else:
                            g.em("vector", _op("tensor_copy", out=m_xstemp[:, 0:63],
                                               in_=m_xscarry[:, 64 * c:64 * c + 63]),
                                 wait=gw("xstemp_r"), inc=(("dve", 1),))
                    carry_in = ("dve", g.now("dve"))
                    sw = [conv_done] + ([carry_in] if xs_chunk else []) + \
                        (gw("xstemp_r") if xs_chunk else [])
                    g.em("scalar", _op("activation", out=dst, in_=ACC[0], func=AF.Silu,
                                       bias=c_convw[:, 144 + r:145 + r], scale=1.0),
                         wait=sw)
                    g.em("scalar", _op("copy", out=st[:, 14:15], in_=dst[:, 0:1]),
                         inc=(("act", 1),))
                    gset("acc", "act")
                    silu_done = ("act", g.now("act"))
                    if xs_chunk:
                        if first:
                            g.em("vector", _op("memset", m_xstemp[:, 63:64], 0.0),
                                 wait=[silu_done], inc=(("dve", 1),))
                        g.em("vector", _op("tensor_copy",
                                           out=m_xscarry[:, 64 * c:64 * c + 63],
                                           in_=m_xstemp[:, 512:575]),
                             wait=[silu_done], inc=(("dve", 1),))
                        g.em("vector", _op("tensor_mul", m_tmpA[:, 0:512],
                                           m_xstemp[:, 0:512], act_blk(c)[:, 0:512]),
                             wait=[silu_done] + gw("m_tmpA"), inc=(("dve", 1),))
                        g.em("vector", _op("tensor_reduce", out=m_cs[:, 8 * c:8 * (c + 1)],
                                           in_=m_tmpA[:, 0:512].rearrange(
                                               "p (cc t) -> p cc t", cc=8),
                                           axis=AX.X, op=ALU.add),
                             inc=(("dvx", 1),))
                        if first:
                            g.em("vector", _op("memset", m_cs[:, 8 * c:8 * c + 1], 0.0))
                        transpose_batch(
                            [(m_xs[:, 2048 * j + 128 * c: 2048 * j + 128 * (c + 1)],
                              m_xstemp[:, 128 * j:128 * (j + 1)]) for j in range(4)],
                            [silu_done, carry_in])
                        gset("xstemp_r", "pe")
            bg_emit(4)
        q4_pe = g.now("pe")
        cs_dvx = g.now("dvx")

        for ci in range(1, 8):
            for b in range(16):
                w = ()
                if b == 0:
                    w = ([("dvx", cs_dvx)] + gw("prev_r") if ci == 1
                         else [("dvx", g.now("dvx"))])
                g.em("vector", _op("scalar_tensor_tensor",
                                   out=m_prev[:, 9 * b + ci:9 * b + ci + 1],
                                   in0=m_prev[:, 9 * b + ci - 1:9 * b + ci],
                                   scalar=c_decay[:, b:b + 1],
                                   in1=m_cs[:, 8 * b + ci - 1:8 * b + ci],
                                   op0=ALU.mult, op1=ALU.add),
                     wait=w, inc=(("dvx", 1),) if b == 15 else ())
        g.em("vector", _op("tensor_copy", out=m_prevbf[:], in_=m_prev[:]),
             wait=[("dvx", g.now("dvx"))], inc=(("dve", 1),))
        prevbf_done = ("dve", g.now("dve"))
        for b in range(16):
            g.em("vector", _op("scalar_tensor_tensor",
                               out=m_prev[:, 9 * b:9 * b + 1],
                               in0=m_prev[:, 9 * b + 7:9 * b + 8],
                               scalar=c_decay[:, b:b + 1],
                               in1=m_cs[:, 8 * b + 7:8 * b + 8],
                               op0=ALU.mult, op1=ALU.add),
                 inc=(("dve", 1),) if b == 15 else ())

        if SUBSTOP[0] >= 3:
            gen_ssd_and_q6(q, t0, prevbf_done, q4_pe)

    def gen_ssd_pair(q, p, prevbf_done, n_vchunks=2):
        if SUBG[0] == 0:
            return
        PHASE[0] = f"q{q}.ssd{p}"
        horder = list(range(0, 32, 2)) + list(range(1, 32, 2))
        mask_end = {}
        ydiag_end = {}
        for b4 in range(min(9, SUBB4[0])):
            if b4 < 8:
                for i in range(4):
                    h = horder[4 * b4 + i]
                    hb = 64 * (h % 2)
                    blk = h // 2
                    lhsT = m_act[hb:hb + 64, QW * blk + 128 * p: QW * blk + 128 * (p + 1)]
                    rhs = m_act[hb:hb + 64,
                                QW * (16 + blk) + 128 * p: QW * (16 + blk) + 128 * (p + 1)]
                    w = []
                    if i == 0:
                        w = gw("pg_bank")
                        if b4 == 0:
                            w = w + [prevbf_done, ("act", g.now("act")),
                                     ("pool", g.now("pool"))]
                    g.em("tensor", _op("matmul", PG[:, 128 * i:128 * (i + 1)],
                                       lhsT, rhs, start=True, stop=True,
                                       tile_position=(hb, 0)),
                         wait=w, inc=(("pe", 1),) if i == 3 else ())
                g_end = g.now("pe")
                if SUBG[0] >= 2:
                    for i in range(4):
                        w = [("pe", g_end)] if i == 0 else []
                        if b4 >= 2 and i == 0 and (b4 - 2) in ydiag_end:
                            w = w + [("pe", ydiag_end[b4 - 2])]
                        g.em("vector", _op("tensor_mul",
                                           m_gt[:, 512 * (b4 % 2) + 128 * i:
                                                512 * (b4 % 2) + 128 * (i + 1)],
                                           PG[:, 128 * i:128 * (i + 1)], c_gmask[:]),
                             wait=w, inc=(("dve", 1),) if i == 3 else ())
                    mask_end[b4] = g.now("dve")
                    guard["pg_bank"] = ("dve", mask_end[b4])
                else:
                    g.em("vector", _op("memset", st[:, 14:15], 0.0),
                         wait=[("pe", g_end)], inc=(("dve", 1),))
                    mask_end[b4] = g.now("dve")
                    guard["pg_bank"] = ("dve", mask_end[b4])
            if b4 >= 1 and SUBQ5[0] >= 2:
                bb = b4 - 1
                for i in range(4):
                    hh = horder[4 * bb + i]
                    hb = 64 * (hh % 2)
                    blk = hh // 2
                    w = []
                    if i == 0:
                        w = [("dve", mask_end[bb])]
                        if bb == 0:
                            w = w + gw("pb_yacc", "pbx0", "pbx1", "pbx2", "pbx3")
                    g.em("tensor", _op("matmul", PB[:, 64 * hh:64 * (hh + 1)],
                                       m_gt[:, 512 * (bb % 2) + 128 * i:
                                            512 * (bb % 2) + 128 * (i + 1)],
                                       m_xs[:, 2048 * p + 64 * hh: 2048 * p + 64 * (hh + 1)],
                                       start=True, stop=True),
                         wait=w, inc=())
                    for cl in range(n_vchunks):
                        ci = 2 * p + cl
                        g.em("tensor", _op("matmul",
                                           PV[64 * (ci % 2):64 * (ci % 2) + 64, hh:hh + 1],
                                           m_act[hb:hb + 64,
                                                 QW * (16 + blk) + 64 * ci: QW * (16 + blk) + 64 * (ci + 1)],
                                           m_prevbf[hb:hb + 64, 9 * blk + ci:9 * blk + ci + 1],
                                           start=True, stop=True,
                                           tile_position=(hb, 64 * (ci % 2))),
                             wait=gw("pv_r") if (hh == 0 and cl == 0) else (),
                             inc=(("pe", 1),) if i == 3 and cl == n_vchunks - 1 else ())
                ydiag_end[bb] = g.now("pe")
        pe_all = g.now("pe")
        if SUBQ5[0] < 2:
            guard[f"yready{p}"] = ("pe", pe_all)
            return
        g.em("vector", _op("tensor_copy", out=m_vcol[:], in_=PV[:, 0:32]),
             wait=[("pe", pe_all)], inc=(("dvx", 1),))
        gset("pv_r", "dvx")
        vcol_dvx = g.now("dvx")
        ysl = p % 2
        for b in range(4):
            g.em("vector", _op("tensor_add",
                               m_y[:, 2048 * ysl + 512 * b: 2048 * ysl + 512 * (b + 1)]
                               .rearrange("p (hh t) -> p hh t", hh=8),
                               PB[:, 512 * b:512 * (b + 1)]
                               .rearrange("p (hh t) -> p hh t", hh=8),
                               m_vcol[:, 8 * b:8 * (b + 1)].unsqueeze(2)
                               .broadcast_to([128, 8, 64])),
                 wait=(gw(f"my{ysl}") + [("dvx", vcol_dvx)]) if b == 0 else (),
                 inc=(("dve", 1),) if b == 3 else ())
        gset("pb_yacc", "dve")
        guard[f"yready{p}"] = ("dve", g.now("dve"))

    def gen_q6_tilepair(q, t0, tiles, store_rows=None):
        if SUBQ5[0] < 3:
            return
        tmpB = m_ln1
        for ti, t in enumerate(tiles):
            PHASE[0] = f"q{q}.q6a_t{t}"
            ysl = t % 2
            ytile = m_y[:, 2048 * ysl:2048 * (ysl + 1)]
            ztile = m_z[:, 2048 * t:2048 * (t + 1)]
            g.em("scalar", _op("activation", out=m_tmpA[:], in_=ztile, func=AF.Silu,
                               bias=0.0, scale=1.0),
                 wait=[guard[f"yready{t}"]] + gw("m_tmpA"))
            g.em("scalar", _op("copy", out=st[:, 14:15], in_=m_tmpA[:, 0:1]),
                 inc=(("act", 1),))
            sz = ("act", g.now("act"))
            g.em("vector", _op("tensor_mul", tmpB[:], ytile, m_tmpA[:]),
                 wait=[sz] + gw("m_ln1"), inc=(("dve", 1),))
            gset(f"my{ysl}", "dve")
            g.em("vector", _op("tensor_mul", SQ, tmpB[:], tmpB[:]))
            g.em("vector", _op("tensor_reduce", out=st[:, 8:9], in_=SQ, axis=AX.X,
                               op=ALU.add), inc=(("dvx", 1),))
            g.em("vector", _op("tensor_scalar", out=st[:, 9:10], in0=st[:, 8:9],
                               scalar1=RD, scalar2=RMSEPS, op0=ALU.mult, op1=ALU.add),
                 wait=[("dvx", g.now("dvx"))], inc=(("dve", 1),))
            g.em("scalar", _op("activation", out=st[:, 10:11], in_=st[:, 9:10],
                               func=AF.Sqrt, bias=0.0, scale=1.0),
                 wait=[("dve", g.now("dve"))])
            g.em("scalar", _op("copy", out=st[:, 14:15], in_=st[:, 10:11]),
                 inc=(("act", 1),))
            g.em("vector", _op("reciprocal", out=st[:, 11:12], in_=st[:, 10:11]),
                 wait=[("act", g.now("act"))], inc=(("dve", 1),))
            g.em("scalar", _op("activation", out=m_tmpA[:], in_=tmpB[:], func=AF.Identity,
                               scale=st[:, 11:12], bias=0.0),
                 wait=[("dve", g.now("dve"))])
            g.em("scalar", _op("copy", out=st[:, 14:15], in_=st[:, 11:12]),
                 inc=(("act", 1),))
            g.em("vector", _op("tensor_mul", tmpB[:], m_tmpA[:], c_rmsbc[:]),
                 wait=[("act", g.now("act"))], inc=(("dve", 1),))
            gset("m_tmpA", "dve")
            yh = ("dve", g.now("dve"))
            for j0 in (0, 8):
                transpose_batch(
                    [(m_yT[:, 2048 * ti + 128 * j: 2048 * ti + 128 * (j + 1)],
                      tmpB[:, 128 * j:128 * (j + 1)]) for j in range(j0, j0 + 8)],
                    [yh])
            gset("m_ln1", "pe")
        a_ev = [("dve", g.now("dve")), ("act", g.now("act"))]
        if SUBQ5[0] < 4 and SUBQ5[0] not in (40, 41):
            return

        for j in range(4):
            PHASE[0] = f"q{q}.q6b_j{j}"
            wsl = j % 2
            for aa in range(4):
                dst_ap = (m_wbuf[:, 8192 * wsl + 2048 * aa:
                                 8192 * wsl + 2048 * (aa + 1)]
                          if WOUTDST[0] == 0 else
                          x1[0:128, 0:2048].bitcast(BF16)[:, 0:2048])
                g.em("sync", _op("dma_start",
                                 out=dst_ap,
                                 in_=cwout[:, 8192 * j + 2048 * aa:
                                            8192 * j + 2048 * (aa + 1)]),
                     wait=gw(f"mwbuf{wsl}") if aa == 0 else (),
                     inc=((f"ldw{wsl}", 16),))
            wld = g.now(f"ldw{wsl}")
            for ti, t in enumerate(tiles):
                if SUBQ5[0] == 40:
                    continue
                slot = PBq[(2 * j + ti) % 4]
                for dc in range(16):
                    g.em("tensor", _op("matmul", slot,
                                       m_yT[:, 2048 * ti + 128 * dc: 2048 * ti + 128 * (dc + 1)],
                                       m_wbuf[:, 8192 * wsl + 512 * dc: 8192 * wsl + 512 * (dc + 1)],
                                       start=(dc == 0), stop=(dc == 15)),
                         wait=([(f"ldw{wsl}", wld)] + a_ev
                               + gw(f"pbx{(2 * j + ti) % 4}", "pb_yacc")
                               if dc == 0 else ()),
                         inc=(("pe", 1),) if dc == 15 else ())
                gset(f"mwbuf{wsl}", "pe")
                mmv = ("pe", g.now("pe"))
                if SUBQ5[0] == 41:
                    g.em("vector", _op("tensor_copy", out=st[:, 12:13],
                                       in_=slot[:, 0:1]),
                         wait=[mmv], inc=(("dve", 1),))
                    guard[f"pbx{(2 * j + ti) % 4}"] = ("dve", g.now("dve"))
                    continue
                g.em("scalar", _op("copy",
                                   out=m_attn[:, 2048 * ti + 512 * j: 2048 * ti + 512 * (j + 1)],
                                   in_=slot),
                     wait=[mmv] + (gw(f"attn{ti}") if j == 0 else []), inc=(("act", 1),))
                guard[f"pbx{(2 * j + ti) % 4}"] = ("act", g.now("act"))
        attn_act = g.now("act")
        if SUBQ5[0] < 5 or SUBQ5[0] in (40, 41):
            return

        for ti, t in enumerate(tiles):
            rows = (t0 - MARG + 128 * t, t0 - MARG + 128 * (t + 1))
            sr = store_rows[ti] if store_rows else (max(rows[0], 0), min(rows[1], L))
            atile = m_attn[:, 2048 * ti:2048 * (ti + 1)]
            usl = t % 2
            ut = m_u[:, 2048 * usl:2048 * (usl + 1)]
            lo_off = sr[0] - rows[0]
            hi_off = sr[1] - rows[0]
            g.em("sync", _op("dma_start", out=ut[lo_off:hi_off, :],
                             in_=u[sr[0]:sr[1], :]),
                 wait=gw(f"mu{usl}"), inc=((f"ldu{usl}", 16),))
            uld = g.now(f"ldu{usl}")
            g.em("vector", None, wait=[("act", attn_act)])
            ln_stats(atile, LNEPS)
            g.em("scalar", _op("activation", out=SQ, in_=atile, func=AF.Identity,
                               scale=st[:, 7:8], bias=st[:, 8:9]),
                 wait=[("dve", g.now("dve"))])
            g.em("scalar", _op("copy", out=st[:, 14:15], in_=st[:, 7:8]),
                 inc=(("act", 1),))
            gset(f"attn{ti}", "act")
            g.em("vector", _op("tensor_add", ut, SQ, ut),
                 wait=[(f"ldu{usl}", uld), ("act", g.now("act"))], inc=(("dve", 1),))
            addv = ("dve", g.now("dve"))
            g.em("sync", _op("dma_start", out=x1[sr[0]:sr[1], :],
                               in_=ut[lo_off:hi_off, :]),
                 wait=[addv], inc=((f"stu{usl}", 16),))
            gset(f"mu{usl}", f"stu{usl}")

    def gen_ssd_and_q6(q, t0, prevbf_done, q4_pe):
        for tp in (0, 2):
            for p in (tp, tp + 1):
                if p < SUBNP[0]:
                    gen_ssd_pair(q, p, prevbf_done)
            gen_q6_tilepair(q, t0, [tp, tp + 1])
        gset("prev_r", "pe")

    def gen_epilogue():
        t0 = L
        PHASE[0] = "epi"
        w = gw("mix_pe_prev")
        for dc in range(16):
            g.em("vector", _op("tensor_copy", out=ln1T_dc(dc)[:, 0:MARG],
                               in_=ln1T_dc(dc)[:, QT:QT + MARG]), wait=w)
        for b in range(32):
            g.em("vector", _op("tensor_copy", out=act_blk(b)[:, 0:63],
                               in_=act_blk(b)[:, QT:QT + 63]), wait=w)
        g.em("vector", _op("tensor_copy", out=m_prevbf[:], in_=m_prev[:]),
             inc=(("dve", 1),))
        prevbf_done = ("dve", g.now("dve"))
        rolls = ("dve", g.now("dve"))

        raw3 = ACC[1][:, 0:3]
        prev_pool = None
        for r in range(48):
            g.em("vector", _op("tensor_copy", out=raw3[:, 0:2],
                               in_=m_rawcarry[:, 2 * r:2 * r + 2]),
                 wait=(gw("acc") if r == 0 else []) +
                      ([prev_pool] if prev_pool else []))
            g.em("vector", _op("memset", raw3[:, 2:3], 0.0), inc=(("dve", 1),))
            rdv = ("dve", g.now("dve"))
            g.em("gpsimd", _op("tensor_mul", ACC[0][:, 0:1], raw3[:, 1:2],
                               c_convw[:, 48 + r:49 + r]),
                 wait=[rdv, ("act", g.now("act"))])
            g.em("gpsimd", _op("tensor_mul", ACC[0][:, 2:3], raw3[:, 0:1],
                               c_convw[:, r:r + 1]), inc=(("pox", 1),))
            g.em("gpsimd", _op("tensor_add", ACC[0][:, 1:2], ACC[0][:, 0:1],
                               ACC[0][:, 2:3]),
                 wait=[("pox", g.now("pox"))], inc=(("pool", 1),))
            cdv = ("pool", g.now("pool"))
            prev_pool = cdv
            xs_chunk = r < 16
            if xs_chunk:
                c = r
                g.em("vector", _op("tensor_copy", out=m_xstemp[:, 0:63],
                                   in_=m_xscarry[:, 64 * c:64 * c + 63]),
                     wait=([rolls] if c == 0 else []) + gw("xstemp_r"),
                     inc=(("dve", 1),))
                cin = ("dve", g.now("dve"))
                g.em("scalar", _op("activation", out=m_xstemp[:, 63:64],
                                   in_=ACC[0][:, 1:2], func=AF.Silu,
                                   bias=c_convw[:, 144 + r:145 + r], scale=1.0),
                     wait=[cdv, cin])
                g.em("scalar", _op("copy", out=st[:, 14:15], in_=m_xstemp[:, 63:64]),
                     inc=(("act", 1),))
                sd = ("act", g.now("act"))
                transpose_batch([(m_xs[:, 128 * c:128 * (c + 1)],
                                  m_xstemp[:, 0:128])], [sd])
                gset("xstemp_r", "pe")
            else:
                b = (r - 16) if r < 32 else (16 + r - 32)
                g.em("scalar", _op("activation", out=act_blk(b)[:, 63:64],
                                   in_=ACC[0][:, 1:2], func=AF.Silu,
                                   bias=c_convw[:, 144 + r:145 + r], scale=1.0),
                     wait=[cdv, rolls])
                g.em("scalar", _op("copy", out=st[:, 14:15], in_=act_blk(b)[:, 63:64]),
                     inc=(("act", 1),))
        conv_done = {"act": g.now("act"), "pe": g.now("pe"), "dve": g.now("dve")}

        for si, s in enumerate(range(4)):
            wsl = si % 2
            wv = m_wbuf[:, 8192 * wsl:8192 * (wsl + 1)].rearrange(
                "p (dc j) -> p dc j", dc=16)
            for a in range(4):
                g.em("sync", _op("dma_start", out=wv[:, 4 * a:4 * a + 4, :],
                                 in_=cwin[:, 8192 * s + 2048 * a: 8192 * s + 2048 * (a + 1)]
                                 .rearrange("p (dc j) -> p dc j", dc=4)),
                     wait=gw(f"mwbuf{wsl}") if a == 0 else (), inc=((f"ldw{wsl}", 16),))
            slab_ld = g.now(f"ldw{wsl}")
            for c4 in range(4):
                oc = 4 * s + c4
                slot, sname = next_group_slot()
                for dc in range(16):
                    g.em("tensor", _op("matmul", slot[:, 0:128],
                                       wv[:, dc, 128 * c4:128 * (c4 + 1)],
                                       ln1T_dc(dc)[:, 0:128],
                                       start=(dc == 0), stop=(dc == 15)),
                         wait=([(f"ldw{wsl}", slab_ld), ("dve", conv_done["dve"]),
                                ("act", conv_done["act"])] + gw(sname))
                         if dc == 0 else (),
                         inc=(("pe", 1),) if dc == 15 else ())
                gset(f"mwbuf{wsl}", "pe")
                mm_done = ("pe", g.now("pe"))
                zt = ztemp_ep[oc % 2]
                g.em("vector", _op("tensor_copy", out=zt[:, 0:128], in_=slot[:, 0:128]),
                     wait=[mm_done] + gw(f"ztemp{oc % 2}"), inc=(("dve", 1),))
                gset(sname, "dve")
                ev = ("dve", g.now("dve"))
                transpose_batch([(m_z[:, 128 * oc:128 * (oc + 1)], zt[:, 0:128])],
                                [ev])
                gset(f"ztemp{oc % 2}", "pe")

        gen_ssd_pair(4, 0, prevbf_done, n_vchunks=1)
        gen_q6_tilepair(4, t0, [0])
        g.barrier(exclude=BG_CH)
        if taps and TAPQ[0] == 4:
            for nm, buf in (("d_y", m_y), ("d_z", m_z), ("d_xs", m_xs),
                            ("d_act", m_act), ("d_prev", m_prev),
                            ("d_ln1T", m_ln1T)):
                g.em("gpsimd", _op("dma_start", out=tapd[nm], in_=buf[:]),
                     inc=(("stt", 16),))
            g.barrier()

    ztemp_ep = None

    nq = {"prologue": 0, "q0": 1, "q1": 2, "q2": 3, "q3": 4}.get(stop_after, 4)
    for q in range(nq):
        gen_quarter(q)
    ztemp_ep = [m_ln1[:, 0:512], m_ln1[:, 512:1024]]
    if stop_after in ("full", "mixer", "ffn1"):
        gen_epilogue()

    stk_mix.close()
    g.barrier()

    stk_ffn = ExitStack()
    f_big1 = sb(stk_ffn, "f_big1", [128, 32768], BF16)
    f_big2 = sb(stk_ffn, "f_big2", [128, 32768], BF16)
    f_mid = sb(stk_ffn, "f_mid", [128, 16384], BF16)
    f_gtmp = sb(stk_ffn, "f_gtmp", [128, 1024], BF16)
    f_utmp = sb(stk_ffn, "f_utmp", [128, 1024], BF16)
    assert nc.sbuf_bytes_remaining >= 0

    ln2T = f_mid[:, 0:8192]
    x1f1 = [f_mid[:, 8192 + 4096 * i:8192 + 4096 * (i + 1)].bitcast(F32)
            for i in range(2)]
    down_stage = f_mid[:, 0:16384].bitcast(F32)
    x1f4 = [f_big1[:, 4096 * i:4096 * (i + 1)].bitcast(F32) for i in range(2)]
    ln2stage = f_big1[:, 0:2048]

    def gen_ffn_pass(pi):
        p0 = QT * pi
        for k in range(4):
            PHASE[0] = f"f{pi}.F1k{k}"
            sl = k % 2
            xt = x1f1[sl]
            g.em("sync", _op("dma_start", out=xt,
                             in_=x1[p0 + 128 * k:p0 + 128 * (k + 1), :]),
                 wait=gw(f"fx1{sl}"), inc=((f"ldu{sl}", 16),))
            ldv = g.now(f"ldu{sl}")
            g.em("vector", None, wait=[(f"ldu{sl}", ldv)])
            ln_stats(xt, LNEPS)
            g.em("scalar", _op("activation", out=ln2stage, in_=xt, func=AF.Identity,
                               scale=st[:, 7:8], bias=st[:, 8:9]),
                 wait=[("dve", g.now("dve"))] + gw("ln2stage"))
            g.em("scalar", _op("copy", out=st[:, 14:15], in_=st[:, 7:8]),
                 inc=(("act", 1),))
            gset(f"fx1{sl}", "act")
            ap = ("act", g.now("act"))
            for j0 in (0, 8):
                transpose_batch(
                    [(ln2T[:, 512 * j + 128 * k: 512 * j + 128 * (k + 1)],
                      ln2stage[:, 128 * j:128 * (j + 1)]) for j in range(j0, j0 + 8)],
                    [ap])
            gset("ln2stage", "pe")
        f1_ev = [("dve", g.now("dve")), ("act", g.now("act"))]

        upslots = [PV[:], PG[:]]
        for s in range(16):
            PHASE[0] = f"f{pi}.F2s{s}"
            wsl = s % 2
            wgb = f_big2[:, 8192 * wsl:8192 * (wsl + 1)]
            wub = f_big2[:, 16384 + 8192 * wsl:16384 + 8192 * (wsl + 1)]
            bgw = bg_ready.get("wgwu", []) if (pi == 0 and s < 2) else []
            g.em("sync", _op("dma_start", out=wgb,
                             in_=cwg[:, 8192 * s:8192 * (s + 1)]),
                 wait=gw(f"fwg{wsl}") + bgw, inc=((f"ldw{wsl}", 16),))
            g.em("sync", _op("dma_start", out=wub,
                             in_=cwu[:, 8192 * s:8192 * (s + 1)]),
                 wait=gw(f"fwu{wsl}") + bgw, inc=((f"ldv{wsl}", 16),))
            slab_ld = [(f"ldw{wsl}", g.now(f"ldw{wsl}")), (f"ldv{wsl}", g.now(f"ldv{wsl}"))]
            for c4 in range(4):
                oc = 4 * s + c4
                gslot = PBq[oc % 4]
                uslot = upslots[oc % 2]
                for hc in range(16):
                    g.em("tensor", _op("matmul", gslot,
                                       wgb.rearrange("p (hc j) -> p hc j", hc=16)[:, hc, 128 * c4:128 * (c4 + 1)],
                                       ln2T[:, 512 * hc:512 * (hc + 1)],
                                       start=(hc == 0), stop=(hc == 15)),
                         wait=(slab_ld + f1_ev + gw(f"fpb{oc % 4}"))
                         if hc == 0 else (),
                         inc=(("pe", 1),) if hc == 15 else ())
                gmm = ("pe", g.now("pe"))
                for hc in range(16):
                    g.em("tensor", _op("matmul", uslot,
                                       wub.rearrange("p (hc j) -> p hc j", hc=16)[:, hc, 128 * c4:128 * (c4 + 1)],
                                       ln2T[:, 512 * hc:512 * (hc + 1)],
                                       start=(hc == 0), stop=(hc == 15)),
                         wait=gw(f"fup{oc % 2}") if hc == 0 else (),
                         inc=(("pe", 1),) if hc == 15 else ())
                gset(f"fwg{wsl}", "pe")
                gset(f"fwu{wsl}", "pe")
                umm = ("pe", g.now("pe"))
                gt = f_gtmp[:, 512 * (oc % 2):512 * (oc % 2 + 1)]
                ut = f_utmp[:, 512 * (oc % 2):512 * (oc % 2 + 1)]
                g.em("scalar", _op("activation", out=gt, in_=gslot, func=AF.Silu,
                                   bias=0.0, scale=c_sgate[:, oc:oc + 1]),
                     wait=[gmm] + gw(f"fgt{oc % 2}"))
                g.em("scalar", _op("copy", out=st[:, 14:15], in_=gt[:, 0:1]),
                     inc=(("act", 1),))
                guard[f"fpb{oc % 4}"] = ("act", g.now("act"))
                sg = ("act", g.now("act"))
                g.em("vector", _op("tensor_scalar", out=ut, in0=uslot,
                                   scalar1=c_sup[:, oc:oc + 1], scalar2=None,
                                   op0=ALU.mult),
                     wait=[umm] + gw(f"fut{oc % 2}"), inc=(("dve", 1),))
                guard[f"fup{oc % 2}"] = ("dve", g.now("dve"))
                g.em("vector", _op("tensor_mul",
                                   f_big1[:, 512 * oc:512 * (oc + 1)], gt, ut),
                     wait=[sg], inc=(("dve", 1),))
                guard[f"fgt{oc % 2}"] = ("dve", g.now("dve"))
                guard[f"fut{oc % 2}"] = ("dve", g.now("dve"))
        f2_dve = g.now("dve")

        for s in range(8):
            PHASE[0] = f"f{pi}.F3s{s}"
            wsl = s % 2
            wdb = f_big2[:, 16384 * wsl:16384 * (wsl + 1)]
            g.em("sync", _op("dma_start", out=wdb,
                             in_=cwd[:, 16384 * s:16384 * (s + 1)]),
                 wait=gw(f"fwd{wsl}")
                 + (gw("fwg0", "fwg1") if (s % 2 == 0) else gw("fwu0", "fwu1"))
                 + ([("dve", f2_dve)] if s < 2 else [])
                 + (bg_ready.get("wd", []) if (pi == 0 and s < 2) else []),
                 inc=((f"ldw{wsl}", 16),))
            slab_ld = g.now(f"ldw{wsl}")
            wdv = wdb.rearrange("p (oc j) -> p oc j", oc=64)
            for k in range(4):
                slot = PBq[(4 * s + k) % 4]
                for oc in range(64):
                    g.em("tensor", _op("matmul", slot[:, 0:256],
                                       f_big1[:, 512 * oc + 128 * k: 512 * oc + 128 * (k + 1)],
                                       wdv[:, oc, :],
                                       start=(oc == 0), stop=(oc == 63)),
                         wait=([(f"ldw{wsl}", slab_ld), ("dve", f2_dve)]
                               + gw(f"fpb{(4 * s + k) % 4}", "dstage")) if oc == 0 else (),
                         inc=(("pe", 1),) if oc == 63 else ())
                gset(f"fwd{wsl}", "pe")
                mmv = ("pe", g.now("pe"))
                g.em("vector", _op("tensor_mul",
                                   down_stage[:, 2048 * k + 256 * s: 2048 * k + 256 * (s + 1)],
                                   slot[:, 0:256], c_sdbc[:, 256 * s:256 * (s + 1)]),
                     wait=[mmv], inc=(("dve", 1),))
                guard[f"fpb{(4 * s + k) % 4}"] = ("dve", g.now("dve"))
        f3_dve = g.now("dve")

        for k in range(4):
            sl = k % 2
            xt = x1f4[sl]
            g.em("sync", _op("dma_start", out=xt,
                             in_=x1[p0 + 128 * k:p0 + 128 * (k + 1), :]),
                 wait=gw(f"fx4{sl}") + ([("dve", f3_dve)] if k < 2 else []),
                 inc=((f"ldu{sl}", 16),))
            ldv = g.now(f"ldu{sl}")
            dtile = down_stage[:, 2048 * k:2048 * (k + 1)]
            g.em("vector", None, wait=[("dve", f3_dve)])
            ln_stats(dtile, LNEPS)
            g.em("scalar", _op("activation", out=SQ, in_=dtile, func=AF.Identity,
                               scale=st[:, 7:8], bias=st[:, 8:9]),
                 wait=[("dve", g.now("dve"))])
            g.em("scalar", _op("copy", out=st[:, 14:15], in_=st[:, 7:8]),
                 inc=(("act", 1),))
            gset("dstage", "act")
            g.em("vector", _op("tensor_add", xt, SQ, xt),
                 wait=[(f"ldu{sl}", ldv), ("act", g.now("act"))], inc=(("dve", 1),))
            addv = ("dve", g.now("dve"))
            g.em("gpsimd", _op("dma_start", out=out[p0 + 128 * k:p0 + 128 * (k + 1), :],
                               in_=xt),
                 wait=[addv], inc=((f"stu{sl}", 16),))
            gset(f"fx4{sl}", f"stu{sl}")
        g.barrier()

    bg_emit(len(bg_units))
    nf = {"full": 4, "ffn1": 1}.get(stop_after, 0)
    for pi in range(nf):
        gen_ffn_pass(pi)

    stk_ffn.close()

    final_waits = [(c, g.cnt[c]) for c in list(g.cnt) if c.startswith(("st", "ld"))]
    g.em("gpsimd", None, wait=final_waits)

    global _LAST_GEN
    _LAST_GEN = g

    with stk_root:
        sems = {}
        sem_stack = ExitStack()
        with sem_stack:
            for chn in list(g.cnt.keys()):
                sems[chn] = sem_stack.enter_context(nc.semaphore(f"s_{chn}"))
            with nc.Block() as block:
                def mk(name):
                    def body(eng):
                        for fn, waits, incs in g.ops[name]:
                            for chn, v in waits:
                                eng.wait_ge(sems[chn], v)
                            ins = fn(eng) if fn is not None else None
                            for chn, amt in incs:
                                ins.then_inc(sems[chn], amt)
                    return body
                block.sync(mk("sync"))
                block.tensor(mk("tensor"))
                block.vector(mk("vector"))
                block.scalar(mk("scalar"))
                block.gpsimd(mk("gpsimd"))
    return nc


def _host_inputs(inputs):
    bf = ml_dtypes.bfloat16
    W_in = np.asarray(inputs["L0_attn_W_in"], dtype=np.float32)
    W_out = np.asarray(inputs["L0_attn_W_out"], dtype=np.float32)
    conv_w = np.asarray(inputs["L0_attn_conv_w"], dtype=np.float32)
    conv_b = np.asarray(inputs["L0_attn_conv_b"], dtype=np.float32)
    A_log = np.asarray(inputs["L0_attn_A_log"], dtype=np.float32)
    rms_w = np.asarray(inputs["L0_attn_rms_w"], dtype=np.float32)
    s_gate = np.asarray(inputs["L0_ffn_s_gate"], dtype=np.float32)
    s_up = np.asarray(inputs["L0_ffn_s_up"], dtype=np.float32)
    s_down = np.asarray(inputs["L0_ffn_s_down"], dtype=np.float32)

    shared = {
        "w_inT": np.ascontiguousarray(W_in.T),
        "w_outT": np.ascontiguousarray(W_out.T),
        "wg8": np.ascontiguousarray(np.asarray(inputs["L0_ffn_w_gate"], dtype=np.int8)),
        "wu8": np.ascontiguousarray(np.asarray(inputs["L0_ffn_w_up"], dtype=np.int8)),
        "wd8": np.ascontiguousarray(np.asarray(inputs["L0_ffn_w_down"], dtype=np.int8)),
        "in_ident": np.eye(128, dtype=bf),
        "in_gmask": np.asarray(
            (np.arange(128)[:, None] // 64 == np.arange(128)[None, :] // 64)
            & (np.arange(128)[None, :] >= np.arange(128)[:, None]), dtype=bf),
        "in_convw": np.ascontiguousarray(np.concatenate(
            [conv_w[:, 0, k].reshape(48, 128).T for k in range(3)]
            + [conv_b.reshape(48, 128).T], axis=1)),
        "in_sgate": np.ascontiguousarray(s_gate.reshape(64, 128).T),
        "in_sup": np.ascontiguousarray(s_up.reshape(64, 128).T),
        "in_sdbc": np.ascontiguousarray(np.tile(s_down[None, :], (128, 1))),
        "in_rmsbc": np.ascontiguousarray(np.tile(rms_w[None, :], (128, 1))),
        "in_arep": np.ascontiguousarray(np.repeat(A_log, 64).reshape(16, 128).T),
    }
    u_all = np.asarray(inputs["L0_attn_u"], dtype=np.float32)
    in_maps = []
    for b in range(8):
        m = {"u": np.ascontiguousarray(u_all[b])}
        m.update(shared)
        in_maps.append(m)
    return in_maps


def kernel(**inputs):
    in_maps = _host_inputs(inputs)
    nc = build()
    res = run_bass_kernel_spmd(nc, in_maps, core_ids=list(range(8)))
    return np.stack([np.asarray(res.results[b]["out"]) for b in range(8)], axis=0)


# version 11
# speedup vs baseline: 1.3704x; 1.0492x over previous
import numpy as np
import ml_dtypes
import concourse.bass as bass
import concourse.mybir as mybir
from concourse.bass_utils import run_bass_kernel_spmd
from contextlib import ExitStack

F32 = mybir.dt.float32
BF16 = mybir.dt.bfloat16
I8 = mybir.dt.int8
I32 = mybir.dt.int32
AF = mybir.ActivationFunctionType
ALU = mybir.AluOpType
AX = mybir.AxisListType

D = 2048
L = 2048
NH = 32
DH = 64
CH = 64
QT = 512
MARG = 64
QW = QT + MARG
LNEPS = 1e-5
RMSEPS = 1.1920929e-07
RD = 1.0 / D

ENGINES = ("sync", "tensor", "vector", "scalar", "gpsimd")
OWN = {"tensor": "pe", "vector": "dve", "scalar": "act", "gpsimd": "pool"}


class Gen:

    def __init__(self):
        self.ops = {e: [] for e in ENGINES}
        import collections
        self.cnt = collections.defaultdict(int)
        for c in ("pe", "dve", "act", "pool"):
            self.cnt[c] = 0
        self.pend = {e: {} for e in ENGINES}
        self.ptags = []

    def now(self, ch):
        return self.cnt[ch]

    def snap(self):
        return dict(self.cnt)

    def barrier(self, snap=None):
        s = snap or self.snap()
        for e in ENGINES:
            for ch, v in s.items():
                if OWN.get(e) == ch:
                    continue
                if v > self.pend[e].get(ch, 0):
                    self.pend[e][ch] = v

    def em(self, eng, fn, wait=(), inc=()):
        if eng == "tensor":
            self.ptags.append(PHASE[0])
        wd = dict(self.pend[eng])
        self.pend[eng] = {}
        for ch, v in wait:
            if v is None:
                continue
            if OWN.get(eng) == ch:
                continue
            if v > wd.get(ch, 0):
                wd[ch] = v
        post = {}
        for ch, amt in inc:
            self.cnt[ch] += amt
            post[ch] = self.cnt[ch]
        if fn is None and inc:
            raise RuntimeError("inc requires a real instruction")
        self.ops[eng].append((fn, tuple(wd.items()), tuple(inc)))
        return post


_LAST_GEN = None


def _op(method, *args, **kwargs):
    return lambda eng: getattr(eng, method)(*args, **kwargs)


SUBSTOP = [3]
SUBQ2 = [3]
SUBQ5 = [5]
SUBG = [2]
SUBNP = [4]
SUBB4 = [9]
EARLYTAP = [1]
CONVENG = ["gpsimd"]
TREVAC = ["mixed"]
DRAINFIX = [0]
DBGK = [0]
DBGWHAT = ["u"]
SUBQ4 = [4]
ZTMP = [0]
WAITPAD = [0]
PHASE = ["?"]
TAPQ = [0]
WOUTDST = [0]


def build(dump_x1=False, stop_after="full", taps=False):
    nc = bass.Bass("TRN2", target_bir_lowering=False)

    u = nc.dram_tensor("u", [L, D], F32, kind="ExternalInput").ap()
    w_inT = nc.dram_tensor("w_inT", [D, 8224], F32, kind="ExternalInput").ap()
    w_outT = nc.dram_tensor("w_outT", [D, D], F32, kind="ExternalInput").ap()
    wg8 = nc.dram_tensor("wg8", [D, 8192], I8, kind="ExternalInput").ap()
    wu8 = nc.dram_tensor("wu8", [D, 8192], I8, kind="ExternalInput").ap()
    wd8 = nc.dram_tensor("wd8", [8192, D], I8, kind="ExternalInput").ap()
    in_ident = nc.dram_tensor("in_ident", [128, 128], BF16, kind="ExternalInput").ap()
    in_gmask = nc.dram_tensor("in_gmask", [128, 128], BF16, kind="ExternalInput").ap()
    in_convw = nc.dram_tensor("in_convw", [128, 192], F32, kind="ExternalInput").ap()
    in_sgate = nc.dram_tensor("in_sgate", [128, 64], F32, kind="ExternalInput").ap()
    in_sup = nc.dram_tensor("in_sup", [128, 64], F32, kind="ExternalInput").ap()
    in_sdbc = nc.dram_tensor("in_sdbc", [128, D], F32, kind="ExternalInput").ap()
    in_rmsbc = nc.dram_tensor("in_rmsbc", [128, D], F32, kind="ExternalInput").ap()
    in_arep = nc.dram_tensor("in_arep", [128, 16], F32, kind="ExternalInput").ap()

    out = nc.dram_tensor("out", [L, D], F32, kind="ExternalOutput").ap()
    tap_specs = [("d_ln1T", 9216, BF16), ("d_act", 18432, BF16), ("d_z", 8192, BF16),
                 ("d_xs", 8192, BF16), ("d_y", 4096, BF16), ("d_cs", 128, F32),
                 ("d_prev", 144, F32), ("d_attn", 4096, F32), ("d_u", 4096, F32),
                 ("d_yT", 4096, BF16), ("d_ln1T2", 9216, BF16), ("d_dbgu", 2048, F32)]
    tapd = {}
    if taps:
        for nm, w, dtp in tap_specs:
            tapd[nm] = nc.dram_tensor(nm, [128, w], dtp, kind="ExternalOutput").ap()
        tapd["d_lnt"] = nc.dram_tensor("d_lnt", [512, D], BF16, kind="ExternalOutput").ap()
        tapd["d_ut"] = nc.dram_tensor("d_ut", [512, D], F32, kind="ExternalOutput").ap()
    x1 = nc.dram_tensor("x1", [L, D], F32,
                        kind="ExternalOutput" if dump_x1 else "Internal").ap()
    cwin = nc.dram_tensor("cwin", [128, 16 * 16 * 512], BF16, kind="Internal").ap()
    cwout = nc.dram_tensor("cwout", [128, 4 * 16 * 512], BF16, kind="Internal").ap()
    cwg = nc.dram_tensor("cwg", [128, 16 * 16 * 512], BF16, kind="Internal").ap()
    cwu = nc.dram_tensor("cwu", [128, 16 * 16 * 512], BF16, kind="Internal").ap()
    cwd = nc.dram_tensor("cwd", [128, 8 * 64 * 256], BF16, kind="Internal").ap()

    cwin_v = cwin.rearrange("p (s dc j) -> p s dc j", s=16, dc=16)
    cwout_v = cwout.rearrange("p (q dc j) -> p q dc j", q=4, dc=16)
    cwg_v = cwg.rearrange("p (s hc j) -> p s hc j", s=16, hc=16)
    cwu_v = cwu.rearrange("p (s hc j) -> p s hc j", s=16, hc=16)
    cwd_v = cwd.rearrange("p (s oc j) -> p s oc j", s=8, oc=64)

    g = Gen()
    stk_root = ExitStack()

    def sb(stack, name, shape, dtype):
        return stack.enter_context(nc.sbuf_tensor(name, shape, dtype))

    c_ident = sb(stk_root, "c_ident", [128, 128], BF16)
    c_gmask = sb(stk_root, "c_gmask", [128, 128], BF16)
    c_convw = sb(stk_root, "c_convw", [128, 192], F32)
    c_sgate = sb(stk_root, "c_sgate", [128, 64], F32)
    c_sup = sb(stk_root, "c_sup", [128, 64], F32)
    c_sdbc = sb(stk_root, "c_sdbc", [128, D], F32)
    c_rmsbc = sb(stk_root, "c_rmsbc", [128, D], F32)
    c_arep = sb(stk_root, "c_arep", [128, 16], F32)
    c_decay = sb(stk_root, "c_decay", [128, 16], F32)
    scr = sb(stk_root, "scr", [128, 2080], F32)
    st = sb(stk_root, "st", [128, 16], F32)

    SQ = scr[:, 0:2048]
    RAW = [scr[:, 0:516], scr[:, 516:1032]]
    ACC = [scr[:, 1032:1544], scr[:, 1544:2056]]

    PB = stk_root.enter_context(nc.psum_tensor("PB", [128, 2048], F32))
    PX = stk_root.enter_context(nc.psum_tensor("PX", [128, 1024], BF16))
    PG = stk_root.enter_context(nc.psum_tensor("PG", [128, 512], F32))
    PTR = stk_root.enter_context(nc.psum_tensor("PTR", [128, 1024], BF16))
    PV = stk_root.enter_context(nc.psum_tensor("PV", [128, 512], F32))
    PBq = [PB[:, 512 * i:512 * (i + 1)] for i in range(4)]

    guard = {}

    def gw(*names):
        out = []
        for n in names:
            if n in guard:
                v = guard[n]
                out.extend(v if isinstance(v, list) else [v])
        return out

    def gset(name, ch):
        guard[name] = (ch, g.now(ch))

    PHASE[0] = "prologue"
    g.em("scalar", _op("copy", out=st[:, 14:15], in_=st[:, 14:15]))
    for dst, src in ((c_ident, in_ident), (c_gmask, in_gmask), (c_convw, in_convw),
                     (c_sgate, in_sgate), (c_sup, in_sup), (c_sdbc, in_sdbc),
                     (c_rmsbc, in_rmsbc), (c_arep, in_arep)):
        g.em("sync", _op("dma_start", out=dst[:], in_=src), inc=(("ldc", 16),))
    consts_ld = g.now("ldc")
    g.em("scalar", _op("activation", out=c_decay[:], in_=c_arep[:], func=AF.Exp,
                       bias=0.0, scale=63.0),
         wait=[("ldc", consts_ld)])
    g.em("scalar", _op("copy", out=st[:, 14:15], in_=c_decay[:, 0:1]),
         inc=(("act", 1),))

    stk_pro = ExitStack()
    pw_f32 = [sb(stk_pro, f"pw_f32_{i}", [128, 8192], F32) for i in range(2)]
    pw_bf = [sb(stk_pro, f"pw_bf_{i}", [128, 8192], BF16) for i in range(2)]
    pw_i8 = [sb(stk_pro, f"pw_i8_{i}", [128, 8192], I8) for i in range(2)]

    cast_rr = ["vector", "scalar"]
    cast_ch = {"vector": "dve", "scalar": "act", "gpsimd": "pool"}
    cast_i = [0]

    def cast_op(dst_ap, src_ap, wait):
        e = cast_rr[cast_i[0] % len(cast_rr)]
        cast_i[0] += 1
        ch = cast_ch[e]
        if e == "scalar":
            fn = _op("copy", out=dst_ap, in_=src_ap)
        else:
            fn = _op("tensor_copy", out=dst_ap, in_=src_ap)
        g.em(e, fn, wait=wait, inc=((ch, 1),))
        return (ch, g.now(ch))

    def prologue_stream(n_blocks, load_src_fn, in_bufs, width, store_dst_fn, nsplit):
        for b in range(n_blocks):
            sl = b % 2
            ib = in_bufs[sl][:, 0:width]
            ob = pw_bf[sl][:, 0:width]
            half = width // 2
            src_ap = load_src_fn(b)
            g.em("sync", _op("dma_start", out=ib[:, 0:half], in_=src_ap[:, 0:half]),
                 wait=gw(f"pin{sl}"), inc=((f"ldp{sl}", 16),))
            g.em("sync", _op("dma_start", out=ib[:, half:width],
                             in_=src_ap[:, half:width]),
                 inc=((f"ldu{sl}", 16),))
            ld_w = [(f"ldp{sl}", g.now(f"ldp{sl}")), (f"ldu{sl}", g.now(f"ldu{sl}"))]
            cw = cast_op(ob, ib, ld_w + gw(f"pbf{sl}"))
            guard[f"pin{sl}"] = cw
            srcs, dsts = store_dst_fn(b, ob)
            if len(srcs) == 1:
                sap, dap = srcs[0], dsts[0]
                hs = sap.shape[1] // 2
                srcs = [sap[:, 0:hs, :], sap[:, hs:, :]]
                dsts = [dap[:, 0:hs, :], dap[:, hs:, :]]
            st_lanes = [f"stp{sl}", f"stu{sl}"]
            for i, (sap, dap) in enumerate(zip(srcs, dsts)):
                g.em("gpsimd", _op("dma_start", out=dap, in_=sap),
                     wait=[cw], inc=((st_lanes[i % 2], 16),))
            guard[f"pbf{sl}"] = [(f"stp{sl}", g.now(f"stp{sl}")),
                                 (f"stu{sl}", g.now(f"stu{sl}"))]

    def win_store(b, ob):
        srcs, dsts = [], []
        obv = ob.rearrange("p (s j) -> p s j", s=16)
        for a in range(4):
            srcs.append(obv[:, 4 * a:4 * a + 4, :])
            dsts.append(cwin_v[:, 4 * a:4 * a + 4, b, :])
        return srcs, dsts

    prologue_stream(16, lambda b: w_inT[128 * b:128 * (b + 1), 0:8192],
                    pw_f32, 8192, win_store, 4)

    def wout_store(b, ob):
        obv = ob.rearrange("p (q j) -> p q j", q=4)
        return [obv], [cwout_v[:, :, b, :]]

    prologue_stream(16, lambda b: w_outT[128 * b:128 * (b + 1), :],
                    pw_f32, 2048, wout_store, 1)

    def mk_wguw_store(cv):
        def f(b, ob):
            srcs, dsts = [], []
            obv = ob.rearrange("p (s j) -> p s j", s=16)
            for a in range(4):
                srcs.append(obv[:, 4 * a:4 * a + 4, :])
                dsts.append(cv[:, 4 * a:4 * a + 4, b, :])
            return srcs, dsts
        return f

    prologue_stream(16, lambda b: wg8[128 * b:128 * (b + 1), :],
                    pw_i8, 8192, mk_wguw_store(cwg_v), 4)
    prologue_stream(16, lambda b: wu8[128 * b:128 * (b + 1), :],
                    pw_i8, 8192, mk_wguw_store(cwu_v), 4)

    def wd_store(b, ob):
        obv = ob.rearrange("p (s j) -> p s j", s=8)
        return [obv], [cwd_v[:, :, b, :]]

    prologue_stream(64, lambda b: wd8[128 * b:128 * (b + 1), :],
                    pw_i8, 2048, wd_store, 1)

    stk_pro.close()
    g.barrier()

    stk_mix = ExitStack()
    m_ln1T = sb(stk_mix, "m_ln1T", [128, 16 * QW], BF16)
    m_act = sb(stk_mix, "m_act", [128, 32 * QW], BF16)
    m_xs = sb(stk_mix, "m_xs", [128, 4 * 2048], BF16)
    m_z = sb(stk_mix, "m_z", [128, 4 * 2048], BF16)
    m_y = sb(stk_mix, "m_y", [128, 2 * 2048], BF16)
    m_xstemp = sb(stk_mix, "m_xstemp", [128, QW], BF16)
    m_xscarry = sb(stk_mix, "m_xscarry", [128, 16 * 64], BF16)
    m_rawcarry = sb(stk_mix, "m_rawcarry", [128, 96], F32)
    m_cs = sb(stk_mix, "m_cs", [128, 128], F32)
    m_prev = sb(stk_mix, "m_prev", [128, 144], F32)
    m_prevbf = sb(stk_mix, "m_prevbf", [128, 144], BF16)
    m_gt = sb(stk_mix, "m_gt", [128, 1024], BF16)
    m_vcol = sb(stk_mix, "m_vcol", [128, 32], F32)
    m_wbuf = sb(stk_mix, "m_wbuf", [128, 16384], BF16)
    m_u = sb(stk_mix, "m_u", [128, 2 * 2048], F32)
    m_ln1 = sb(stk_mix, "m_ln1", [128, 2048], BF16)
    m_yT = sb(stk_mix, "m_yT", [128, 2 * 2048], BF16)
    m_tmpA = sb(stk_mix, "m_tmpA", [128, 2048], BF16)
    m_attn = sb(stk_mix, "m_attn", [128, 2 * 2048], F32)
    m_dbgu = m_attn[:, 0:2048]

    sbuf_left = nc.sbuf_bytes_remaining
    assert sbuf_left >= 0, f"SBUF over budget: {sbuf_left}"

    def ln1T_dc(dc):
        return m_ln1T[:, QW * dc: QW * (dc + 1)]

    def act_blk(b):
        return m_act[:, QW * b: QW * (b + 1)]

    TRB = [PTR[:], PX[:]]
    trb_state = {"bank": 0, "batch": 0}

    def transpose_batch(jobs, wait):
        last = None
        for j0 in range(0, len(jobs), 8):
            sub = jobs[j0:j0 + 8]
            b = trb_state["bank"]
            trb_state["bank"] ^= 1
            bank = TRB[b]
            eng = "vector" if (trb_state["batch"] % 2 or TREVAC[0] == "vector") else "scalar"
            trb_state["batch"] += 1
            ch = "dve" if eng == "vector" else "act"
            for i, (dst, src) in enumerate(sub):
                g.em("tensor", _op("transpose", bank[:, 128 * i:128 * (i + 1)],
                                   src, c_ident[:]),
                     wait=(list(wait) + gw(f"trb{b}")) if i == 0 else (),
                     inc=(("pe", 1),) if i == len(sub) - 1 else ())
            pe_v = g.now("pe")
            for i, (dst, src) in enumerate(sub):
                fn = (_op("tensor_copy", out=dst, in_=bank[:, 128 * i:128 * (i + 1)])
                      if eng == "vector" else
                      _op("copy", out=dst, in_=bank[:, 128 * i:128 * (i + 1)]))
                g.em(eng, fn, wait=[("pe", pe_v)] if i == 0 else (),
                     inc=((ch, 1),) if i == len(sub) - 1 else ())
            guard[f"trb{b}"] = (ch, g.now(ch))
            last = (ch, g.now(ch))
        return last

    def rsqrt_col(dst, src):
        c13i = st[:, 13:14].bitcast(I32)
        c13f = st[:, 13:14]
        c12f = st[:, 12:13]
        g.em("vector", _op("tensor_scalar", out=c13i, in0=src.bitcast(I32),
                           scalar1=1, scalar2=None, op0=ALU.arith_shift_right))
        g.em("vector", _op("tensor_scalar", out=c13i, in0=c13i,
                           scalar1=-1, scalar2=None, op0=ALU.bitwise_xor))
        g.em("vector", _op("tensor_scalar", out=c13i, in0=c13i,
                           scalar1=0x5f3759df + 1, scalar2=None, op0=ALU.add))
        for it in range(3):
            g.em("vector", _op("tensor_mul", c12f, c13f, c13f))
            g.em("vector", _op("tensor_mul", c12f, c12f, src))
            g.em("vector", _op("tensor_scalar", out=c12f, in0=c12f,
                               scalar1=-0.5, scalar2=1.5, op0=ALU.mult, op1=ALU.add))
            g.em("vector", _op("tensor_mul", dst if it == 2 else c13f, c13f, c12f))

    def ln_stats(tile_ap, eps):
        g.em("vector", _op("tensor_mul", SQ, tile_ap, tile_ap), inc=(("dve", 1),))
        g.em("vector", _op("tensor_reduce", out=st[:, 1:2], in_=SQ, axis=AX.X,
                           op=ALU.add), inc=(("dvx", 1),))
        vA = g.now("dvx")
        g.em("vector", _op("tensor_reduce", out=st[:, 0:1], in_=tile_ap, axis=AX.X,
                           op=ALU.add), inc=(("dvx", 1),))
        g.em("vector", _op("tensor_scalar", out=st[:, 3:4], in0=st[:, 1:2],
                           scalar1=RD, scalar2=eps, op0=ALU.mult, op1=ALU.add),
             wait=[("dvx", vA)], inc=(("dvx", 1),))
        g.em("vector", _op("tensor_scalar", out=st[:, 2:3], in0=st[:, 0:1],
                           scalar1=-RD, scalar2=None, op0=ALU.mult),
             wait=[("dvx", vA + 1)], inc=(("dvx", 1),))
        g.em("vector", _op("tensor_mul", st[:, 4:5], st[:, 2:3], st[:, 2:3]),
             wait=[("dvx", g.now("dvx"))], inc=(("dvx", 1),))
        g.em("vector", _op("tensor_sub", st[:, 5:6], st[:, 3:4], st[:, 4:5]),
             wait=[("dvx", g.now("dvx"))], inc=(("dve", 1),))
        g.em("scalar", _op("activation", out=st[:, 6:7], in_=st[:, 5:6], func=AF.Sqrt,
                           bias=0.0, scale=1.0),
             wait=[("dve", g.now("dve"))])
        g.em("scalar", _op("copy", out=st[:, 14:15], in_=st[:, 6:7]),
             inc=(("act", 1),))
        g.em("vector", _op("reciprocal", out=st[:, 7:8], in_=st[:, 6:7]),
             wait=[("act", g.now("act"))], inc=(("dvx", 1),))
        g.em("vector", _op("tensor_mul", st[:, 8:9], st[:, 2:3], st[:, 7:8]),
             wait=[("dvx", g.now("dvx"))], inc=(("dve", 1),))

    SLAB_ORDER = [0, 1, 2, 3, 8, 9, 10, 11, 12, 13, 14, 15, 4, 5, 6, 7]
    pbx_slots = [PBq[0], PBq[1], PBq[2], PBq[3]]
    pbx_i = [0]

    def next_group_slot():
        i = pbx_i[0] % 4
        pbx_i[0] += 1
        return pbx_slots[i], f"pbx{i}"

    def gen_quarter(q):
        gen_quarter_inner(q)
        g.barrier()
        if taps and q == TAPQ[0]:
            for nm, buf in (("d_ln1T", m_ln1T), ("d_act", m_act), ("d_z", m_z),
                            ("d_xs", m_xs), ("d_y", m_y), ("d_cs", m_cs),
                            ("d_prev", m_prev), ("d_attn", m_attn), ("d_u", m_u),
                            ("d_yT", m_yT)):
                g.em("gpsimd", _op("dma_start", out=tapd[nm], in_=buf[:]),
                     inc=(("stt", 16),))
            g.barrier()
        gset("mix_pe_prev", "pe")

    def gen_quarter_inner(q):
        t0 = QT * q
        PHASE[0] = f"q{q}.rolls"
        first = (q == 0)

        if first:
            g.em("vector", _op("memset", m_rawcarry[:], 0.0), inc=(("dve", 1),))
            g.em("vector", _op("memset", m_xscarry[:], 0.0))
            g.em("vector", _op("memset", m_prev[:], 0.0))
            g.em("vector", _op("memset", m_xs[0:64, 0:2048], 0.0))
            for dc in range(16):
                g.em("vector", _op("memset", ln1T_dc(dc)[:, 0:MARG], 0.0))
            for b in range(32):
                g.em("vector", _op("memset", act_blk(b)[:, 0:MARG], 0.0))
            g.em("vector", _op("memset", m_xstemp[:, 0:MARG], 0.0), inc=(("dve", 1),))
        else:
            w = gw("mix_pe_prev")
            for dc in range(16):
                g.em("vector", _op("tensor_copy", out=ln1T_dc(dc)[:, 0:MARG],
                                   in_=ln1T_dc(dc)[:, QT:QT + MARG]), wait=w)
            for b in range(32):
                g.em("vector", _op("tensor_copy", out=act_blk(b)[:, 0:63],
                                   in_=act_blk(b)[:, QT:QT + 63]), wait=w)
            g.em("vector", _op("memset", st[:, 15:16], 0.0), inc=(("dve", 1),))
        rolls_done = ("dve", g.now("dve"))

        if SUBSTOP[0] < 1:
            return
        for k in range(4):
            PHASE[0] = f"q{q}.Q2k{k}"
            sl = k % 2
            ut = m_u[:, 2048 * sl:2048 * (sl + 1)]
            g.em("sync", _op("dma_start", out=ut[:, 0:1024],
                             in_=u[t0 + 128 * k: t0 + 128 * (k + 1), 0:1024]),
                 wait=gw(f"mu{sl}"), inc=((f"ldu{sl}", 16),))
            g.em("sync", _op("dma_start", out=ut[:, 1024:2048],
                             in_=u[t0 + 128 * k: t0 + 128 * (k + 1), 1024:2048]),
                 inc=((f"ldb{sl}", 16),))
            g.em("vector", None, wait=[(f"ldu{sl}", g.now(f"ldu{sl}")),
                                       (f"ldb{sl}", g.now(f"ldb{sl}"))])
            if taps and q == 0 and k == DBGK[0] and DBGWHAT[0] == "u":
                g.em("vector", _op("tensor_copy", out=m_dbgu, in_=ut),
                     inc=(("dve", 1),))
                g.em("gpsimd", _op("dma_start", out=tapd["d_dbgu"], in_=m_dbgu),
                     wait=[("dve", g.now("dve"))], inc=(("stt", 16),))
                gset("attn0", "stt")
            if taps and q == 0 and k == DBGK[0] and DBGWHAT[0] == "ln1":
                pass
            if SUBQ2[0] >= 1:
                ln_stats(ut, LNEPS)
            if SUBQ2[0] >= 2:
                g.em("scalar", _op("activation", out=m_ln1[:], in_=ut, func=AF.Identity,
                                   scale=st[:, 7:8], bias=st[:, 8:9]),
                     wait=[("dve", g.now("dve"))] + gw("m_ln1"))
                g.em("scalar", _op("copy", out=st[:, 14:15], in_=st[:, 7:8]),
                     inc=(("act", 1),))
                gset(f"mu{sl}", "act")
                if taps and q == 0:
                    g.em("vector", _op("tensor_copy",
                                       out=m_attn[:, 2048 + 16 * k:2048 + 16 * (k + 1)],
                                       in_=st[:]),
                         wait=[("act", g.now("act"))])
            else:
                g.em("vector", _op("tensor_copy", out=m_ln1[:], in_=ut),
                     wait=gw("m_ln1"), inc=(("dve", 1),))
                gset(f"mu{sl}", "dve")
            if DRAINFIX[0]:
                g.em("vector", _op("memset", st[:, 13:14], 0.0), inc=(("dve", 1),))
            if taps and q == 0 and k == DBGK[0] and DBGWHAT[0] == "ln1":
                g.em("vector", _op("tensor_copy", out=m_dbgu[:, 0:1024],
                                   in_=m_ln1[:, 0:1024]), inc=(("dve", 1),))
                g.em("vector", _op("tensor_copy", out=m_dbgu[:, 1024:2048],
                                   in_=m_ln1[:, 1024:2048]), inc=(("dve", 1),))
                g.em("gpsimd", _op("dma_start", out=tapd["d_dbgu"], in_=m_dbgu),
                     wait=[("dve", g.now("dve"))], inc=(("stt", 16),))
                gset("attn0", "stt")
            apply_done = (("act", g.now("act")) if SUBQ2[0] >= 2
                          else ("dve", g.now("dve")))
            if taps and (SUBSTOP[0] == 1 or EARLYTAP[0] in (2, 3)) and q == 0:
                g.em("gpsimd", _op("dma_start",
                                   out=tapd["d_lnt"][128 * k:128 * (k + 1), :],
                                   in_=m_ln1[:]),
                     wait=[apply_done], inc=(("stt", 16),))
                g.em("gpsimd", _op("dma_start",
                                   out=tapd["d_ut"][128 * k:128 * (k + 1), :],
                                   in_=ut),
                     wait=[apply_done], inc=(("stt", 16),))
                gset(f"mu{sl}", "stt")
            if SUBQ2[0] >= 3:
                for j0 in (0, 8):
                    transpose_batch(
                        [(ln1T_dc(j)[:, MARG + 128 * k: MARG + 128 * (k + 1)],
                          m_ln1[:, 128 * j:128 * (j + 1)]) for j in range(j0, j0 + 8)],
                        [apply_done, rolls_done])
                gset("m_ln1", "pe")
        q2_done = {"pe": g.now("pe"), "dve": g.now("dve"), "act": g.now("act")}
        if taps and q == 0 and EARLYTAP[0]:
            g.em("gpsimd", _op("dma_start", out=tapd["d_ln1T2"], in_=m_ln1T[:]),
                 wait=[("pe", q2_done["pe"]), ("dve", q2_done["dve"]),
                       ("act", q2_done["act"])], inc=(("stt", 16),))

        if SUBSTOP[0] < 2:
            return
        ztemp = ([m_ln1[:, 0:512], m_ln1[:, 512:1024]] if ZTMP[0] == 0 else
                 [m_tmpA[:, 0:512], m_tmpA[:, 512:1024]])
        for si, s in enumerate(SLAB_ORDER):
            PHASE[0] = f"q{q}.Q4s{si}"
            wsl = si % 2
            wv = m_wbuf[:, 8192 * wsl:8192 * (wsl + 1)].rearrange(
                "p (dc j) -> p dc j", dc=16)
            for a in range(4):
                lane = f"ldw{wsl}" if a % 2 == 0 else f"ldp{wsl}"
                g.em("sync", _op("dma_start", out=wv[:, 4 * a:4 * a + 4, :],
                                 in_=cwin[:, 8192 * s + 2048 * a: 8192 * s + 2048 * (a + 1)]
                                 .rearrange("p (dc j) -> p dc j", dc=4)),
                     wait=gw(f"mwbuf{wsl}") if a == 0 else (),
                     inc=((lane, 16),))
            slab_ld = [(f"ldw{wsl}", g.now(f"ldw{wsl}")),
                       (f"ldp{wsl}", g.now(f"ldp{wsl}"))]
            for c4 in range(4):
                if SUBQ4[0] < 2:
                    continue
                oc = 4 * s + c4
                if SUBQ4[0] < 4 and oc >= 16:
                    continue
                slot, sname = next_group_slot()
                rhs_c0 = 0 if oc < 16 else MARG
                waits = slab_ld + [
                         ("dve", q2_done["dve"]), ("act", q2_done["act"])] + gw(sname)
                for dc in range(16):
                    g.em("tensor", _op("matmul", slot[:],
                                       wv[:, dc, 128 * c4:128 * (c4 + 1)],
                                       ln1T_dc(dc)[:, rhs_c0:rhs_c0 + 512],
                                       start=(dc == 0), stop=(dc == 15)),
                         wait=waits if dc == 0 else (),
                         inc=(("pe", 1),) if dc == 15 else ())
                gset(f"mwbuf{wsl}", "pe")
                mm_done = ("pe", g.now("pe"))
                if oc < 16:
                    zt = ztemp[oc % 2]
                    if SUBQ4[0] == 20:
                        continue
                    g.em("vector", _op("tensor_copy", out=zt, in_=slot[:]),
                         wait=[mm_done] + gw(f"ztemp{oc % 2}"), inc=(("dve", 1),))
                    gset(sname, "dve")
                    ev = ("dve", g.now("dve"))
                    if SUBQ4[0] >= 3:
                        transpose_batch(
                            [(m_z[:, 2048 * j + 128 * oc: 2048 * j + 128 * (oc + 1)],
                              zt[:, 128 * j:128 * (j + 1)]) for j in range(4)], [ev])
                        gset(f"ztemp{oc % 2}", "pe")
                else:
                    r = oc - 16
                    par = r % 2
                    raw = RAW[par]
                    xs_chunk = 16 <= oc < 32
                    c = oc - 16 if xs_chunk else None
                    dst = (m_xstemp[:, 63:575] if xs_chunk
                           else act_blk((oc - 32) if oc < 48 else (16 + oc - 48))[:, 63:575])
                    g.em("vector", _op("tensor_copy", out=raw[:, 2:514], in_=slot[:]),
                         wait=[mm_done] + gw(f"raw{par}"), inc=(("dvx", 1),))
                    ev_dvx = g.now("dvx")
                    gset(sname, "dvx")
                    g.em("vector", _op("tensor_copy", out=raw[:, 0:2],
                                       in_=m_rawcarry[:, 2 * r:2 * r + 2]),
                         wait=gw(f"rawc{r}"), inc=(("dvx", 1),))
                    rc_dvx = g.now("dvx")
                    g.em("vector", _op("tensor_copy", out=m_rawcarry[:, 2 * r:2 * r + 2],
                                       in_=raw[:, 512:514]),
                         wait=[("dvx", ev_dvx)], inc=(("dve", 1),))
                    gset(f"rawc{r}", "dve")
                    g.em("vector", _op("tensor_mul", ACC[1], raw[:, 0:512],
                                       c_convw[:, r:r + 1].broadcast_to([128, 512])),
                         wait=[("dvx", rc_dvx)] + gw("acc"))
                    g.em("vector", _op("scalar_tensor_tensor", out=ACC[0],
                                       in0=raw[:, 1:513],
                                       scalar=c_convw[:, 48 + r:49 + r],
                                       in1=ACC[1], op0=ALU.mult, op1=ALU.add))
                    g.em("vector", _op("scalar_tensor_tensor", out=ACC[0],
                                       in0=raw[:, 2:514],
                                       scalar=c_convw[:, 96 + r:97 + r],
                                       in1=ACC[0], op0=ALU.mult, op1=ALU.add),
                         inc=(("dve", 1),))
                    gset(f"raw{par}", "dve")
                    conv_done = ("dve", g.now("dve"))
                    if xs_chunk:
                        if first:
                            g.em("vector", _op("memset", m_xstemp[:, 0:63], 0.0),
                                 wait=gw("xstemp_r"), inc=(("dve", 1),))
                        else:
                            g.em("vector", _op("tensor_copy", out=m_xstemp[:, 0:63],
                                               in_=m_xscarry[:, 64 * c:64 * c + 63]),
                                 wait=gw("xstemp_r"), inc=(("dve", 1),))
                    carry_in = ("dve", g.now("dve"))
                    sw = [conv_done] + ([carry_in] if xs_chunk else []) + \
                        (gw("xstemp_r") if xs_chunk else [])
                    g.em("scalar", _op("activation", out=dst, in_=ACC[0], func=AF.Silu,
                                       bias=c_convw[:, 144 + r:145 + r], scale=1.0),
                         wait=sw)
                    g.em("scalar", _op("copy", out=st[:, 14:15], in_=dst[:, 0:1]),
                         inc=(("act", 1),))
                    gset("acc", "act")
                    silu_done = ("act", g.now("act"))
                    if xs_chunk:
                        if first:
                            g.em("vector", _op("memset", m_xstemp[:, 63:64], 0.0),
                                 wait=[silu_done], inc=(("dve", 1),))
                        g.em("vector", _op("tensor_copy",
                                           out=m_xscarry[:, 64 * c:64 * c + 63],
                                           in_=m_xstemp[:, 512:575]),
                             wait=[silu_done], inc=(("dve", 1),))
                        g.em("vector", _op("tensor_mul", m_tmpA[:, 0:512],
                                           m_xstemp[:, 0:512], act_blk(c)[:, 0:512]),
                             wait=[silu_done] + gw("m_tmpA"), inc=(("dve", 1),))
                        g.em("vector", _op("tensor_reduce", out=m_cs[:, 8 * c:8 * (c + 1)],
                                           in_=m_tmpA[:, 0:512].rearrange(
                                               "p (cc t) -> p cc t", cc=8),
                                           axis=AX.X, op=ALU.add),
                             inc=(("dvx", 1),))
                        if first:
                            g.em("vector", _op("memset", m_cs[:, 8 * c:8 * c + 1], 0.0))
                        transpose_batch(
                            [(m_xs[:, 2048 * j + 128 * c: 2048 * j + 128 * (c + 1)],
                              m_xstemp[:, 128 * j:128 * (j + 1)]) for j in range(4)],
                            [silu_done, carry_in])
                        gset("xstemp_r", "pe")
        q4_pe = g.now("pe")
        cs_dvx = g.now("dvx")

        for ci in range(1, 8):
            for b in range(16):
                w = ()
                if b == 0:
                    w = ([("dvx", cs_dvx)] + gw("prev_r") if ci == 1
                         else [("dvx", g.now("dvx"))])
                g.em("vector", _op("scalar_tensor_tensor",
                                   out=m_prev[:, 9 * b + ci:9 * b + ci + 1],
                                   in0=m_prev[:, 9 * b + ci - 1:9 * b + ci],
                                   scalar=c_decay[:, b:b + 1],
                                   in1=m_cs[:, 8 * b + ci - 1:8 * b + ci],
                                   op0=ALU.mult, op1=ALU.add),
                     wait=w, inc=(("dvx", 1),) if b == 15 else ())
        g.em("vector", _op("tensor_copy", out=m_prevbf[:], in_=m_prev[:]),
             wait=[("dvx", g.now("dvx"))], inc=(("dve", 1),))
        prevbf_done = ("dve", g.now("dve"))
        for b in range(16):
            g.em("vector", _op("scalar_tensor_tensor",
                               out=m_prev[:, 9 * b:9 * b + 1],
                               in0=m_prev[:, 9 * b + 7:9 * b + 8],
                               scalar=c_decay[:, b:b + 1],
                               in1=m_cs[:, 8 * b + 7:8 * b + 8],
                               op0=ALU.mult, op1=ALU.add),
                 inc=(("dve", 1),) if b == 15 else ())

        if SUBSTOP[0] >= 3:
            gen_ssd_and_q6(q, t0, prevbf_done, q4_pe)

    def gen_ssd_pair(q, p, prevbf_done, n_vchunks=2):
        if SUBG[0] == 0:
            return
        PHASE[0] = f"q{q}.ssd{p}"
        horder = list(range(0, 32, 2)) + list(range(1, 32, 2))
        mask_end = {}
        ydiag_end = {}
        for b4 in range(min(9, SUBB4[0])):
            if b4 < 8:
                for i in range(4):
                    h = horder[4 * b4 + i]
                    hb = 64 * (h % 2)
                    blk = h // 2
                    lhsT = m_act[hb:hb + 64, QW * blk + 128 * p: QW * blk + 128 * (p + 1)]
                    rhs = m_act[hb:hb + 64,
                                QW * (16 + blk) + 128 * p: QW * (16 + blk) + 128 * (p + 1)]
                    w = []
                    if i == 0:
                        w = gw("pg_bank")
                        if b4 == 0:
                            w = w + [prevbf_done, ("act", g.now("act")),
                                     ("pool", g.now("pool"))]
                    g.em("tensor", _op("matmul", PG[:, 128 * i:128 * (i + 1)],
                                       lhsT, rhs, start=True, stop=True,
                                       tile_position=(hb, 0)),
                         wait=w, inc=(("pe", 1),) if i == 3 else ())
                g_end = g.now("pe")
                if SUBG[0] >= 2:
                    for i in range(4):
                        w = [("pe", g_end)] if i == 0 else []
                        if b4 >= 2 and i == 0 and (b4 - 2) in ydiag_end:
                            w = w + [("pe", ydiag_end[b4 - 2])]
                        g.em("vector", _op("tensor_mul",
                                           m_gt[:, 512 * (b4 % 2) + 128 * i:
                                                512 * (b4 % 2) + 128 * (i + 1)],
                                           PG[:, 128 * i:128 * (i + 1)], c_gmask[:]),
                             wait=w, inc=(("dve", 1),) if i == 3 else ())
                    mask_end[b4] = g.now("dve")
                    guard["pg_bank"] = ("dve", mask_end[b4])
                else:
                    g.em("vector", _op("memset", st[:, 14:15], 0.0),
                         wait=[("pe", g_end)], inc=(("dve", 1),))
                    mask_end[b4] = g.now("dve")
                    guard["pg_bank"] = ("dve", mask_end[b4])
            if b4 >= 1 and SUBQ5[0] >= 2:
                bb = b4 - 1
                for i in range(4):
                    hh = horder[4 * bb + i]
                    hb = 64 * (hh % 2)
                    blk = hh // 2
                    w = []
                    if i == 0:
                        w = [("dve", mask_end[bb])]
                        if bb == 0:
                            w = w + gw("pb_yacc", "pbx0", "pbx1", "pbx2", "pbx3")
                    g.em("tensor", _op("matmul", PB[:, 64 * hh:64 * (hh + 1)],
                                       m_gt[:, 512 * (bb % 2) + 128 * i:
                                            512 * (bb % 2) + 128 * (i + 1)],
                                       m_xs[:, 2048 * p + 64 * hh: 2048 * p + 64 * (hh + 1)],
                                       start=True, stop=True),
                         wait=w, inc=())
                    for cl in range(n_vchunks):
                        ci = 2 * p + cl
                        g.em("tensor", _op("matmul",
                                           PV[64 * (ci % 2):64 * (ci % 2) + 64, hh:hh + 1],
                                           m_act[hb:hb + 64,
                                                 QW * (16 + blk) + 64 * ci: QW * (16 + blk) + 64 * (ci + 1)],
                                           m_prevbf[hb:hb + 64, 9 * blk + ci:9 * blk + ci + 1],
                                           start=True, stop=True,
                                           tile_position=(hb, 64 * (ci % 2))),
                             wait=gw("pv_r") if (hh == 0 and cl == 0) else (),
                             inc=(("pe", 1),) if i == 3 and cl == n_vchunks - 1 else ())
                ydiag_end[bb] = g.now("pe")
        pe_all = g.now("pe")
        if SUBQ5[0] < 2:
            guard[f"yready{p}"] = ("pe", pe_all)
            return
        g.em("vector", _op("tensor_copy", out=m_vcol[:], in_=PV[:, 0:32]),
             wait=[("pe", pe_all)], inc=(("dvx", 1),))
        gset("pv_r", "dvx")
        vcol_dvx = g.now("dvx")
        ysl = p % 2
        for b in range(4):
            g.em("vector", _op("tensor_add",
                               m_y[:, 2048 * ysl + 512 * b: 2048 * ysl + 512 * (b + 1)]
                               .rearrange("p (hh t) -> p hh t", hh=8),
                               PB[:, 512 * b:512 * (b + 1)]
                               .rearrange("p (hh t) -> p hh t", hh=8),
                               m_vcol[:, 8 * b:8 * (b + 1)].unsqueeze(2)
                               .broadcast_to([128, 8, 64])),
                 wait=(gw(f"my{ysl}") + [("dvx", vcol_dvx)]) if b == 0 else (),
                 inc=(("dve", 1),) if b == 3 else ())
        gset("pb_yacc", "dve")
        guard[f"yready{p}"] = ("dve", g.now("dve"))

    def gen_q6_tilepair(q, t0, tiles, store_rows=None):
        if SUBQ5[0] < 3:
            return
        tmpB = m_ln1
        for ti, t in enumerate(tiles):
            PHASE[0] = f"q{q}.q6a_t{t}"
            ysl = t % 2
            ytile = m_y[:, 2048 * ysl:2048 * (ysl + 1)]
            ztile = m_z[:, 2048 * t:2048 * (t + 1)]
            g.em("scalar", _op("activation", out=m_tmpA[:], in_=ztile, func=AF.Silu,
                               bias=0.0, scale=1.0),
                 wait=[guard[f"yready{t}"]] + gw("m_tmpA"))
            g.em("scalar", _op("copy", out=st[:, 14:15], in_=m_tmpA[:, 0:1]),
                 inc=(("act", 1),))
            sz = ("act", g.now("act"))
            g.em("vector", _op("tensor_mul", tmpB[:], ytile, m_tmpA[:]),
                 wait=[sz] + gw("m_ln1"), inc=(("dve", 1),))
            gset(f"my{ysl}", "dve")
            g.em("vector", _op("tensor_mul", SQ, tmpB[:], tmpB[:]))
            g.em("vector", _op("tensor_reduce", out=st[:, 8:9], in_=SQ, axis=AX.X,
                               op=ALU.add), inc=(("dvx", 1),))
            g.em("vector", _op("tensor_scalar", out=st[:, 9:10], in0=st[:, 8:9],
                               scalar1=RD, scalar2=RMSEPS, op0=ALU.mult, op1=ALU.add),
                 wait=[("dvx", g.now("dvx"))], inc=(("dve", 1),))
            g.em("scalar", _op("activation", out=st[:, 10:11], in_=st[:, 9:10],
                               func=AF.Sqrt, bias=0.0, scale=1.0),
                 wait=[("dve", g.now("dve"))])
            g.em("scalar", _op("copy", out=st[:, 14:15], in_=st[:, 10:11]),
                 inc=(("act", 1),))
            g.em("vector", _op("reciprocal", out=st[:, 11:12], in_=st[:, 10:11]),
                 wait=[("act", g.now("act"))], inc=(("dve", 1),))
            g.em("scalar", _op("activation", out=m_tmpA[:], in_=tmpB[:], func=AF.Identity,
                               scale=st[:, 11:12], bias=0.0),
                 wait=[("dve", g.now("dve"))])
            g.em("scalar", _op("copy", out=st[:, 14:15], in_=st[:, 11:12]),
                 inc=(("act", 1),))
            g.em("vector", _op("tensor_mul", tmpB[:], m_tmpA[:], c_rmsbc[:]),
                 wait=[("act", g.now("act"))], inc=(("dve", 1),))
            gset("m_tmpA", "dve")
            yh = ("dve", g.now("dve"))
            for j0 in (0, 8):
                transpose_batch(
                    [(m_yT[:, 2048 * ti + 128 * j: 2048 * ti + 128 * (j + 1)],
                      tmpB[:, 128 * j:128 * (j + 1)]) for j in range(j0, j0 + 8)],
                    [yh])
            gset("m_ln1", "pe")
        a_ev = [("dve", g.now("dve")), ("act", g.now("act"))]
        if SUBQ5[0] < 4 and SUBQ5[0] not in (40, 41):
            return

        for j in range(4):
            PHASE[0] = f"q{q}.q6b_j{j}"
            wsl = j % 2
            for aa in range(4):
                dst_ap = (m_wbuf[:, 8192 * wsl + 2048 * aa:
                                 8192 * wsl + 2048 * (aa + 1)]
                          if WOUTDST[0] == 0 else
                          x1[0:128, 0:2048].bitcast(BF16)[:, 0:2048])
                g.em("sync", _op("dma_start",
                                 out=dst_ap,
                                 in_=cwout[:, 8192 * j + 2048 * aa:
                                            8192 * j + 2048 * (aa + 1)]),
                     wait=gw(f"mwbuf{wsl}") if aa == 0 else (),
                     inc=(((f"ldw{wsl}" if aa % 2 == 0 else f"ldp{wsl}"), 16),))
            wld = [(f"ldw{wsl}", g.now(f"ldw{wsl}")), (f"ldp{wsl}", g.now(f"ldp{wsl}"))]
            for ti, t in enumerate(tiles):
                if SUBQ5[0] == 40:
                    continue
                slot = PBq[(2 * j + ti) % 4]
                for dc in range(16):
                    g.em("tensor", _op("matmul", slot,
                                       m_yT[:, 2048 * ti + 128 * dc: 2048 * ti + 128 * (dc + 1)],
                                       m_wbuf[:, 8192 * wsl + 512 * dc: 8192 * wsl + 512 * (dc + 1)],
                                       start=(dc == 0), stop=(dc == 15)),
                         wait=(wld + a_ev
                               + gw(f"pbx{(2 * j + ti) % 4}", "pb_yacc")
                               if dc == 0 else ()),
                         inc=(("pe", 1),) if dc == 15 else ())
                gset(f"mwbuf{wsl}", "pe")
                mmv = ("pe", g.now("pe"))
                if SUBQ5[0] == 41:
                    g.em("vector", _op("tensor_copy", out=st[:, 12:13],
                                       in_=slot[:, 0:1]),
                         wait=[mmv], inc=(("dve", 1),))
                    guard[f"pbx{(2 * j + ti) % 4}"] = ("dve", g.now("dve"))
                    continue
                g.em("scalar", _op("copy",
                                   out=m_attn[:, 2048 * ti + 512 * j: 2048 * ti + 512 * (j + 1)],
                                   in_=slot),
                     wait=[mmv] + (gw(f"attn{ti}") if j == 0 else []), inc=(("act", 1),))
                guard[f"pbx{(2 * j + ti) % 4}"] = ("act", g.now("act"))
        attn_act = g.now("act")
        if SUBQ5[0] < 5 or SUBQ5[0] in (40, 41):
            return

        for ti, t in enumerate(tiles):
            rows = (t0 - MARG + 128 * t, t0 - MARG + 128 * (t + 1))
            sr = store_rows[ti] if store_rows else (max(rows[0], 0), min(rows[1], L))
            atile = m_attn[:, 2048 * ti:2048 * (ti + 1)]
            usl = t % 2
            ut = m_u[:, 2048 * usl:2048 * (usl + 1)]
            lo_off = sr[0] - rows[0]
            hi_off = sr[1] - rows[0]
            g.em("sync", _op("dma_start", out=ut[lo_off:hi_off, :],
                             in_=u[sr[0]:sr[1], :]),
                 wait=gw(f"mu{usl}"), inc=((f"ldu{usl}", 16),))
            uld = g.now(f"ldu{usl}")
            g.em("vector", None, wait=[("act", attn_act)])
            ln_stats(atile, LNEPS)
            g.em("scalar", _op("activation", out=SQ, in_=atile, func=AF.Identity,
                               scale=st[:, 7:8], bias=st[:, 8:9]),
                 wait=[("dve", g.now("dve"))])
            g.em("scalar", _op("copy", out=st[:, 14:15], in_=st[:, 7:8]),
                 inc=(("act", 1),))
            gset(f"attn{ti}", "act")
            g.em("vector", _op("tensor_add", ut, SQ, ut),
                 wait=[(f"ldu{usl}", uld), ("act", g.now("act"))], inc=(("dve", 1),))
            addv = ("dve", g.now("dve"))
            g.em("gpsimd", _op("dma_start", out=x1[sr[0]:sr[1], :],
                               in_=ut[lo_off:hi_off, :]),
                 wait=[addv], inc=((f"stu{usl}", 16),))
            gset(f"mu{usl}", f"stu{usl}")

    def gen_ssd_and_q6(q, t0, prevbf_done, q4_pe):
        for tp in (0, 2):
            for p in (tp, tp + 1):
                if p < SUBNP[0]:
                    gen_ssd_pair(q, p, prevbf_done)
            gen_q6_tilepair(q, t0, [tp, tp + 1])
        gset("prev_r", "pe")

    def gen_epilogue():
        t0 = L
        PHASE[0] = "epi"
        w = gw("mix_pe_prev")
        for dc in range(16):
            g.em("vector", _op("tensor_copy", out=ln1T_dc(dc)[:, 0:MARG],
                               in_=ln1T_dc(dc)[:, QT:QT + MARG]), wait=w)
        for b in range(32):
            g.em("vector", _op("tensor_copy", out=act_blk(b)[:, 0:63],
                               in_=act_blk(b)[:, QT:QT + 63]), wait=w)
        g.em("vector", _op("tensor_copy", out=m_prevbf[:], in_=m_prev[:]),
             inc=(("dve", 1),))
        prevbf_done = ("dve", g.now("dve"))
        rolls = ("dve", g.now("dve"))

        raw3 = ACC[1][:, 0:3]
        prev_pool = None
        for r in range(48):
            g.em("vector", _op("tensor_copy", out=raw3[:, 0:2],
                               in_=m_rawcarry[:, 2 * r:2 * r + 2]),
                 wait=(gw("acc") if r == 0 else []) +
                      ([prev_pool] if prev_pool else []))
            g.em("vector", _op("memset", raw3[:, 2:3], 0.0), inc=(("dve", 1),))
            rdv = ("dve", g.now("dve"))
            g.em("gpsimd", _op("tensor_mul", ACC[0][:, 0:1], raw3[:, 1:2],
                               c_convw[:, 48 + r:49 + r]),
                 wait=[rdv, ("act", g.now("act"))])
            g.em("gpsimd", _op("tensor_mul", ACC[0][:, 2:3], raw3[:, 0:1],
                               c_convw[:, r:r + 1]), inc=(("pox", 1),))
            g.em("gpsimd", _op("tensor_add", ACC[0][:, 1:2], ACC[0][:, 0:1],
                               ACC[0][:, 2:3]),
                 wait=[("pox", g.now("pox"))], inc=(("pool", 1),))
            cdv = ("pool", g.now("pool"))
            prev_pool = cdv
            xs_chunk = r < 16
            if xs_chunk:
                c = r
                g.em("vector", _op("tensor_copy", out=m_xstemp[:, 0:63],
                                   in_=m_xscarry[:, 64 * c:64 * c + 63]),
                     wait=([rolls] if c == 0 else []) + gw("xstemp_r"),
                     inc=(("dve", 1),))
                cin = ("dve", g.now("dve"))
                g.em("scalar", _op("activation", out=m_xstemp[:, 63:64],
                                   in_=ACC[0][:, 1:2], func=AF.Silu,
                                   bias=c_convw[:, 144 + r:145 + r], scale=1.0),
                     wait=[cdv, cin])
                g.em("scalar", _op("copy", out=st[:, 14:15], in_=m_xstemp[:, 63:64]),
                     inc=(("act", 1),))
                sd = ("act", g.now("act"))
                transpose_batch([(m_xs[:, 128 * c:128 * (c + 1)],
                                  m_xstemp[:, 0:128])], [sd])
                gset("xstemp_r", "pe")
            else:
                b = (r - 16) if r < 32 else (16 + r - 32)
                g.em("scalar", _op("activation", out=act_blk(b)[:, 63:64],
                                   in_=ACC[0][:, 1:2], func=AF.Silu,
                                   bias=c_convw[:, 144 + r:145 + r], scale=1.0),
                     wait=[cdv, rolls])
                g.em("scalar", _op("copy", out=st[:, 14:15], in_=act_blk(b)[:, 63:64]),
                     inc=(("act", 1),))
        conv_done = {"act": g.now("act"), "pe": g.now("pe"), "dve": g.now("dve")}

        for si, s in enumerate(range(4)):
            wsl = si % 2
            wv = m_wbuf[:, 8192 * wsl:8192 * (wsl + 1)].rearrange(
                "p (dc j) -> p dc j", dc=16)
            for a in range(4):
                g.em("sync", _op("dma_start", out=wv[:, 4 * a:4 * a + 4, :],
                                 in_=cwin[:, 8192 * s + 2048 * a: 8192 * s + 2048 * (a + 1)]
                                 .rearrange("p (dc j) -> p dc j", dc=4)),
                     wait=gw(f"mwbuf{wsl}") if a == 0 else (), inc=((f"ldw{wsl}", 16),))
            slab_ld = g.now(f"ldw{wsl}")
            for c4 in range(4):
                oc = 4 * s + c4
                slot, sname = next_group_slot()
                for dc in range(16):
                    g.em("tensor", _op("matmul", slot[:, 0:128],
                                       wv[:, dc, 128 * c4:128 * (c4 + 1)],
                                       ln1T_dc(dc)[:, 0:128],
                                       start=(dc == 0), stop=(dc == 15)),
                         wait=([(f"ldw{wsl}", slab_ld), ("dve", conv_done["dve"]),
                                ("act", conv_done["act"])] + gw(sname))
                         if dc == 0 else (),
                         inc=(("pe", 1),) if dc == 15 else ())
                gset(f"mwbuf{wsl}", "pe")
                mm_done = ("pe", g.now("pe"))
                zt = ztemp_ep[oc % 2]
                g.em("vector", _op("tensor_copy", out=zt[:, 0:128], in_=slot[:, 0:128]),
                     wait=[mm_done] + gw(f"ztemp{oc % 2}"), inc=(("dve", 1),))
                gset(sname, "dve")
                ev = ("dve", g.now("dve"))
                transpose_batch([(m_z[:, 128 * oc:128 * (oc + 1)], zt[:, 0:128])],
                                [ev])
                gset(f"ztemp{oc % 2}", "pe")

        gen_ssd_pair(4, 0, prevbf_done, n_vchunks=1)
        gen_q6_tilepair(4, t0, [0])
        g.barrier()
        if taps and TAPQ[0] == 4:
            for nm, buf in (("d_y", m_y), ("d_z", m_z), ("d_xs", m_xs),
                            ("d_act", m_act), ("d_prev", m_prev),
                            ("d_ln1T", m_ln1T)):
                g.em("gpsimd", _op("dma_start", out=tapd[nm], in_=buf[:]),
                     inc=(("stt", 16),))
            g.barrier()

    ztemp_ep = None

    nq = {"prologue": 0, "q0": 1, "q1": 2, "q2": 3, "q3": 4}.get(stop_after, 4)
    for q in range(nq):
        gen_quarter(q)
    ztemp_ep = [m_ln1[:, 0:512], m_ln1[:, 512:1024]]
    if stop_after in ("full", "mixer", "ffn1"):
        gen_epilogue()

    stk_mix.close()
    g.barrier()

    stk_ffn = ExitStack()
    f_big1 = sb(stk_ffn, "f_big1", [128, 32768], BF16)
    f_big2 = sb(stk_ffn, "f_big2", [128, 32768], BF16)
    f_mid = sb(stk_ffn, "f_mid", [128, 16384], BF16)
    f_gtmp = sb(stk_ffn, "f_gtmp", [128, 1024], BF16)
    f_utmp = sb(stk_ffn, "f_utmp", [128, 1024], BF16)
    assert nc.sbuf_bytes_remaining >= 0

    ln2T = f_mid[:, 0:8192]
    x1f1 = [f_mid[:, 8192 + 4096 * i:8192 + 4096 * (i + 1)].bitcast(F32)
            for i in range(2)]
    down_stage = f_mid[:, 0:16384].bitcast(F32)
    x1f4 = [f_big1[:, 4096 * i:4096 * (i + 1)].bitcast(F32) for i in range(2)]
    ln2stage = f_big1[:, 0:2048]

    def gen_ffn_pass(pi):
        p0 = QT * pi
        for k in range(4):
            PHASE[0] = f"f{pi}.F1k{k}"
            sl = k % 2
            xt = x1f1[sl]
            g.em("sync", _op("dma_start", out=xt,
                             in_=x1[p0 + 128 * k:p0 + 128 * (k + 1), :]),
                 wait=gw(f"fx1{sl}"), inc=((f"ldu{sl}", 16),))
            ldv = g.now(f"ldu{sl}")
            g.em("vector", None, wait=[(f"ldu{sl}", ldv)])
            ln_stats(xt, LNEPS)
            g.em("scalar", _op("activation", out=ln2stage, in_=xt, func=AF.Identity,
                               scale=st[:, 7:8], bias=st[:, 8:9]),
                 wait=[("dve", g.now("dve"))] + gw("ln2stage"))
            g.em("scalar", _op("copy", out=st[:, 14:15], in_=st[:, 7:8]),
                 inc=(("act", 1),))
            gset(f"fx1{sl}", "act")
            ap = ("act", g.now("act"))
            for j0 in (0, 8):
                transpose_batch(
                    [(ln2T[:, 512 * j + 128 * k: 512 * j + 128 * (k + 1)],
                      ln2stage[:, 128 * j:128 * (j + 1)]) for j in range(j0, j0 + 8)],
                    [ap])
            gset("ln2stage", "pe")
        f1_ev = [("dve", g.now("dve")), ("act", g.now("act"))]

        upslots = [PV[:], PG[:]]
        for s in range(16):
            PHASE[0] = f"f{pi}.F2s{s}"
            wsl = s % 2
            wgb = f_big2[:, 8192 * wsl:8192 * (wsl + 1)]
            wub = f_big2[:, 16384 + 8192 * wsl:16384 + 8192 * (wsl + 1)]
            g.em("sync", _op("dma_start", out=wgb,
                             in_=cwg[:, 8192 * s:8192 * (s + 1)]),
                 wait=gw(f"fwg{wsl}"), inc=((f"ldw{wsl}", 16),))
            g.em("sync", _op("dma_start", out=wub,
                             in_=cwu[:, 8192 * s:8192 * (s + 1)]),
                 wait=gw(f"fwu{wsl}"), inc=((f"ldv{wsl}", 16),))
            slab_ld = [(f"ldw{wsl}", g.now(f"ldw{wsl}")), (f"ldv{wsl}", g.now(f"ldv{wsl}"))]
            for c4 in range(4):
                oc = 4 * s + c4
                gslot = PBq[oc % 4]
                uslot = upslots[oc % 2]
                for hc in range(16):
                    g.em("tensor", _op("matmul", gslot,
                                       wgb.rearrange("p (hc j) -> p hc j", hc=16)[:, hc, 128 * c4:128 * (c4 + 1)],
                                       ln2T[:, 512 * hc:512 * (hc + 1)],
                                       start=(hc == 0), stop=(hc == 15)),
                         wait=(slab_ld + f1_ev + gw(f"fpb{oc % 4}"))
                         if hc == 0 else (),
                         inc=(("pe", 1),) if hc == 15 else ())
                gmm = ("pe", g.now("pe"))
                for hc in range(16):
                    g.em("tensor", _op("matmul", uslot,
                                       wub.rearrange("p (hc j) -> p hc j", hc=16)[:, hc, 128 * c4:128 * (c4 + 1)],
                                       ln2T[:, 512 * hc:512 * (hc + 1)],
                                       start=(hc == 0), stop=(hc == 15)),
                         wait=gw(f"fup{oc % 2}") if hc == 0 else (),
                         inc=(("pe", 1),) if hc == 15 else ())
                gset(f"fwg{wsl}", "pe")
                gset(f"fwu{wsl}", "pe")
                umm = ("pe", g.now("pe"))
                gt = f_gtmp[:, 512 * (oc % 2):512 * (oc % 2 + 1)]
                ut = f_utmp[:, 512 * (oc % 2):512 * (oc % 2 + 1)]
                g.em("scalar", _op("activation", out=gt, in_=gslot, func=AF.Silu,
                                   bias=0.0, scale=c_sgate[:, oc:oc + 1]),
                     wait=[gmm] + gw(f"fgt{oc % 2}"))
                g.em("scalar", _op("copy", out=st[:, 14:15], in_=gt[:, 0:1]),
                     inc=(("act", 1),))
                guard[f"fpb{oc % 4}"] = ("act", g.now("act"))
                sg = ("act", g.now("act"))
                g.em("vector", _op("tensor_scalar", out=ut, in0=uslot,
                                   scalar1=c_sup[:, oc:oc + 1], scalar2=None,
                                   op0=ALU.mult),
                     wait=[umm] + gw(f"fut{oc % 2}"), inc=(("dve", 1),))
                guard[f"fup{oc % 2}"] = ("dve", g.now("dve"))
                g.em("vector", _op("tensor_mul",
                                   f_big1[:, 512 * oc:512 * (oc + 1)], gt, ut),
                     wait=[sg], inc=(("dve", 1),))
                guard[f"fgt{oc % 2}"] = ("dve", g.now("dve"))
                guard[f"fut{oc % 2}"] = ("dve", g.now("dve"))
        f2_dve = g.now("dve")

        for s in range(8):
            PHASE[0] = f"f{pi}.F3s{s}"
            wsl = s % 2
            wdb = f_big2[:, 16384 * wsl:16384 * (wsl + 1)]
            g.em("sync", _op("dma_start", out=wdb,
                             in_=cwd[:, 16384 * s:16384 * (s + 1)]),
                 wait=gw(f"fwd{wsl}")
                 + (gw("fwg0", "fwg1") if (s % 2 == 0) else gw("fwu0", "fwu1"))
                 + ([("dve", f2_dve)] if s < 2 else []),
                 inc=((f"ldw{wsl}", 16),))
            slab_ld = g.now(f"ldw{wsl}")
            wdv = wdb.rearrange("p (oc j) -> p oc j", oc=64)
            for k in range(4):
                slot = PBq[(4 * s + k) % 4]
                for oc in range(64):
                    g.em("tensor", _op("matmul", slot[:, 0:256],
                                       f_big1[:, 512 * oc + 128 * k: 512 * oc + 128 * (k + 1)],
                                       wdv[:, oc, :],
                                       start=(oc == 0), stop=(oc == 63)),
                         wait=([(f"ldw{wsl}", slab_ld), ("dve", f2_dve)]
                               + gw(f"fpb{(4 * s + k) % 4}", "dstage")) if oc == 0 else (),
                         inc=(("pe", 1),) if oc == 63 else ())
                gset(f"fwd{wsl}", "pe")
                mmv = ("pe", g.now("pe"))
                g.em("vector", _op("tensor_mul",
                                   down_stage[:, 2048 * k + 256 * s: 2048 * k + 256 * (s + 1)],
                                   slot[:, 0:256], c_sdbc[:, 256 * s:256 * (s + 1)]),
                     wait=[mmv], inc=(("dve", 1),))
                guard[f"fpb{(4 * s + k) % 4}"] = ("dve", g.now("dve"))
        f3_dve = g.now("dve")

        for k in range(4):
            sl = k % 2
            xt = x1f4[sl]
            g.em("sync", _op("dma_start", out=xt,
                             in_=x1[p0 + 128 * k:p0 + 128 * (k + 1), :]),
                 wait=gw(f"fx4{sl}") + ([("dve", f3_dve)] if k < 2 else []),
                 inc=((f"ldu{sl}", 16),))
            ldv = g.now(f"ldu{sl}")
            dtile = down_stage[:, 2048 * k:2048 * (k + 1)]
            g.em("vector", None, wait=[("dve", f3_dve)])
            ln_stats(dtile, LNEPS)
            g.em("scalar", _op("activation", out=SQ, in_=dtile, func=AF.Identity,
                               scale=st[:, 7:8], bias=st[:, 8:9]),
                 wait=[("dve", g.now("dve"))])
            g.em("scalar", _op("copy", out=st[:, 14:15], in_=st[:, 7:8]),
                 inc=(("act", 1),))
            gset("dstage", "act")
            g.em("vector", _op("tensor_add", xt, SQ, xt),
                 wait=[(f"ldu{sl}", ldv), ("act", g.now("act"))], inc=(("dve", 1),))
            addv = ("dve", g.now("dve"))
            g.em("gpsimd", _op("dma_start", out=out[p0 + 128 * k:p0 + 128 * (k + 1), :],
                               in_=xt),
                 wait=[addv], inc=((f"stu{sl}", 16),))
            gset(f"fx4{sl}", f"stu{sl}")
        g.barrier()

    nf = {"full": 4, "ffn1": 1}.get(stop_after, 0)
    for pi in range(nf):
        gen_ffn_pass(pi)

    stk_ffn.close()

    final_waits = [(c, g.cnt[c]) for c in list(g.cnt) if c.startswith(("st", "ld"))]
    g.em("gpsimd", None, wait=final_waits)

    global _LAST_GEN
    _LAST_GEN = g

    with stk_root:
        sems = {}
        sem_stack = ExitStack()
        with sem_stack:
            for chn in list(g.cnt.keys()):
                sems[chn] = sem_stack.enter_context(nc.semaphore(f"s_{chn}"))
            with nc.Block() as block:
                def mk(name):
                    def body(eng):
                        for fn, waits, incs in g.ops[name]:
                            for chn, v in waits:
                                eng.wait_ge(sems[chn], v)
                            ins = fn(eng) if fn is not None else None
                            for chn, amt in incs:
                                ins.then_inc(sems[chn], amt)
                    return body
                block.sync(mk("sync"))
                block.tensor(mk("tensor"))
                block.vector(mk("vector"))
                block.scalar(mk("scalar"))
                block.gpsimd(mk("gpsimd"))
    return nc


def _host_inputs(inputs):
    bf = ml_dtypes.bfloat16
    W_in = np.asarray(inputs["L0_attn_W_in"], dtype=np.float32)
    W_out = np.asarray(inputs["L0_attn_W_out"], dtype=np.float32)
    conv_w = np.asarray(inputs["L0_attn_conv_w"], dtype=np.float32)
    conv_b = np.asarray(inputs["L0_attn_conv_b"], dtype=np.float32)
    A_log = np.asarray(inputs["L0_attn_A_log"], dtype=np.float32)
    rms_w = np.asarray(inputs["L0_attn_rms_w"], dtype=np.float32)
    s_gate = np.asarray(inputs["L0_ffn_s_gate"], dtype=np.float32)
    s_up = np.asarray(inputs["L0_ffn_s_up"], dtype=np.float32)
    s_down = np.asarray(inputs["L0_ffn_s_down"], dtype=np.float32)

    shared = {
        "w_inT": np.ascontiguousarray(W_in.T),
        "w_outT": np.ascontiguousarray(W_out.T),
        "wg8": np.ascontiguousarray(np.asarray(inputs["L0_ffn_w_gate"], dtype=np.int8)),
        "wu8": np.ascontiguousarray(np.asarray(inputs["L0_ffn_w_up"], dtype=np.int8)),
        "wd8": np.ascontiguousarray(np.asarray(inputs["L0_ffn_w_down"], dtype=np.int8)),
        "in_ident": np.eye(128, dtype=bf),
        "in_gmask": np.asarray(
            (np.arange(128)[:, None] // 64 == np.arange(128)[None, :] // 64)
            & (np.arange(128)[None, :] >= np.arange(128)[:, None]), dtype=bf),
        "in_convw": np.ascontiguousarray(np.concatenate(
            [conv_w[:, 0, k].reshape(48, 128).T for k in range(3)]
            + [conv_b.reshape(48, 128).T], axis=1)),
        "in_sgate": np.ascontiguousarray(s_gate.reshape(64, 128).T),
        "in_sup": np.ascontiguousarray(s_up.reshape(64, 128).T),
        "in_sdbc": np.ascontiguousarray(np.tile(s_down[None, :], (128, 1))),
        "in_rmsbc": np.ascontiguousarray(np.tile(rms_w[None, :], (128, 1))),
        "in_arep": np.ascontiguousarray(np.repeat(A_log, 64).reshape(16, 128).T),
    }
    u_all = np.asarray(inputs["L0_attn_u"], dtype=np.float32)
    in_maps = []
    for b in range(8):
        m = {"u": np.ascontiguousarray(u_all[b])}
        m.update(shared)
        in_maps.append(m)
    return in_maps


def kernel(**inputs):
    in_maps = _host_inputs(inputs)
    nc = build()
    res = run_bass_kernel_spmd(nc, in_maps, core_ids=list(range(8)))
    return np.stack([np.asarray(res.results[b]["out"]) for b in range(8)], axis=0)


# version 12
# speedup vs baseline: 1.3806x; 1.0075x over previous
import numpy as np
import ml_dtypes
import concourse.bass as bass
import concourse.mybir as mybir
from concourse.bass_utils import run_bass_kernel_spmd
from contextlib import ExitStack

F32 = mybir.dt.float32
BF16 = mybir.dt.bfloat16
I8 = mybir.dt.int8
I32 = mybir.dt.int32
AF = mybir.ActivationFunctionType
ALU = mybir.AluOpType
AX = mybir.AxisListType

D = 2048
L = 2048
NH = 32
DH = 64
CH = 64
QT = 512
MARG = 64
QW = QT + MARG
LNEPS = 1e-5
RMSEPS = 1.1920929e-07
RD = 1.0 / D

ENGINES = ("sync", "tensor", "vector", "scalar", "gpsimd")
OWN = {"tensor": "pe", "vector": "dve", "scalar": "act", "gpsimd": "pool"}


class Gen:

    def __init__(self):
        self.ops = {e: [] for e in ENGINES}
        import collections
        self.cnt = collections.defaultdict(int)
        for c in ("pe", "dve", "act", "pool"):
            self.cnt[c] = 0
        self.pend = {e: {} for e in ENGINES}
        self.ptags = []

    def now(self, ch):
        return self.cnt[ch]

    def snap(self):
        return dict(self.cnt)

    def barrier(self, snap=None):
        s = snap or self.snap()
        for e in ENGINES:
            for ch, v in s.items():
                if OWN.get(e) == ch:
                    continue
                if v > self.pend[e].get(ch, 0):
                    self.pend[e][ch] = v

    def em(self, eng, fn, wait=(), inc=()):
        if eng == "tensor":
            self.ptags.append(PHASE[0])
        wd = dict(self.pend[eng])
        self.pend[eng] = {}
        for ch, v in wait:
            if v is None:
                continue
            if OWN.get(eng) == ch:
                continue
            if v > wd.get(ch, 0):
                wd[ch] = v
        post = {}
        for ch, amt in inc:
            self.cnt[ch] += amt
            post[ch] = self.cnt[ch]
        if fn is None and inc:
            raise RuntimeError("inc requires a real instruction")
        self.ops[eng].append((fn, tuple(wd.items()), tuple(inc)))
        return post


_LAST_GEN = None


def _op(method, *args, **kwargs):
    return lambda eng: getattr(eng, method)(*args, **kwargs)


SUBSTOP = [3]
SUBQ2 = [3]
SUBQ5 = [5]
SUBG = [2]
SUBNP = [4]
SUBB4 = [9]
EARLYTAP = [1]
CONVENG = ["gpsimd"]
TREVAC = ["mixed"]
DRAINFIX = [0]
DBGK = [0]
DBGWHAT = ["u"]
SUBQ4 = [4]
ZTMP = [0]
WAITPAD = [0]
PHASE = ["?"]
TAPQ = [0]
WOUTDST = [0]


def build(dump_x1=False, stop_after="full", taps=False):
    nc = bass.Bass("TRN2", target_bir_lowering=False)

    u = nc.dram_tensor("u", [L, D], F32, kind="ExternalInput").ap()
    w_inT = nc.dram_tensor("w_inT", [D, 8224], F32, kind="ExternalInput").ap()
    w_outT = nc.dram_tensor("w_outT", [D, D], F32, kind="ExternalInput").ap()
    wg8 = nc.dram_tensor("wg8", [D, 8192], I8, kind="ExternalInput").ap()
    wu8 = nc.dram_tensor("wu8", [D, 8192], I8, kind="ExternalInput").ap()
    wd8 = nc.dram_tensor("wd8", [8192, D], I8, kind="ExternalInput").ap()
    in_ident = nc.dram_tensor("in_ident", [128, 128], BF16, kind="ExternalInput").ap()
    in_gmask = nc.dram_tensor("in_gmask", [128, 128], BF16, kind="ExternalInput").ap()
    in_convw = nc.dram_tensor("in_convw", [128, 192], F32, kind="ExternalInput").ap()
    in_sgate = nc.dram_tensor("in_sgate", [128, 64], F32, kind="ExternalInput").ap()
    in_sup = nc.dram_tensor("in_sup", [128, 64], F32, kind="ExternalInput").ap()
    in_sdbc = nc.dram_tensor("in_sdbc", [128, D], F32, kind="ExternalInput").ap()
    in_rmsbc = nc.dram_tensor("in_rmsbc", [128, D], F32, kind="ExternalInput").ap()
    in_arep = nc.dram_tensor("in_arep", [128, 16], F32, kind="ExternalInput").ap()

    out = nc.dram_tensor("out", [L, D], F32, kind="ExternalOutput").ap()
    tap_specs = [("d_ln1T", 9216, BF16), ("d_act", 18432, BF16), ("d_z", 8192, BF16),
                 ("d_xs", 8192, BF16), ("d_y", 4096, BF16), ("d_cs", 128, F32),
                 ("d_prev", 144, F32), ("d_attn", 4096, F32), ("d_u", 4096, F32),
                 ("d_yT", 4096, BF16), ("d_ln1T2", 9216, BF16), ("d_dbgu", 2048, F32)]
    tapd = {}
    if taps:
        for nm, w, dtp in tap_specs:
            tapd[nm] = nc.dram_tensor(nm, [128, w], dtp, kind="ExternalOutput").ap()
        tapd["d_lnt"] = nc.dram_tensor("d_lnt", [512, D], BF16, kind="ExternalOutput").ap()
        tapd["d_ut"] = nc.dram_tensor("d_ut", [512, D], F32, kind="ExternalOutput").ap()
    x1 = nc.dram_tensor("x1", [L, D], F32,
                        kind="ExternalOutput" if dump_x1 else "Internal").ap()
    cwin = nc.dram_tensor("cwin", [128, 16 * 16 * 512], BF16, kind="Internal").ap()
    cwout = nc.dram_tensor("cwout", [128, 4 * 16 * 512], BF16, kind="Internal").ap()
    cwg = nc.dram_tensor("cwg", [128, 16 * 16 * 512], BF16, kind="Internal").ap()
    cwu = nc.dram_tensor("cwu", [128, 16 * 16 * 512], BF16, kind="Internal").ap()
    cwd = nc.dram_tensor("cwd", [128, 8 * 64 * 256], BF16, kind="Internal").ap()

    cwin_v = cwin.rearrange("p (s dc j) -> p s dc j", s=16, dc=16)
    cwout_v = cwout.rearrange("p (q dc j) -> p q dc j", q=4, dc=16)
    cwg_v = cwg.rearrange("p (s hc j) -> p s hc j", s=16, hc=16)
    cwu_v = cwu.rearrange("p (s hc j) -> p s hc j", s=16, hc=16)
    cwd_v = cwd.rearrange("p (s oc j) -> p s oc j", s=8, oc=64)

    g = Gen()
    stk_root = ExitStack()

    def sb(stack, name, shape, dtype):
        return stack.enter_context(nc.sbuf_tensor(name, shape, dtype))

    c_ident = sb(stk_root, "c_ident", [128, 128], BF16)
    c_gmask = sb(stk_root, "c_gmask", [128, 128], BF16)
    c_convw = sb(stk_root, "c_convw", [128, 192], F32)
    c_sgate = sb(stk_root, "c_sgate", [128, 64], F32)
    c_sup = sb(stk_root, "c_sup", [128, 64], F32)
    c_sdbc = sb(stk_root, "c_sdbc", [128, D], F32)
    c_rmsbc = sb(stk_root, "c_rmsbc", [128, D], F32)
    c_arep = sb(stk_root, "c_arep", [128, 16], F32)
    c_decay = sb(stk_root, "c_decay", [128, 16], F32)
    scr = sb(stk_root, "scr", [128, 2080], F32)
    st = sb(stk_root, "st", [128, 16], F32)

    SQ = scr[:, 0:2048]
    RAW = [scr[:, 0:516], scr[:, 516:1032]]
    ACC = [scr[:, 1032:1544], scr[:, 1544:2056]]

    PB = stk_root.enter_context(nc.psum_tensor("PB", [128, 2048], F32))
    PX = stk_root.enter_context(nc.psum_tensor("PX", [128, 1024], BF16))
    PG = stk_root.enter_context(nc.psum_tensor("PG", [128, 512], F32))
    PTR = stk_root.enter_context(nc.psum_tensor("PTR", [128, 1024], BF16))
    PV = stk_root.enter_context(nc.psum_tensor("PV", [128, 512], F32))
    PBq = [PB[:, 512 * i:512 * (i + 1)] for i in range(4)]

    guard = {}

    def gw(*names):
        out = []
        for n in names:
            if n in guard:
                v = guard[n]
                out.extend(v if isinstance(v, list) else [v])
        return out

    def gset(name, ch):
        guard[name] = (ch, g.now(ch))

    PHASE[0] = "prologue"
    g.em("scalar", _op("copy", out=st[:, 14:15], in_=st[:, 14:15]))
    for dst, src in ((c_ident, in_ident), (c_gmask, in_gmask), (c_convw, in_convw),
                     (c_sgate, in_sgate), (c_sup, in_sup), (c_sdbc, in_sdbc),
                     (c_rmsbc, in_rmsbc), (c_arep, in_arep)):
        g.em("sync", _op("dma_start", out=dst[:], in_=src), inc=(("ldc", 16),))
    consts_ld = g.now("ldc")
    g.em("scalar", _op("activation", out=c_decay[:], in_=c_arep[:], func=AF.Exp,
                       bias=0.0, scale=63.0),
         wait=[("ldc", consts_ld)])
    g.em("scalar", _op("copy", out=st[:, 14:15], in_=c_decay[:, 0:1]),
         inc=(("act", 1),))

    stk_pro = ExitStack()
    pw_f32 = [sb(stk_pro, f"pw_f32_{i}", [128, 8192], F32) for i in range(2)]
    pw_bf = [sb(stk_pro, f"pw_bf_{i}", [128, 8192], BF16) for i in range(2)]
    pw_i8 = [sb(stk_pro, f"pw_i8_{i}", [128, 8192], I8) for i in range(2)]

    cast_rr = ["vector", "scalar"]
    cast_ch = {"vector": "dve", "scalar": "act", "gpsimd": "pool"}
    cast_i = [0]

    def cast_op(dst_ap, src_ap, wait):
        e = cast_rr[cast_i[0] % len(cast_rr)]
        cast_i[0] += 1
        ch = cast_ch[e]
        if e == "scalar":
            fn = _op("copy", out=dst_ap, in_=src_ap)
        else:
            fn = _op("tensor_copy", out=dst_ap, in_=src_ap)
        g.em(e, fn, wait=wait, inc=((ch, 1),))
        return (ch, g.now(ch))

    def prologue_stream(n_blocks, load_src_fn, in_bufs, width, store_dst_fn, nsplit):
        for b in range(n_blocks):
            sl = b % 2
            ib = in_bufs[sl][:, 0:width]
            ob = pw_bf[sl][:, 0:width]
            half = width // 2
            src_ap = load_src_fn(b)
            g.em("sync", _op("dma_start", out=ib[:, 0:half], in_=src_ap[:, 0:half]),
                 wait=gw(f"pin{sl}"), inc=((f"ldp{sl}", 16),))
            g.em("sync", _op("dma_start", out=ib[:, half:width],
                             in_=src_ap[:, half:width]),
                 inc=((f"ldu{sl}", 16),))
            ld_w = [(f"ldp{sl}", g.now(f"ldp{sl}")), (f"ldu{sl}", g.now(f"ldu{sl}"))]
            cw = cast_op(ob, ib, ld_w + gw(f"pbf{sl}"))
            guard[f"pin{sl}"] = cw
            srcs, dsts = store_dst_fn(b, ob)
            if len(srcs) == 1:
                sap, dap = srcs[0], dsts[0]
                hs = sap.shape[1] // 2
                srcs = [sap[:, 0:hs, :], sap[:, hs:, :]]
                dsts = [dap[:, 0:hs, :], dap[:, hs:, :]]
            st_lanes = [f"stp{sl}", f"stu{sl}"]
            for i, (sap, dap) in enumerate(zip(srcs, dsts)):
                g.em("gpsimd", _op("dma_start", out=dap, in_=sap),
                     wait=[cw], inc=((st_lanes[i % 2], 16),))
            guard[f"pbf{sl}"] = [(f"stp{sl}", g.now(f"stp{sl}")),
                                 (f"stu{sl}", g.now(f"stu{sl}"))]

    def win_store(b, ob):
        srcs, dsts = [], []
        obv = ob.rearrange("p (s j) -> p s j", s=16)
        for a in range(4):
            srcs.append(obv[:, 4 * a:4 * a + 4, :])
            dsts.append(cwin_v[:, 4 * a:4 * a + 4, b, :])
        return srcs, dsts

    prologue_stream(16, lambda b: w_inT[128 * b:128 * (b + 1), 0:8192],
                    pw_f32, 8192, win_store, 4)

    def wout_store(b, ob):
        obv = ob.rearrange("p (q j) -> p q j", q=4)
        return [obv], [cwout_v[:, :, b, :]]

    prologue_stream(16, lambda b: w_outT[128 * b:128 * (b + 1), :],
                    pw_f32, 2048, wout_store, 1)

    def mk_wguw_store(cv):
        def f(b, ob):
            srcs, dsts = [], []
            obv = ob.rearrange("p (s j) -> p s j", s=16)
            for a in range(4):
                srcs.append(obv[:, 4 * a:4 * a + 4, :])
                dsts.append(cv[:, 4 * a:4 * a + 4, b, :])
            return srcs, dsts
        return f

    prologue_stream(16, lambda b: wg8[128 * b:128 * (b + 1), :],
                    pw_i8, 8192, mk_wguw_store(cwg_v), 4)
    prologue_stream(16, lambda b: wu8[128 * b:128 * (b + 1), :],
                    pw_i8, 8192, mk_wguw_store(cwu_v), 4)

    def wd_store(b, ob):
        obv = ob.rearrange("p (s j) -> p s j", s=8)
        return [obv], [cwd_v[:, :, b, :]]

    prologue_stream(64, lambda b: wd8[128 * b:128 * (b + 1), :],
                    pw_i8, 2048, wd_store, 1)

    stk_pro.close()
    g.barrier()

    stk_mix = ExitStack()
    m_ln1T = sb(stk_mix, "m_ln1T", [128, 16 * QW], BF16)
    m_act = sb(stk_mix, "m_act", [128, 32 * QW], BF16)
    m_xs = sb(stk_mix, "m_xs", [128, 4 * 2048], BF16)
    m_z = sb(stk_mix, "m_z", [128, 4 * 2048], BF16)
    m_y = sb(stk_mix, "m_y", [128, 2 * 2048], BF16)
    m_xstemp = sb(stk_mix, "m_xstemp", [128, QW], BF16)
    m_xscarry = sb(stk_mix, "m_xscarry", [128, 16 * 64], BF16)
    m_rawcarry = sb(stk_mix, "m_rawcarry", [128, 96], F32)
    m_cs = sb(stk_mix, "m_cs", [128, 128], F32)
    m_prev = sb(stk_mix, "m_prev", [128, 144], F32)
    m_prevbf = sb(stk_mix, "m_prevbf", [128, 144], BF16)
    m_gt = sb(stk_mix, "m_gt", [128, 1024], BF16)
    m_vcol = sb(stk_mix, "m_vcol", [128, 32], F32)
    m_wbuf = sb(stk_mix, "m_wbuf", [128, 16384], BF16)
    m_u = sb(stk_mix, "m_u", [128, 2 * 2048], F32)
    m_ln1 = sb(stk_mix, "m_ln1", [128, 2048], BF16)
    m_yT = sb(stk_mix, "m_yT", [128, 2 * 2048], BF16)
    m_tmpA = sb(stk_mix, "m_tmpA", [128, 2048], BF16)
    m_attn = sb(stk_mix, "m_attn", [128, 2 * 2048], F32)
    m_dbgu = m_attn[:, 0:2048]

    sbuf_left = nc.sbuf_bytes_remaining
    assert sbuf_left >= 0, f"SBUF over budget: {sbuf_left}"

    def ln1T_dc(dc):
        return m_ln1T[:, QW * dc: QW * (dc + 1)]

    def act_blk(b):
        return m_act[:, QW * b: QW * (b + 1)]

    TRB = [PTR[:], PX[:]]
    trb_state = {"bank": 0, "batch": 0}

    def transpose_batch(jobs, wait):
        last = None
        for j0 in range(0, len(jobs), 8):
            sub = jobs[j0:j0 + 8]
            b = trb_state["bank"]
            trb_state["bank"] ^= 1
            bank = TRB[b]
            eng = "vector" if (trb_state["batch"] % 2 or TREVAC[0] == "vector") else "scalar"
            trb_state["batch"] += 1
            ch = "dve" if eng == "vector" else "act"
            for i, (dst, src) in enumerate(sub):
                g.em("tensor", _op("transpose", bank[:, 128 * i:128 * (i + 1)],
                                   src, c_ident[:]),
                     wait=(list(wait) + gw(f"trb{b}")) if i == 0 else (),
                     inc=(("pe", 1),) if i == len(sub) - 1 else ())
            pe_v = g.now("pe")
            for i, (dst, src) in enumerate(sub):
                fn = (_op("tensor_copy", out=dst, in_=bank[:, 128 * i:128 * (i + 1)])
                      if eng == "vector" else
                      _op("copy", out=dst, in_=bank[:, 128 * i:128 * (i + 1)]))
                g.em(eng, fn, wait=[("pe", pe_v)] if i == 0 else (),
                     inc=((ch, 1),) if i == len(sub) - 1 else ())
            guard[f"trb{b}"] = (ch, g.now(ch))
            last = (ch, g.now(ch))
        return last

    def rsqrt_col(dst, src):
        c13i = st[:, 13:14].bitcast(I32)
        c13f = st[:, 13:14]
        c12f = st[:, 12:13]
        g.em("vector", _op("tensor_scalar", out=c13i, in0=src.bitcast(I32),
                           scalar1=1, scalar2=None, op0=ALU.arith_shift_right))
        g.em("vector", _op("tensor_scalar", out=c13i, in0=c13i,
                           scalar1=-1, scalar2=None, op0=ALU.bitwise_xor))
        g.em("vector", _op("tensor_scalar", out=c13i, in0=c13i,
                           scalar1=0x5f3759df + 1, scalar2=None, op0=ALU.add))
        for it in range(3):
            g.em("vector", _op("tensor_mul", c12f, c13f, c13f))
            g.em("vector", _op("tensor_mul", c12f, c12f, src))
            g.em("vector", _op("tensor_scalar", out=c12f, in0=c12f,
                               scalar1=-0.5, scalar2=1.5, op0=ALU.mult, op1=ALU.add))
            g.em("vector", _op("tensor_mul", dst if it == 2 else c13f, c13f, c12f))

    def ln_stats(tile_ap, eps):
        g.em("vector", _op("tensor_mul", SQ, tile_ap, tile_ap), inc=(("dve", 1),))
        g.em("vector", _op("tensor_reduce", out=st[:, 1:2], in_=SQ, axis=AX.X,
                           op=ALU.add), inc=(("dvx", 1),))
        vA = g.now("dvx")
        g.em("vector", _op("tensor_reduce", out=st[:, 0:1], in_=tile_ap, axis=AX.X,
                           op=ALU.add), inc=(("dvx", 1),))
        g.em("vector", _op("tensor_scalar", out=st[:, 3:4], in0=st[:, 1:2],
                           scalar1=RD, scalar2=eps, op0=ALU.mult, op1=ALU.add),
             wait=[("dvx", vA)], inc=(("dvx", 1),))
        g.em("vector", _op("tensor_scalar", out=st[:, 2:3], in0=st[:, 0:1],
                           scalar1=-RD, scalar2=None, op0=ALU.mult),
             wait=[("dvx", vA + 1)], inc=(("dvx", 1),))
        g.em("vector", _op("tensor_mul", st[:, 4:5], st[:, 2:3], st[:, 2:3]),
             wait=[("dvx", g.now("dvx"))], inc=(("dvx", 1),))
        g.em("vector", _op("tensor_sub", st[:, 5:6], st[:, 3:4], st[:, 4:5]),
             wait=[("dvx", g.now("dvx"))], inc=(("dve", 1),))
        g.em("scalar", _op("activation", out=st[:, 6:7], in_=st[:, 5:6], func=AF.Sqrt,
                           bias=0.0, scale=1.0),
             wait=[("dve", g.now("dve"))])
        g.em("scalar", _op("copy", out=st[:, 14:15], in_=st[:, 6:7]),
             inc=(("act", 1),))
        g.em("vector", _op("reciprocal", out=st[:, 7:8], in_=st[:, 6:7]),
             wait=[("act", g.now("act"))], inc=(("dvx", 1),))
        g.em("vector", _op("tensor_mul", st[:, 8:9], st[:, 2:3], st[:, 7:8]),
             wait=[("dvx", g.now("dvx"))], inc=(("dve", 1),))

    SLAB_ORDER = [0, 1, 2, 3, 8, 9, 10, 11, 12, 13, 14, 15, 4, 5, 6, 7]
    pbx_slots = [PBq[0], PBq[1], PBq[2], PBq[3]]
    pbx_i = [0]

    def next_group_slot():
        i = pbx_i[0] % 4
        pbx_i[0] += 1
        return pbx_slots[i], f"pbx{i}"

    def gen_quarter(q):
        gen_quarter_inner(q)
        g.barrier()
        if taps and q == TAPQ[0]:
            for nm, buf in (("d_ln1T", m_ln1T), ("d_act", m_act), ("d_z", m_z),
                            ("d_xs", m_xs), ("d_y", m_y), ("d_cs", m_cs),
                            ("d_prev", m_prev), ("d_attn", m_attn), ("d_u", m_u),
                            ("d_yT", m_yT)):
                g.em("gpsimd", _op("dma_start", out=tapd[nm], in_=buf[:]),
                     inc=(("stt", 16),))
            g.barrier()
        gset("mix_pe_prev", "pe")

    def gen_quarter_inner(q):
        t0 = QT * q
        PHASE[0] = f"q{q}.rolls"
        first = (q == 0)

        if first:
            g.em("vector", _op("memset", m_rawcarry[:], 0.0), inc=(("dve", 1),))
            g.em("vector", _op("memset", m_xscarry[:], 0.0))
            g.em("vector", _op("memset", m_prev[:], 0.0))
            g.em("vector", _op("memset", m_xs[0:64, 0:2048], 0.0))
            for dc in range(16):
                g.em("vector", _op("memset", ln1T_dc(dc)[:, 0:MARG], 0.0))
            for b in range(32):
                g.em("vector", _op("memset", act_blk(b)[:, 0:MARG], 0.0))
            g.em("vector", _op("memset", m_xstemp[:, 0:MARG], 0.0), inc=(("dve", 1),))
        else:
            w = gw("mix_pe_prev")
            for dc in range(16):
                g.em("vector", _op("tensor_copy", out=ln1T_dc(dc)[:, 0:MARG],
                                   in_=ln1T_dc(dc)[:, QT:QT + MARG]), wait=w)
            for b in range(32):
                g.em("vector", _op("tensor_copy", out=act_blk(b)[:, 0:63],
                                   in_=act_blk(b)[:, QT:QT + 63]), wait=w)
            g.em("vector", _op("memset", st[:, 15:16], 0.0), inc=(("dve", 1),))
        rolls_done = ("dve", g.now("dve"))

        if SUBSTOP[0] < 1:
            return
        for k in range(4):
            PHASE[0] = f"q{q}.Q2k{k}"
            sl = k % 2
            ut = m_u[:, 2048 * sl:2048 * (sl + 1)]
            g.em("sync", _op("dma_start", out=ut[:, 0:1024],
                             in_=u[t0 + 128 * k: t0 + 128 * (k + 1), 0:1024]),
                 wait=gw(f"mu{sl}"), inc=((f"ldu{sl}", 16),))
            g.em("sync", _op("dma_start", out=ut[:, 1024:2048],
                             in_=u[t0 + 128 * k: t0 + 128 * (k + 1), 1024:2048]),
                 inc=((f"ldb{sl}", 16),))
            g.em("vector", None, wait=[(f"ldu{sl}", g.now(f"ldu{sl}")),
                                       (f"ldb{sl}", g.now(f"ldb{sl}"))])
            if taps and q == 0 and k == DBGK[0] and DBGWHAT[0] == "u":
                g.em("vector", _op("tensor_copy", out=m_dbgu, in_=ut),
                     inc=(("dve", 1),))
                g.em("gpsimd", _op("dma_start", out=tapd["d_dbgu"], in_=m_dbgu),
                     wait=[("dve", g.now("dve"))], inc=(("stt", 16),))
                gset("attn0", "stt")
            if taps and q == 0 and k == DBGK[0] and DBGWHAT[0] == "ln1":
                pass
            if SUBQ2[0] >= 1:
                ln_stats(ut, LNEPS)
            if SUBQ2[0] >= 2:
                g.em("scalar", _op("activation", out=m_ln1[:], in_=ut, func=AF.Identity,
                                   scale=st[:, 7:8], bias=st[:, 8:9]),
                     wait=[("dve", g.now("dve"))] + gw("m_ln1"))
                g.em("scalar", _op("copy", out=st[:, 14:15], in_=st[:, 7:8]),
                     inc=(("act", 1),))
                gset(f"mu{sl}", "act")
                if taps and q == 0:
                    g.em("vector", _op("tensor_copy",
                                       out=m_attn[:, 2048 + 16 * k:2048 + 16 * (k + 1)],
                                       in_=st[:]),
                         wait=[("act", g.now("act"))])
            else:
                g.em("vector", _op("tensor_copy", out=m_ln1[:], in_=ut),
                     wait=gw("m_ln1"), inc=(("dve", 1),))
                gset(f"mu{sl}", "dve")
            if DRAINFIX[0]:
                g.em("vector", _op("memset", st[:, 13:14], 0.0), inc=(("dve", 1),))
            if taps and q == 0 and k == DBGK[0] and DBGWHAT[0] == "ln1":
                g.em("vector", _op("tensor_copy", out=m_dbgu[:, 0:1024],
                                   in_=m_ln1[:, 0:1024]), inc=(("dve", 1),))
                g.em("vector", _op("tensor_copy", out=m_dbgu[:, 1024:2048],
                                   in_=m_ln1[:, 1024:2048]), inc=(("dve", 1),))
                g.em("gpsimd", _op("dma_start", out=tapd["d_dbgu"], in_=m_dbgu),
                     wait=[("dve", g.now("dve"))], inc=(("stt", 16),))
                gset("attn0", "stt")
            apply_done = (("act", g.now("act")) if SUBQ2[0] >= 2
                          else ("dve", g.now("dve")))
            if taps and (SUBSTOP[0] == 1 or EARLYTAP[0] in (2, 3)) and q == 0:
                g.em("gpsimd", _op("dma_start",
                                   out=tapd["d_lnt"][128 * k:128 * (k + 1), :],
                                   in_=m_ln1[:]),
                     wait=[apply_done], inc=(("stt", 16),))
                g.em("gpsimd", _op("dma_start",
                                   out=tapd["d_ut"][128 * k:128 * (k + 1), :],
                                   in_=ut),
                     wait=[apply_done], inc=(("stt", 16),))
                gset(f"mu{sl}", "stt")
            if SUBQ2[0] >= 3:
                for j0 in (0, 8):
                    transpose_batch(
                        [(ln1T_dc(j)[:, MARG + 128 * k: MARG + 128 * (k + 1)],
                          m_ln1[:, 128 * j:128 * (j + 1)]) for j in range(j0, j0 + 8)],
                        [apply_done, rolls_done])
                gset("m_ln1", "pe")
        q2_done = {"pe": g.now("pe"), "dve": g.now("dve"), "act": g.now("act")}
        if taps and q == 0 and EARLYTAP[0]:
            g.em("gpsimd", _op("dma_start", out=tapd["d_ln1T2"], in_=m_ln1T[:]),
                 wait=[("pe", q2_done["pe"]), ("dve", q2_done["dve"]),
                       ("act", q2_done["act"])], inc=(("stt", 16),))

        if SUBSTOP[0] < 2:
            return
        ztemp = ([m_ln1[:, 0:512], m_ln1[:, 512:1024]] if ZTMP[0] == 0 else
                 [m_tmpA[:, 0:512], m_tmpA[:, 512:1024]])
        for si, s in enumerate(SLAB_ORDER):
            PHASE[0] = f"q{q}.Q4s{si}"
            wsl = si % 2
            wv = m_wbuf[:, 8192 * wsl:8192 * (wsl + 1)].rearrange(
                "p (dc j) -> p dc j", dc=16)
            for a in range(4):
                lane = f"ldw{wsl}" if a % 2 == 0 else f"ldp{wsl}"
                g.em("sync", _op("dma_start", out=wv[:, 4 * a:4 * a + 4, :],
                                 in_=cwin[:, 8192 * s + 2048 * a: 8192 * s + 2048 * (a + 1)]
                                 .rearrange("p (dc j) -> p dc j", dc=4)),
                     wait=gw(f"mwbuf{wsl}") if a == 0 else (),
                     inc=((lane, 16),))
            slab_ld = [(f"ldw{wsl}", g.now(f"ldw{wsl}")),
                       (f"ldp{wsl}", g.now(f"ldp{wsl}"))]
            for c4 in range(4):
                if SUBQ4[0] < 2:
                    continue
                oc = 4 * s + c4
                if SUBQ4[0] < 4 and oc >= 16:
                    continue
                slot, sname = next_group_slot()
                rhs_c0 = 0 if oc < 16 else MARG
                waits = slab_ld + [
                         ("dve", q2_done["dve"]), ("act", q2_done["act"])] + gw(sname)
                for dc in range(16):
                    g.em("tensor", _op("matmul", slot[:],
                                       wv[:, dc, 128 * c4:128 * (c4 + 1)],
                                       ln1T_dc(dc)[:, rhs_c0:rhs_c0 + 512],
                                       start=(dc == 0), stop=(dc == 15)),
                         wait=waits if dc == 0 else (),
                         inc=(("pe", 1),) if dc == 15 else ())
                gset(f"mwbuf{wsl}", "pe")
                mm_done = ("pe", g.now("pe"))
                if oc < 16:
                    zt = ztemp[oc % 2]
                    if SUBQ4[0] == 20:
                        continue
                    g.em("vector", _op("tensor_copy", out=zt, in_=slot[:]),
                         wait=[mm_done] + gw(f"ztemp{oc % 2}"), inc=(("dve", 1),))
                    gset(sname, "dve")
                    ev = ("dve", g.now("dve"))
                    if SUBQ4[0] >= 3:
                        transpose_batch(
                            [(m_z[:, 2048 * j + 128 * oc: 2048 * j + 128 * (oc + 1)],
                              zt[:, 128 * j:128 * (j + 1)]) for j in range(4)], [ev])
                        gset(f"ztemp{oc % 2}", "pe")
                else:
                    r = oc - 16
                    par = r % 2
                    raw = RAW[par]
                    xs_chunk = 16 <= oc < 32
                    c = oc - 16 if xs_chunk else None
                    dst = (m_xstemp[:, 63:575] if xs_chunk
                           else act_blk((oc - 32) if oc < 48 else (16 + oc - 48))[:, 63:575])
                    g.em("vector", _op("tensor_copy", out=raw[:, 2:514], in_=slot[:]),
                         wait=[mm_done] + gw(f"raw{par}"), inc=(("dvx", 1),))
                    ev_dvx = g.now("dvx")
                    gset(sname, "dvx")
                    g.em("vector", _op("tensor_copy", out=raw[:, 0:2],
                                       in_=m_rawcarry[:, 2 * r:2 * r + 2]),
                         wait=gw(f"rawc{r}"), inc=(("dvx", 1),))
                    rc_dvx = g.now("dvx")
                    g.em("vector", _op("tensor_copy", out=m_rawcarry[:, 2 * r:2 * r + 2],
                                       in_=raw[:, 512:514]),
                         wait=[("dvx", ev_dvx)], inc=(("dve", 1),))
                    gset(f"rawc{r}", "dve")
                    g.em("vector", _op("tensor_mul", ACC[1], raw[:, 0:512],
                                       c_convw[:, r:r + 1].broadcast_to([128, 512])),
                         wait=[("dvx", rc_dvx)] + gw("acc"))
                    g.em("vector", _op("scalar_tensor_tensor", out=ACC[0],
                                       in0=raw[:, 1:513],
                                       scalar=c_convw[:, 48 + r:49 + r],
                                       in1=ACC[1], op0=ALU.mult, op1=ALU.add))
                    g.em("vector", _op("scalar_tensor_tensor", out=ACC[0],
                                       in0=raw[:, 2:514],
                                       scalar=c_convw[:, 96 + r:97 + r],
                                       in1=ACC[0], op0=ALU.mult, op1=ALU.add),
                         inc=(("dve", 1),))
                    gset(f"raw{par}", "dve")
                    conv_done = ("dve", g.now("dve"))
                    if xs_chunk:
                        if first:
                            g.em("vector", _op("memset", m_xstemp[:, 0:63], 0.0),
                                 wait=gw("xstemp_r"), inc=(("dve", 1),))
                        else:
                            g.em("vector", _op("tensor_copy", out=m_xstemp[:, 0:63],
                                               in_=m_xscarry[:, 64 * c:64 * c + 63]),
                                 wait=gw("xstemp_r"), inc=(("dve", 1),))
                    carry_in = ("dve", g.now("dve"))
                    sw = [conv_done] + ([carry_in] if xs_chunk else []) + \
                        (gw("xstemp_r") if xs_chunk else [])
                    g.em("scalar", _op("activation", out=dst, in_=ACC[0], func=AF.Silu,
                                       bias=c_convw[:, 144 + r:145 + r], scale=1.0),
                         wait=sw)
                    g.em("scalar", _op("copy", out=st[:, 14:15], in_=dst[:, 0:1]),
                         inc=(("act", 1),))
                    gset("acc", "act")
                    silu_done = ("act", g.now("act"))
                    if xs_chunk:
                        if first:
                            g.em("vector", _op("memset", m_xstemp[:, 63:64], 0.0),
                                 wait=[silu_done], inc=(("dve", 1),))
                        g.em("vector", _op("tensor_copy",
                                           out=m_xscarry[:, 64 * c:64 * c + 63],
                                           in_=m_xstemp[:, 512:575]),
                             wait=[silu_done], inc=(("dve", 1),))
                        g.em("vector", _op("tensor_mul", m_tmpA[:, 0:512],
                                           m_xstemp[:, 0:512], act_blk(c)[:, 0:512]),
                             wait=[silu_done] + gw("m_tmpA"), inc=(("dve", 1),))
                        g.em("vector", _op("tensor_reduce", out=m_cs[:, 8 * c:8 * (c + 1)],
                                           in_=m_tmpA[:, 0:512].rearrange(
                                               "p (cc t) -> p cc t", cc=8),
                                           axis=AX.X, op=ALU.add),
                             inc=(("dvx", 1),))
                        if first:
                            g.em("vector", _op("memset", m_cs[:, 8 * c:8 * c + 1], 0.0))
                        transpose_batch(
                            [(m_xs[:, 2048 * j + 128 * c: 2048 * j + 128 * (c + 1)],
                              m_xstemp[:, 128 * j:128 * (j + 1)]) for j in range(4)],
                            [silu_done, carry_in])
                        gset("xstemp_r", "pe")
        q4_pe = g.now("pe")
        cs_dvx = g.now("dvx")

        for ci in range(1, 8):
            for b in range(16):
                w = ()
                if b == 0:
                    w = ([("dvx", cs_dvx)] + gw("prev_r") if ci == 1
                         else [("dvx", g.now("dvx"))])
                g.em("vector", _op("scalar_tensor_tensor",
                                   out=m_prev[:, 9 * b + ci:9 * b + ci + 1],
                                   in0=m_prev[:, 9 * b + ci - 1:9 * b + ci],
                                   scalar=c_decay[:, b:b + 1],
                                   in1=m_cs[:, 8 * b + ci - 1:8 * b + ci],
                                   op0=ALU.mult, op1=ALU.add),
                     wait=w, inc=(("dvx", 1),) if b == 15 else ())
        g.em("vector", _op("tensor_copy", out=m_prevbf[:], in_=m_prev[:]),
             wait=[("dvx", g.now("dvx"))], inc=(("dve", 1),))
        prevbf_done = ("dve", g.now("dve"))
        for b in range(16):
            g.em("vector", _op("scalar_tensor_tensor",
                               out=m_prev[:, 9 * b:9 * b + 1],
                               in0=m_prev[:, 9 * b + 7:9 * b + 8],
                               scalar=c_decay[:, b:b + 1],
                               in1=m_cs[:, 8 * b + 7:8 * b + 8],
                               op0=ALU.mult, op1=ALU.add),
                 inc=(("dve", 1),) if b == 15 else ())

        if SUBSTOP[0] >= 3:
            gen_ssd_and_q6(q, t0, prevbf_done, q4_pe)

    def gen_ssd_pair(q, p, prevbf_done, n_vchunks=2):
        if SUBG[0] == 0:
            return
        PHASE[0] = f"q{q}.ssd{p}"
        horder = list(range(0, 32, 2)) + list(range(1, 32, 2))
        mask_end = {}
        ydiag_end = {}
        for b4 in range(min(9, SUBB4[0])):
            if b4 < 8:
                for i in range(4):
                    h = horder[4 * b4 + i]
                    hb = 64 * (h % 2)
                    blk = h // 2
                    lhsT = m_act[hb:hb + 64, QW * blk + 128 * p: QW * blk + 128 * (p + 1)]
                    rhs = m_act[hb:hb + 64,
                                QW * (16 + blk) + 128 * p: QW * (16 + blk) + 128 * (p + 1)]
                    w = []
                    if i == 0:
                        w = gw("pg_bank")
                        if b4 == 0:
                            w = w + [prevbf_done, ("act", g.now("act")),
                                     ("pool", g.now("pool"))]
                    g.em("tensor", _op("matmul", PG[:, 128 * i:128 * (i + 1)],
                                       lhsT, rhs, start=True, stop=True,
                                       tile_position=(hb, 0)),
                         wait=w, inc=(("pe", 1),) if i == 3 else ())
                g_end = g.now("pe")
                if SUBG[0] >= 2:
                    for i in range(4):
                        w = [("pe", g_end)] if i == 0 else []
                        if b4 >= 2 and i == 0 and (b4 - 2) in ydiag_end:
                            w = w + [("pe", ydiag_end[b4 - 2])]
                        g.em("vector", _op("tensor_mul",
                                           m_gt[:, 512 * (b4 % 2) + 128 * i:
                                                512 * (b4 % 2) + 128 * (i + 1)],
                                           PG[:, 128 * i:128 * (i + 1)], c_gmask[:]),
                             wait=w, inc=(("dve", 1),) if i == 3 else ())
                    mask_end[b4] = g.now("dve")
                    guard["pg_bank"] = ("dve", mask_end[b4])
                else:
                    g.em("vector", _op("memset", st[:, 14:15], 0.0),
                         wait=[("pe", g_end)], inc=(("dve", 1),))
                    mask_end[b4] = g.now("dve")
                    guard["pg_bank"] = ("dve", mask_end[b4])
            if b4 >= 1 and SUBQ5[0] >= 2:
                bb = b4 - 1
                for i in range(4):
                    hh = horder[4 * bb + i]
                    hb = 64 * (hh % 2)
                    blk = hh // 2
                    w = []
                    if i == 0:
                        w = [("dve", mask_end[bb])]
                        if bb == 0:
                            w = w + gw("pb_yacc", "pbx0", "pbx1", "pbx2", "pbx3")
                    g.em("tensor", _op("matmul", PB[:, 64 * hh:64 * (hh + 1)],
                                       m_gt[:, 512 * (bb % 2) + 128 * i:
                                            512 * (bb % 2) + 128 * (i + 1)],
                                       m_xs[:, 2048 * p + 64 * hh: 2048 * p + 64 * (hh + 1)],
                                       start=True, stop=True),
                         wait=w, inc=())
                    for cl in range(n_vchunks):
                        ci = 2 * p + cl
                        g.em("tensor", _op("matmul",
                                           PV[64 * (ci % 2):64 * (ci % 2) + 64, hh:hh + 1],
                                           m_act[hb:hb + 64,
                                                 QW * (16 + blk) + 64 * ci: QW * (16 + blk) + 64 * (ci + 1)],
                                           m_prevbf[hb:hb + 64, 9 * blk + ci:9 * blk + ci + 1],
                                           start=True, stop=True,
                                           tile_position=(hb, 64 * (ci % 2))),
                             wait=gw("pv_r") if (hh == 0 and cl == 0) else (),
                             inc=(("pe", 1),) if i == 3 and cl == n_vchunks - 1 else ())
                ydiag_end[bb] = g.now("pe")
        pe_all = g.now("pe")
        if SUBQ5[0] < 2:
            guard[f"yready{p}"] = ("pe", pe_all)
            return
        g.em("vector", _op("tensor_copy", out=m_vcol[:], in_=PV[:, 0:32]),
             wait=[("pe", pe_all)], inc=(("dvx", 1),))
        gset("pv_r", "dvx")
        vcol_dvx = g.now("dvx")
        ysl = p % 2
        for b in range(4):
            g.em("vector", _op("tensor_add",
                               m_y[:, 2048 * ysl + 512 * b: 2048 * ysl + 512 * (b + 1)]
                               .rearrange("p (hh t) -> p hh t", hh=8),
                               PB[:, 512 * b:512 * (b + 1)]
                               .rearrange("p (hh t) -> p hh t", hh=8),
                               m_vcol[:, 8 * b:8 * (b + 1)].unsqueeze(2)
                               .broadcast_to([128, 8, 64])),
                 wait=(gw(f"my{ysl}") + [("dvx", vcol_dvx)]) if b == 0 else (),
                 inc=(("dve", 1),) if b == 3 else ())
        gset("pb_yacc", "dve")
        guard[f"yready{p}"] = ("dve", g.now("dve"))

    def gen_q6_tilepair(q, t0, tiles, store_rows=None):
        if SUBQ5[0] < 3:
            return
        tmpB = m_ln1
        for ti, t in enumerate(tiles):
            PHASE[0] = f"q{q}.q6a_t{t}"
            ysl = t % 2
            ytile = m_y[:, 2048 * ysl:2048 * (ysl + 1)]
            ztile = m_z[:, 2048 * t:2048 * (t + 1)]
            g.em("scalar", _op("activation", out=m_tmpA[:], in_=ztile, func=AF.Silu,
                               bias=0.0, scale=1.0),
                 wait=[guard[f"yready{t}"]] + gw("m_tmpA"))
            g.em("scalar", _op("copy", out=st[:, 14:15], in_=m_tmpA[:, 0:1]),
                 inc=(("act", 1),))
            sz = ("act", g.now("act"))
            g.em("vector", _op("tensor_mul", tmpB[:], ytile, m_tmpA[:]),
                 wait=[sz] + gw("m_ln1"), inc=(("dve", 1),))
            gset(f"my{ysl}", "dve")
            g.em("vector", _op("tensor_mul", SQ, tmpB[:], tmpB[:]))
            g.em("vector", _op("tensor_reduce", out=st[:, 8:9], in_=SQ, axis=AX.X,
                               op=ALU.add), inc=(("dvx", 1),))
            g.em("vector", _op("tensor_scalar", out=st[:, 9:10], in0=st[:, 8:9],
                               scalar1=RD, scalar2=RMSEPS, op0=ALU.mult, op1=ALU.add),
                 wait=[("dvx", g.now("dvx"))], inc=(("dve", 1),))
            g.em("scalar", _op("activation", out=st[:, 10:11], in_=st[:, 9:10],
                               func=AF.Sqrt, bias=0.0, scale=1.0),
                 wait=[("dve", g.now("dve"))])
            g.em("scalar", _op("copy", out=st[:, 14:15], in_=st[:, 10:11]),
                 inc=(("act", 1),))
            g.em("vector", _op("reciprocal", out=st[:, 11:12], in_=st[:, 10:11]),
                 wait=[("act", g.now("act"))], inc=(("dve", 1),))
            g.em("scalar", _op("activation", out=m_tmpA[:], in_=tmpB[:], func=AF.Identity,
                               scale=st[:, 11:12], bias=0.0),
                 wait=[("dve", g.now("dve"))])
            g.em("scalar", _op("copy", out=st[:, 14:15], in_=st[:, 11:12]),
                 inc=(("act", 1),))
            g.em("vector", _op("tensor_mul", tmpB[:], m_tmpA[:], c_rmsbc[:]),
                 wait=[("act", g.now("act"))], inc=(("dve", 1),))
            gset("m_tmpA", "dve")
            yh = ("dve", g.now("dve"))
            for j0 in (0, 8):
                transpose_batch(
                    [(m_yT[:, 2048 * ti + 128 * j: 2048 * ti + 128 * (j + 1)],
                      tmpB[:, 128 * j:128 * (j + 1)]) for j in range(j0, j0 + 8)],
                    [yh])
            gset("m_ln1", "pe")
        a_ev = [("dve", g.now("dve")), ("act", g.now("act"))]
        if SUBQ5[0] < 4 and SUBQ5[0] not in (40, 41):
            return

        for j in range(4):
            PHASE[0] = f"q{q}.q6b_j{j}"
            wsl = j % 2
            for aa in range(4):
                dst_ap = (m_wbuf[:, 8192 * wsl + 2048 * aa:
                                 8192 * wsl + 2048 * (aa + 1)]
                          if WOUTDST[0] == 0 else
                          x1[0:128, 0:2048].bitcast(BF16)[:, 0:2048])
                g.em("sync", _op("dma_start",
                                 out=dst_ap,
                                 in_=cwout[:, 8192 * j + 2048 * aa:
                                            8192 * j + 2048 * (aa + 1)]),
                     wait=gw(f"mwbuf{wsl}") if aa == 0 else (),
                     inc=(((f"ldw{wsl}" if aa % 2 == 0 else f"ldp{wsl}"), 16),))
            wld = [(f"ldw{wsl}", g.now(f"ldw{wsl}")), (f"ldp{wsl}", g.now(f"ldp{wsl}"))]
            for ti, t in enumerate(tiles):
                if SUBQ5[0] == 40:
                    continue
                slot = PBq[(2 * j + ti) % 4]
                for dc in range(16):
                    g.em("tensor", _op("matmul", slot,
                                       m_yT[:, 2048 * ti + 128 * dc: 2048 * ti + 128 * (dc + 1)],
                                       m_wbuf[:, 8192 * wsl + 512 * dc: 8192 * wsl + 512 * (dc + 1)],
                                       start=(dc == 0), stop=(dc == 15)),
                         wait=(wld + a_ev
                               + gw(f"pbx{(2 * j + ti) % 4}", "pb_yacc")
                               if dc == 0 else ()),
                         inc=(("pe", 1),) if dc == 15 else ())
                gset(f"mwbuf{wsl}", "pe")
                mmv = ("pe", g.now("pe"))
                if SUBQ5[0] == 41:
                    g.em("vector", _op("tensor_copy", out=st[:, 12:13],
                                       in_=slot[:, 0:1]),
                         wait=[mmv], inc=(("dve", 1),))
                    guard[f"pbx{(2 * j + ti) % 4}"] = ("dve", g.now("dve"))
                    continue
                g.em("scalar", _op("copy",
                                   out=m_attn[:, 2048 * ti + 512 * j: 2048 * ti + 512 * (j + 1)],
                                   in_=slot),
                     wait=[mmv] + (gw(f"attn{ti}") if j == 0 else []), inc=(("act", 1),))
                guard[f"pbx{(2 * j + ti) % 4}"] = ("act", g.now("act"))
        attn_act = g.now("act")
        if SUBQ5[0] < 5 or SUBQ5[0] in (40, 41):
            return

        for ti, t in enumerate(tiles):
            rows = (t0 - MARG + 128 * t, t0 - MARG + 128 * (t + 1))
            sr = store_rows[ti] if store_rows else (max(rows[0], 0), min(rows[1], L))
            atile = m_attn[:, 2048 * ti:2048 * (ti + 1)]
            usl = t % 2
            ut = m_u[:, 2048 * usl:2048 * (usl + 1)]
            lo_off = sr[0] - rows[0]
            hi_off = sr[1] - rows[0]
            g.em("sync", _op("dma_start", out=ut[lo_off:hi_off, :],
                             in_=u[sr[0]:sr[1], :]),
                 wait=gw(f"mu{usl}"), inc=((f"ldu{usl}", 16),))
            uld = g.now(f"ldu{usl}")
            g.em("vector", None, wait=[("act", attn_act)])
            ln_stats(atile, LNEPS)
            g.em("scalar", _op("activation", out=SQ, in_=atile, func=AF.Identity,
                               scale=st[:, 7:8], bias=st[:, 8:9]),
                 wait=[("dve", g.now("dve"))])
            g.em("scalar", _op("copy", out=st[:, 14:15], in_=st[:, 7:8]),
                 inc=(("act", 1),))
            gset(f"attn{ti}", "act")
            g.em("vector", _op("tensor_add", ut, SQ, ut),
                 wait=[(f"ldu{usl}", uld), ("act", g.now("act"))], inc=(("dve", 1),))
            addv = ("dve", g.now("dve"))
            g.em("gpsimd", _op("dma_start", out=x1[sr[0]:sr[1], :],
                               in_=ut[lo_off:hi_off, :]),
                 wait=[addv], inc=((f"stu{usl}", 16),))
            gset(f"mu{usl}", f"stu{usl}")

    def gen_ssd_and_q6(q, t0, prevbf_done, q4_pe):
        for tp in (0, 2):
            for p in (tp, tp + 1):
                if p < SUBNP[0]:
                    gen_ssd_pair(q, p, prevbf_done)
            gen_q6_tilepair(q, t0, [tp, tp + 1])
        gset("prev_r", "pe")

    def gen_epilogue():
        t0 = L
        PHASE[0] = "epi"
        w = gw("mix_pe_prev")
        for dc in range(16):
            g.em("vector", _op("tensor_copy", out=ln1T_dc(dc)[:, 0:MARG],
                               in_=ln1T_dc(dc)[:, QT:QT + MARG]), wait=w)
        for b in range(32):
            g.em("vector", _op("tensor_copy", out=act_blk(b)[:, 0:63],
                               in_=act_blk(b)[:, QT:QT + 63]), wait=w)
        g.em("vector", _op("tensor_copy", out=m_prevbf[:], in_=m_prev[:]),
             inc=(("dve", 1),))
        prevbf_done = ("dve", g.now("dve"))
        rolls = ("dve", g.now("dve"))

        raw3 = ACC[1][:, 0:3]
        prev_pool = None
        for r in range(48):
            g.em("vector", _op("tensor_copy", out=raw3[:, 0:2],
                               in_=m_rawcarry[:, 2 * r:2 * r + 2]),
                 wait=(gw("acc") if r == 0 else []) +
                      ([prev_pool] if prev_pool else []))
            g.em("vector", _op("memset", raw3[:, 2:3], 0.0), inc=(("dve", 1),))
            rdv = ("dve", g.now("dve"))
            g.em("gpsimd", _op("tensor_mul", ACC[0][:, 0:1], raw3[:, 1:2],
                               c_convw[:, 48 + r:49 + r]),
                 wait=[rdv, ("act", g.now("act"))])
            g.em("gpsimd", _op("tensor_mul", ACC[0][:, 2:3], raw3[:, 0:1],
                               c_convw[:, r:r + 1]), inc=(("pox", 1),))
            g.em("gpsimd", _op("tensor_add", ACC[0][:, 1:2], ACC[0][:, 0:1],
                               ACC[0][:, 2:3]),
                 wait=[("pox", g.now("pox"))], inc=(("pool", 1),))
            cdv = ("pool", g.now("pool"))
            prev_pool = cdv
            xs_chunk = r < 16
            if xs_chunk:
                c = r
                g.em("vector", _op("tensor_copy", out=m_xstemp[:, 0:63],
                                   in_=m_xscarry[:, 64 * c:64 * c + 63]),
                     wait=([rolls] if c == 0 else []) + gw("xstemp_r"),
                     inc=(("dve", 1),))
                cin = ("dve", g.now("dve"))
                g.em("scalar", _op("activation", out=m_xstemp[:, 63:64],
                                   in_=ACC[0][:, 1:2], func=AF.Silu,
                                   bias=c_convw[:, 144 + r:145 + r], scale=1.0),
                     wait=[cdv, cin])
                g.em("scalar", _op("copy", out=st[:, 14:15], in_=m_xstemp[:, 63:64]),
                     inc=(("act", 1),))
                sd = ("act", g.now("act"))
                transpose_batch([(m_xs[:, 128 * c:128 * (c + 1)],
                                  m_xstemp[:, 0:128])], [sd])
                gset("xstemp_r", "pe")
            else:
                b = (r - 16) if r < 32 else (16 + r - 32)
                g.em("scalar", _op("activation", out=act_blk(b)[:, 63:64],
                                   in_=ACC[0][:, 1:2], func=AF.Silu,
                                   bias=c_convw[:, 144 + r:145 + r], scale=1.0),
                     wait=[cdv, rolls])
                g.em("scalar", _op("copy", out=st[:, 14:15], in_=act_blk(b)[:, 63:64]),
                     inc=(("act", 1),))
        conv_done = {"act": g.now("act"), "pe": g.now("pe"), "dve": g.now("dve")}

        for si, s in enumerate(range(4)):
            wsl = si % 2
            wv = m_wbuf[:, 8192 * wsl:8192 * (wsl + 1)].rearrange(
                "p (dc j) -> p dc j", dc=16)
            for a in range(4):
                g.em("sync", _op("dma_start", out=wv[:, 4 * a:4 * a + 4, :],
                                 in_=cwin[:, 8192 * s + 2048 * a: 8192 * s + 2048 * (a + 1)]
                                 .rearrange("p (dc j) -> p dc j", dc=4)),
                     wait=gw(f"mwbuf{wsl}") if a == 0 else (), inc=((f"ldw{wsl}", 16),))
            slab_ld = g.now(f"ldw{wsl}")
            for c4 in range(4):
                oc = 4 * s + c4
                slot, sname = next_group_slot()
                for dc in range(16):
                    g.em("tensor", _op("matmul", slot[:, 0:128],
                                       wv[:, dc, 128 * c4:128 * (c4 + 1)],
                                       ln1T_dc(dc)[:, 0:128],
                                       start=(dc == 0), stop=(dc == 15)),
                         wait=([(f"ldw{wsl}", slab_ld), ("dve", conv_done["dve"]),
                                ("act", conv_done["act"])] + gw(sname))
                         if dc == 0 else (),
                         inc=(("pe", 1),) if dc == 15 else ())
                gset(f"mwbuf{wsl}", "pe")
                mm_done = ("pe", g.now("pe"))
                zt = ztemp_ep[oc % 2]
                g.em("vector", _op("tensor_copy", out=zt[:, 0:128], in_=slot[:, 0:128]),
                     wait=[mm_done] + gw(f"ztemp{oc % 2}"), inc=(("dve", 1),))
                gset(sname, "dve")
                ev = ("dve", g.now("dve"))
                transpose_batch([(m_z[:, 128 * oc:128 * (oc + 1)], zt[:, 0:128])],
                                [ev])
                gset(f"ztemp{oc % 2}", "pe")

        gen_ssd_pair(4, 0, prevbf_done, n_vchunks=1)
        gen_q6_tilepair(4, t0, [0])
        g.barrier()
        if taps and TAPQ[0] == 4:
            for nm, buf in (("d_y", m_y), ("d_z", m_z), ("d_xs", m_xs),
                            ("d_act", m_act), ("d_prev", m_prev),
                            ("d_ln1T", m_ln1T)):
                g.em("gpsimd", _op("dma_start", out=tapd[nm], in_=buf[:]),
                     inc=(("stt", 16),))
            g.barrier()

    ztemp_ep = None

    nq = {"prologue": 0, "q0": 1, "q1": 2, "q2": 3, "q3": 4}.get(stop_after, 4)
    for q in range(nq):
        gen_quarter(q)
    ztemp_ep = [m_ln1[:, 0:512], m_ln1[:, 512:1024]]
    if stop_after in ("full", "mixer", "ffn1"):
        gen_epilogue()

    stk_mix.close()
    g.barrier()

    stk_ffn = ExitStack()
    f_big1 = sb(stk_ffn, "f_big1", [128, 32768], BF16)
    f_big2 = sb(stk_ffn, "f_big2", [128, 32768], BF16)
    f_mid = sb(stk_ffn, "f_mid", [128, 16384], BF16)
    f_gtmp = sb(stk_ffn, "f_gtmp", [128, 1024], BF16)
    f_utmp = sb(stk_ffn, "f_utmp", [128, 1024], BF16)
    assert nc.sbuf_bytes_remaining >= 0

    ln2T = f_mid[:, 0:8192]
    x1f1 = [f_mid[:, 8192 + 4096 * i:8192 + 4096 * (i + 1)].bitcast(F32)
            for i in range(2)]
    down_stage = f_mid[:, 0:16384].bitcast(F32)
    x1f4 = [f_big1[:, 4096 * i:4096 * (i + 1)].bitcast(F32) for i in range(2)]
    ln2stage = f_big1[:, 0:2048]

    def gen_ffn_pass(pi):
        p0 = QT * pi
        for k in range(4):
            PHASE[0] = f"f{pi}.F1k{k}"
            sl = k % 2
            xt = x1f1[sl]
            g.em("sync", _op("dma_start", out=xt[:, 0:1024],
                             in_=x1[p0 + 128 * k:p0 + 128 * (k + 1), 0:1024]),
                 wait=gw(f"fx1{sl}"), inc=((f"ldu{sl}", 16),))
            g.em("sync", _op("dma_start", out=xt[:, 1024:2048],
                             in_=x1[p0 + 128 * k:p0 + 128 * (k + 1), 1024:2048]),
                 inc=((f"ldb{sl}", 16),))
            g.em("vector", None, wait=[(f"ldu{sl}", g.now(f"ldu{sl}")),
                                       (f"ldb{sl}", g.now(f"ldb{sl}"))])
            ln_stats(xt, LNEPS)
            g.em("scalar", _op("activation", out=ln2stage, in_=xt, func=AF.Identity,
                               scale=st[:, 7:8], bias=st[:, 8:9]),
                 wait=[("dve", g.now("dve"))] + gw("ln2stage"))
            g.em("scalar", _op("copy", out=st[:, 14:15], in_=st[:, 7:8]),
                 inc=(("act", 1),))
            gset(f"fx1{sl}", "act")
            ap = ("act", g.now("act"))
            for j0 in (0, 8):
                transpose_batch(
                    [(ln2T[:, 512 * j + 128 * k: 512 * j + 128 * (k + 1)],
                      ln2stage[:, 128 * j:128 * (j + 1)]) for j in range(j0, j0 + 8)],
                    [ap])
            gset("ln2stage", "pe")
        f1_ev = [("dve", g.now("dve")), ("act", g.now("act"))]

        upslots = [PV[:], PG[:]]
        for s in range(16):
            PHASE[0] = f"f{pi}.F2s{s}"
            wsl = s % 2
            wgb = f_big2[:, 8192 * wsl:8192 * (wsl + 1)]
            wub = f_big2[:, 16384 + 8192 * wsl:16384 + 8192 * (wsl + 1)]
            g.em("sync", _op("dma_start", out=wgb,
                             in_=cwg[:, 8192 * s:8192 * (s + 1)]),
                 wait=gw(f"fwg{wsl}"), inc=((f"ldw{wsl}", 16),))
            g.em("sync", _op("dma_start", out=wub,
                             in_=cwu[:, 8192 * s:8192 * (s + 1)]),
                 wait=gw(f"fwu{wsl}"), inc=((f"ldv{wsl}", 16),))
            slab_ld = [(f"ldw{wsl}", g.now(f"ldw{wsl}")), (f"ldv{wsl}", g.now(f"ldv{wsl}"))]
            for c4 in range(4):
                oc = 4 * s + c4
                gslot = PBq[oc % 4]
                uslot = upslots[oc % 2]
                for hc in range(16):
                    g.em("tensor", _op("matmul", gslot,
                                       wgb.rearrange("p (hc j) -> p hc j", hc=16)[:, hc, 128 * c4:128 * (c4 + 1)],
                                       ln2T[:, 512 * hc:512 * (hc + 1)],
                                       start=(hc == 0), stop=(hc == 15)),
                         wait=(slab_ld + f1_ev + gw(f"fpb{oc % 4}"))
                         if hc == 0 else (),
                         inc=(("pe", 1),) if hc == 15 else ())
                gmm = ("pe", g.now("pe"))
                for hc in range(16):
                    g.em("tensor", _op("matmul", uslot,
                                       wub.rearrange("p (hc j) -> p hc j", hc=16)[:, hc, 128 * c4:128 * (c4 + 1)],
                                       ln2T[:, 512 * hc:512 * (hc + 1)],
                                       start=(hc == 0), stop=(hc == 15)),
                         wait=gw(f"fup{oc % 2}") if hc == 0 else (),
                         inc=(("pe", 1),) if hc == 15 else ())
                gset(f"fwg{wsl}", "pe")
                gset(f"fwu{wsl}", "pe")
                umm = ("pe", g.now("pe"))
                gt = f_gtmp[:, 512 * (oc % 2):512 * (oc % 2 + 1)]
                ut = f_utmp[:, 512 * (oc % 2):512 * (oc % 2 + 1)]
                g.em("scalar", _op("activation", out=gt, in_=gslot, func=AF.Silu,
                                   bias=0.0, scale=c_sgate[:, oc:oc + 1]),
                     wait=[gmm] + gw(f"fgt{oc % 2}"))
                g.em("scalar", _op("copy", out=st[:, 14:15], in_=gt[:, 0:1]),
                     inc=(("act", 1),))
                guard[f"fpb{oc % 4}"] = ("act", g.now("act"))
                sg = ("act", g.now("act"))
                g.em("vector", _op("tensor_scalar", out=ut, in0=uslot,
                                   scalar1=c_sup[:, oc:oc + 1], scalar2=None,
                                   op0=ALU.mult),
                     wait=[umm] + gw(f"fut{oc % 2}"), inc=(("dve", 1),))
                guard[f"fup{oc % 2}"] = ("dve", g.now("dve"))
                g.em("vector", _op("tensor_mul",
                                   f_big1[:, 512 * oc:512 * (oc + 1)], gt, ut),
                     wait=[sg], inc=(("dve", 1),))
                guard[f"fgt{oc % 2}"] = ("dve", g.now("dve"))
                guard[f"fut{oc % 2}"] = ("dve", g.now("dve"))
        f2_dve = g.now("dve")

        for s in range(8):
            PHASE[0] = f"f{pi}.F3s{s}"
            wsl = s % 2
            wdb = f_big2[:, 16384 * wsl:16384 * (wsl + 1)]
            g.em("sync", _op("dma_start", out=wdb,
                             in_=cwd[:, 16384 * s:16384 * (s + 1)]),
                 wait=gw(f"fwd{wsl}")
                 + (gw("fwg0", "fwg1") if (s % 2 == 0) else gw("fwu0", "fwu1"))
                 + ([("dve", f2_dve)] if s < 2 else []),
                 inc=((f"ldw{wsl}", 16),))
            slab_ld = g.now(f"ldw{wsl}")
            wdv = wdb.rearrange("p (oc j) -> p oc j", oc=64)
            for k in range(4):
                slot = PBq[(4 * s + k) % 4]
                for oc in range(64):
                    g.em("tensor", _op("matmul", slot[:, 0:256],
                                       f_big1[:, 512 * oc + 128 * k: 512 * oc + 128 * (k + 1)],
                                       wdv[:, oc, :],
                                       start=(oc == 0), stop=(oc == 63)),
                         wait=([(f"ldw{wsl}", slab_ld), ("dve", f2_dve)]
                               + gw(f"fpb{(4 * s + k) % 4}", "dstage")) if oc == 0 else (),
                         inc=(("pe", 1),) if oc == 63 else ())
                gset(f"fwd{wsl}", "pe")
                mmv = ("pe", g.now("pe"))
                g.em("vector", _op("tensor_mul",
                                   down_stage[:, 2048 * k + 256 * s: 2048 * k + 256 * (s + 1)],
                                   slot[:, 0:256], c_sdbc[:, 256 * s:256 * (s + 1)]),
                     wait=[mmv], inc=(("dve", 1),))
                guard[f"fpb{(4 * s + k) % 4}"] = ("dve", g.now("dve"))
        f3_dve = g.now("dve")

        for k in range(4):
            sl = k % 2
            xt = x1f4[sl]
            g.em("sync", _op("dma_start", out=xt[:, 0:1024],
                             in_=x1[p0 + 128 * k:p0 + 128 * (k + 1), 0:1024]),
                 wait=gw(f"fx4{sl}") + ([("dve", f3_dve)] if k < 2 else []),
                 inc=((f"ldu{sl}", 16),))
            g.em("sync", _op("dma_start", out=xt[:, 1024:2048],
                             in_=x1[p0 + 128 * k:p0 + 128 * (k + 1), 1024:2048]),
                 inc=((f"ldb{sl}", 16),))
            ldv = g.now(f"ldu{sl}")
            ldvb = g.now(f"ldb{sl}")
            dtile = down_stage[:, 2048 * k:2048 * (k + 1)]
            g.em("vector", None, wait=[("dve", f3_dve)])
            ln_stats(dtile, LNEPS)
            g.em("scalar", _op("activation", out=SQ, in_=dtile, func=AF.Identity,
                               scale=st[:, 7:8], bias=st[:, 8:9]),
                 wait=[("dve", g.now("dve"))])
            g.em("scalar", _op("copy", out=st[:, 14:15], in_=st[:, 7:8]),
                 inc=(("act", 1),))
            gset("dstage", "act")
            g.em("vector", _op("tensor_add", xt, SQ, xt),
                 wait=[(f"ldu{sl}", ldv), (f"ldb{sl}", ldvb),
                       ("act", g.now("act"))], inc=(("dve", 1),))
            addv = ("dve", g.now("dve"))
            g.em("gpsimd", _op("dma_start", out=out[p0 + 128 * k:p0 + 128 * (k + 1), :],
                               in_=xt),
                 wait=[addv], inc=((f"stu{sl}", 16),))
            gset(f"fx4{sl}", f"stu{sl}")
        g.barrier()

    nf = {"full": 4, "ffn1": 1}.get(stop_after, 0)
    for pi in range(nf):
        gen_ffn_pass(pi)

    stk_ffn.close()

    final_waits = [(c, g.cnt[c]) for c in list(g.cnt) if c.startswith(("st", "ld"))]
    g.em("gpsimd", None, wait=final_waits)

    global _LAST_GEN
    _LAST_GEN = g

    with stk_root:
        sems = {}
        sem_stack = ExitStack()
        with sem_stack:
            for chn in list(g.cnt.keys()):
                sems[chn] = sem_stack.enter_context(nc.semaphore(f"s_{chn}"))
            with nc.Block() as block:
                def mk(name):
                    def body(eng):
                        for fn, waits, incs in g.ops[name]:
                            for chn, v in waits:
                                eng.wait_ge(sems[chn], v)
                            ins = fn(eng) if fn is not None else None
                            for chn, amt in incs:
                                ins.then_inc(sems[chn], amt)
                    return body
                block.sync(mk("sync"))
                block.tensor(mk("tensor"))
                block.vector(mk("vector"))
                block.scalar(mk("scalar"))
                block.gpsimd(mk("gpsimd"))
    return nc


def _host_inputs(inputs):
    bf = ml_dtypes.bfloat16
    W_in = np.asarray(inputs["L0_attn_W_in"], dtype=np.float32)
    W_out = np.asarray(inputs["L0_attn_W_out"], dtype=np.float32)
    conv_w = np.asarray(inputs["L0_attn_conv_w"], dtype=np.float32)
    conv_b = np.asarray(inputs["L0_attn_conv_b"], dtype=np.float32)
    A_log = np.asarray(inputs["L0_attn_A_log"], dtype=np.float32)
    rms_w = np.asarray(inputs["L0_attn_rms_w"], dtype=np.float32)
    s_gate = np.asarray(inputs["L0_ffn_s_gate"], dtype=np.float32)
    s_up = np.asarray(inputs["L0_ffn_s_up"], dtype=np.float32)
    s_down = np.asarray(inputs["L0_ffn_s_down"], dtype=np.float32)

    shared = {
        "w_inT": np.ascontiguousarray(W_in.T),
        "w_outT": np.ascontiguousarray(W_out.T),
        "wg8": np.ascontiguousarray(np.asarray(inputs["L0_ffn_w_gate"], dtype=np.int8)),
        "wu8": np.ascontiguousarray(np.asarray(inputs["L0_ffn_w_up"], dtype=np.int8)),
        "wd8": np.ascontiguousarray(np.asarray(inputs["L0_ffn_w_down"], dtype=np.int8)),
        "in_ident": np.eye(128, dtype=bf),
        "in_gmask": np.asarray(
            (np.arange(128)[:, None] // 64 == np.arange(128)[None, :] // 64)
            & (np.arange(128)[None, :] >= np.arange(128)[:, None]), dtype=bf),
        "in_convw": np.ascontiguousarray(np.concatenate(
            [conv_w[:, 0, k].reshape(48, 128).T for k in range(3)]
            + [conv_b.reshape(48, 128).T], axis=1)),
        "in_sgate": np.ascontiguousarray(s_gate.reshape(64, 128).T),
        "in_sup": np.ascontiguousarray(s_up.reshape(64, 128).T),
        "in_sdbc": np.ascontiguousarray(np.tile(s_down[None, :], (128, 1))),
        "in_rmsbc": np.ascontiguousarray(np.tile(rms_w[None, :], (128, 1))),
        "in_arep": np.ascontiguousarray(np.repeat(A_log, 64).reshape(16, 128).T),
    }
    u_all = np.asarray(inputs["L0_attn_u"], dtype=np.float32)
    in_maps = []
    for b in range(8):
        m = {"u": np.ascontiguousarray(u_all[b])}
        m.update(shared)
        in_maps.append(m)
    return in_maps


def kernel(**inputs):
    in_maps = _host_inputs(inputs)
    nc = build()
    res = run_bass_kernel_spmd(nc, in_maps, core_ids=list(range(8)))
    return np.stack([np.asarray(res.results[b]["out"]) for b in range(8)], axis=0)
